# version 12
# baseline (speedup 1.0000x reference)
import numpy as np
import ml_dtypes
import concourse.bass as bass
import concourse.mybir as mybir
from concourse.bass_utils import run_bass_kernel_spmd

F32 = mybir.dt.float32
F32R = mybir.dt.float32r
BF16 = mybir.dt.bfloat16
AF = mybir.ActivationFunctionType
ALU = mybir.AluOpType
AX = mybir.AxisListType

P = 128
DM = 1024
NCHUNK = DM // P
H = 16
E = 8
F = 4096
FS = 2048
FT = 256
MSUB = FT // P
NEG = -60000.0
EPS = 1e-5
SBUF_BYTES = 229376

MOFF = [0, 1, 2, 4, 8]
CAP = 384
CAPT = CAP // P


def build_masks():
    allowed = set(range(5)) | {8, 16, 32, 64, 128, 256, 512, 1024}
    qi = np.arange(P)[:, None]
    ki = np.arange(P)[None, :]
    tiles = []
    for m in MOFF:
        d = 128 * m + qi - ki
        tiles.append(np.where(np.isin(d, list(allowed)), 0.0, NEG).astype(np.float32))
    return np.stack(tiles).astype(ml_dtypes.bfloat16)


def urdhva_block_cols(c):
    out = [(c, c)]
    if c >= 1:
        out.append((c - 1, 8 + (c - 1)))
    if c <= 6:
        out.append((c + 1, 15 + c))
    return out


def build_program(n_seq=2, T=2048):
    NT = T // P
    NG = T // 512
    PASS_TOK = min(1024, T)
    NPASS = T // PASS_TOK
    GPP = PASS_TOK // 512
    TPP = PASS_TOK // P
    NFT = F // FT
    NFTS = FS // FT

    nc = bass.Bass("TRN2", target_bir_lowering=False)

    x_in = nc.dram_tensor("x_in", [n_seq * T, DM], F32, kind="ExternalInput").ap()
    wq = {}
    for pname in ("q", "k", "v", "o"):
        wq[pname + "v"] = nc.dram_tensor(f"{pname}_Wv", [8, P, P], F32, kind="ExternalInput").ap()
        wq[pname + "r"] = nc.dram_tensor(f"{pname}_Wr", [7, P, P], F32, kind="ExternalInput").ap()
        wq[pname + "l"] = nc.dram_tensor(f"{pname}_Wl", [7, P, P], F32, kind="ExternalInput").ap()
        wq[pname + "b"] = nc.dram_tensor(f"{pname}_b", [DM], F32, kind="ExternalInput").ap()
    gate_w = nc.dram_tensor("gate_w", [E, DM], F32, kind="ExternalInput").ap()
    gate_k = nc.dram_tensor("gate_k", [E, DM, F], F32, kind="ExternalInput").ap()
    up_k = nc.dram_tensor("up_k", [E, DM, F], F32, kind="ExternalInput").ap()
    down_k = nc.dram_tensor("down_k", [E, F, DM], F32, kind="ExternalInput").ap()
    sh_gate = nc.dram_tensor("sh_gate", [DM, FS], F32, kind="ExternalInput").ap()
    sh_up = nc.dram_tensor("sh_up", [DM, FS], F32, kind="ExternalInput").ap()
    sh_down = nc.dram_tensor("sh_down", [FS, DM], F32, kind="ExternalInput").ap()
    ident_f = nc.dram_tensor("ident_f", [P, P], F32, kind="ExternalInput").ap()
    ident_b = nc.dram_tensor("ident_b", [P, P], BF16, kind="ExternalInput").ap()
    masks_in = nc.dram_tensor("masks_in", [5, P, P], BF16, kind="ExternalInput").ap()
    ones_in = nc.dram_tensor("ones_in", [1, P], F32, kind="ExternalInput").ap()
    lt_in = nc.dram_tensor("lt_in", [P, P], F32, kind="ExternalInput").ap()
    rtc_in = nc.dram_tensor("rtc_in", [P, 3 * E], F32, kind="ExternalInput").ap()
    rtc2_in = nc.dram_tensor("rtc2_in", [P, 2], F32, kind="ExternalInput").ap()

    x1_dram = nc.dram_tensor("x1_scratch", [n_seq * T, DM], F32, kind="Internal").ap()
    ln2_dram = nc.dram_tensor("ln2_scratch", [n_seq, NG, NCHUNK, P, 512], F32, kind="Internal").ap()
    ln2tm_dram = nc.dram_tensor("ln2tm_scratch", [n_seq * T, DM], F32, kind="Internal").ap()
    idx_dram = nc.dram_tensor("idx_scratch", [P * E * CAPT, 1], mybir.dt.int32, kind="Internal").ap()
    eo_dram = nc.dram_tensor("eo_scratch", [E * CAP, DM], F32, kind="Internal").ap()
    out_d = nc.dram_tensor("out", [n_seq * T, DM], F32, kind="ExternalOutput").ap()

    off = [18432]

    def sb(name, shape, dtype, at=None):
        nbytes = int(np.prod(shape[1:])) * mybir.dt.size(dtype)
        o = off[0] if at is None else at
        h = nc.alloc_sbuf_tensor_at(name, shape, dtype, offset=o)
        if at is None:
            off[0] = o + ((nbytes + 31) // 32) * 32
        return h

    idb = sb("idb", [P, P], BF16)
    idf = sb("idf", [P, P], F32)
    idr = sb("idr", [P, P], F32R)
    msk = sb("msk", [P, 5 * P], BF16)
    onesr = sb("onesr", [1, P], F32R)
    vb_row = sb("vb_row", [1, DM], F32R)
    ob_row = sb("ob_row", [1, DM], F32R)
    qb_col = sb("qb_col", [P, NCHUNK], F32)
    kb_col = sb("kb_col", [P, NCHUNK], F32)
    gwT = sb("gwT", [P, NCHUNK * E], F32R)
    comb = sb("comb", [P, NT * E], F32)
    sm = sb("smalls", [P, 24], F32)
    SM_SUM, SM_SSQ, SM_MEAN, SM_M2, SM_E2, SM_VAR, SM_RSTD, SM_NMRS = range(8)
    SM_NEGMAX, SM_ESUM, SM_LNS, SM_SINV, SM_M1, SM_MM2 = range(8, 14)
    SM_NEGMAX1, SM_ESUM1, SM_LNS1, SM_SINV1 = range(14, 18)
    SM_GES, SM_GLN, SM_GSI = range(18, 21)
    o_tm = sb("o_tm", [P, DM], F32)
    o_ln = sb("o_ln", [P, DM], F32)
    x_tile = sb("x_tile", [P, 2 * DM], F32)
    x1_t = sb("x1_t", [P, 2 * DM], F32)
    lnt = sb("lnt", [P, DM], F32)
    probs = sb("probs", [P, 2 * 5 * P], BF16)
    probs_km = sb("probs_km", [P, 2 * 5 * P], BF16)
    diag = sb("diag", [P, 2 * P], BF16)
    p8 = sb("p8", [P, E], F32)
    pw8 = sb("pw8", [P, E], F32)
    keep8 = sb("keep8", [P, E], F32)
    attn_fm = sb("attn_fm", [P, NCHUNK * P], F32R)
    LT_r = sb("LT_r", [P, P], F32R)
    onescol_r = sb("onescol_r", [P, 1], F32R)
    rtc = sb("rtc", [P, 3 * E], F32)
    iota_f = sb("iota_f", [P, 1], F32)
    base_row = sb("base_row", [1, E], F32R)
    ass_r = sb("ass_r", [P, E], F32R)
    pos_sb = sb("pos_sb", [P, E], F32)
    rt_s = sb("rt_s", [P, 16], F32)
    w12_sb = sb("w12_sb", [P, 2 * (1024 // P)], F32)
    sl_sb = sb("sl_sb", [P, 2 * (1024 // P)], mybir.dt.int32)
    ad_sb = sb("ad_sb", [P, 2], mybir.dt.int32)
    tok_sb = sb("tok_sb", [P, 2], mybir.dt.int32)
    idx_sb = sb("idx_sb", [P, E * CAPT], mybir.dt.int32)
    PERM_END = off[0]
    assert PERM_END <= 75 * 1024, PERM_END

    R1 = PERM_END
    gw_rows = sb("gw_rows", [P, DM], F32R, at=R1)
    qkvW = sb("qkvW", [P, 3 * 22 * P], F32R, at=R1)
    R1_END = R1 + 3 * 22 * P * 4
    ln1_q = sb("ln1_q", [P, NCHUNK * 256], F32R, at=R1_END)
    ln2_grp = sb("ln2_grp", [P, NCHUNK * 512], F32R, at=R1_END + NCHUNK * 256 * 4)
    R3 = R1_END + NCHUNK * 256 * 4 + NCHUNK * 512 * 4
    q_grp = sb("q_grp", [P, NCHUNK * 512], BF16, at=R3)
    k_fm = sb("k_fm", [P, NCHUNK * T], BF16, at=R3 + NCHUNK * 512 * 2)
    v_tm = sb("v_tm", [P, NT * DM], BF16, at=R3 + NCHUNK * 512 * 2 + NCHUNK * T * 2)
    R3V = R3 + NCHUNK * 512 * 2 + 2 * NCHUNK * T * 2
    oW = sb("oW", [P, 22 * P], F32R, at=R3V)
    assert R3V + 22 * P * 4 <= SBUF_BYTES, R3V + 22 * P * 4

    DD = R1
    ln2_pass = sb("ln2_pass", [P, GPP * NCHUNK * 512], F32R, at=DD)
    DY = DD + GPP * NCHUNK * 512 * 4
    y_acc = sb("y_acc", [P, TPP * DM], F32, at=DY)
    a_t = sb("a_t", [P, MSUB * 512], F32R, at=DY + TPP * DM * 4)
    sg_t = sb("sg_t", [P, 512], F32, at=DY + TPP * DM * 4 + MSUB * 512 * 4)
    DB = DY + TPP * DM * 4 + MSUB * 512 * 4 + 2048
    gbuf = [sb(f"gbuf{i}", [P, NCHUNK * FT], F32R, at=DB + i * NCHUNK * FT * 4) for i in range(2)]
    ubuf = [sb(f"ubuf{i}", [P, NCHUNK * FT], F32R, at=DB + (2 + i) * NCHUNK * FT * 4) for i in range(2)]
    dbuf = [sb(f"dbuf{i}", [P, MSUB * DM], F32R, at=DB + 4 * NCHUNK * FT * 4 + i * MSUB * DM * 4) for i in range(2)]
    DS = DB + 4 * NCHUNK * FT * 4 + 2 * MSUB * DM * 4
    xg_tm = sb("xg_tm", [P, CAPT * DM], F32, at=DS)
    xg_fm = sb("xg_fm", [P, NCHUNK * CAP], F32R, at=DS + CAPT * DM * 4)
    eo_acc = sb("eo_acc", [P, CAPT * DM], F32, at=DS + CAPT * DM * 4 + NCHUNK * CAP * 4)
    D_END = DS + 2 * CAPT * DM * 4 + NCHUNK * CAP * 4
    assert D_END <= SBUF_BYTES, D_END

    PS = nc.alloc_psum_tensor("PS", [P, 4096], F32)

    def bank(b, width=512, dt=F32):
        ap = PS.ap()[:, b * 512: b * 512 + width]
        return ap.bitcast(dt) if dt != F32 else ap

    from contextlib import ExitStack
    with ExitStack() as ctx:
        sems = {}
        for name in ("wq", "w0", "w1", "xl0", "xl1", "st0", "st1", "so0", "so1",
                     "l2", "l2r", "lt", "ix", "xg", "eo", "eg", "t", "v", "s", "g"):
            sems[name] = ctx.enter_context(nc.semaphore(name))
        block = ctx.enter_context(nc.Block())

        def emit(role, eng):
            cnt = {k: 0 for k in sems}

            def inc(sem, inst, n=1):
                cnt[sem] += n
                if inst is not None:
                    inst.then_inc(sems[sem], n)
                return cnt[sem]

            def wt(r, sem, val):
                if role == r and val > 0:
                    eng.wait_ge(sems[sem], val)

            def MM(f): return f(eng) if role == "tensor" else None
            def DV(f): return f(eng) if role == "vector" else None
            def SC(f): return f(eng) if role == "scalar" else None
            def GP(f): return f(eng) if role == "gpsimd" else None
            def SY(f): return f(eng) if role == "sync" else None

            smc = lambda i_: sm.ap()[:, i_:i_ + 1]

            def ln_stats(src, width=DM, wait_v_scalar=0, wait_xl=None,
                         wait_t_scalar=0, wait_v_vector=0):
                if wait_xl:
                    wt("vector", wait_xl[0], wait_xl[1])
                if wait_v_vector:
                    wt("vector", "v", wait_v_vector)
                i = DV(lambda e: e.reduce_sum(smc(SM_SUM), src(), axis=AX.X))
                tv = inc("v", i)
                if wait_xl:
                    wt("scalar", wait_xl[0], wait_xl[1])
                if wait_v_scalar:
                    wt("scalar", "v", wait_v_scalar)
                if wait_t_scalar:
                    wt("scalar", "t", wait_t_scalar)
                if lt_last[0]:
                    wt("scalar", "lt", lt_last[0])
                i = SC(lambda e: e.activation(lnt.ap()[:, 0:width], src(), AF.Square,
                                              accum_out=smc(SM_SSQ)))
                s_sq = inc("s", i)
                wt("scalar", "v", tv)
                i = SC(lambda e: e.activation(smc(SM_MEAN), smc(SM_SUM), AF.Identity, scale=1.0 / width))
                s_mean = inc("s", i)
                wt("scalar", "s", s_mean)
                i = SC(lambda e: e.activation(smc(SM_M2), smc(SM_MEAN), AF.Square))
                i = SC(lambda e: e.activation(smc(SM_E2), smc(SM_SSQ), AF.Identity, scale=1.0 / width))
                ts_ = inc("s", i)
                wt("vector", "s", ts_)
                i = DV(lambda e: e.tensor_sub(smc(SM_VAR), smc(SM_E2), smc(SM_M2)))
                v_sub = inc("v", i)
                wt("vector", "v", v_sub)
                i = DV(lambda e: e.tensor_scalar_add(smc(SM_VAR), smc(SM_VAR), float(EPS)))
                tv2 = inc("v", i)
                wt("scalar", "v", tv2)
                i = SC(lambda e: e.activation(smc(SM_LNS), smc(SM_VAR), AF.Ln))
                s_lnv = inc("s", i)
                wt("scalar", "s", s_lnv)
                i = SC(lambda e: e.activation(smc(SM_RSTD), smc(SM_LNS), AF.Exp, scale=-0.5))
                ts2 = inc("s", i)
                wt("vector", "s", ts2)
                i = DV(lambda e: e.scalar_tensor_tensor(smc(SM_NMRS), smc(SM_MEAN), -1.0,
                                                        smc(SM_RSTD), ALU.mult, ALU.mult))
                tv3 = inc("v", i)
                wt("scalar", "v", tv3)
                return ts2

            i = SY(lambda e: e.dma_start(out=idf.ap()[:], in_=ident_f)); inc("wq", i, 16)
            i = SY(lambda e: e.dma_start(out=idr.ap()[:], in_=ident_f.bitcast(F32R))); inc("wq", i, 16)
            i = SY(lambda e: e.dma_start(out=idb.ap()[:], in_=ident_b)); inc("wq", i, 16)
            i = SY(lambda e: e.dma_start(
                out=msk.ap()[:].rearrange("p (m q) -> p m q", m=5),
                in_=masks_in.rearrange("m p q -> p m q"))); inc("wq", i, 16)
            i = SY(lambda e: e.dma_start(out=onesr.ap()[:], in_=ones_in.bitcast(F32R))); inc("wq", i, 16)
            i = SY(lambda e: e.dma_start(out=vb_row.ap()[:], in_=wq["vb"].unsqueeze(0).bitcast(F32R))); inc("wq", i, 16)
            i = SY(lambda e: e.dma_start(out=ob_row.ap()[:], in_=wq["ob"].unsqueeze(0).bitcast(F32R))); inc("wq", i, 16)
            for c in range(NCHUNK):
                i = SY(lambda e, c=c: e.dma_start(out=qb_col.ap()[:, c:c + 1],
                                                  in_=wq["qb"][c * P:(c + 1) * P].unsqueeze(1)))
                inc("wq", i, 16)
                i = SY(lambda e, c=c: e.dma_start(out=kb_col.ap()[:, c:c + 1],
                                                  in_=wq["kb"][c * P:(c + 1) * P].unsqueeze(1)))
                inc("wq", i, 16)
            i = SY(lambda e: e.dma_start(out=LT_r.ap()[:], in_=lt_in.bitcast(F32R))); inc("wq", i, 16)
            i = SY(lambda e: e.dma_start(out=onescol_r.ap()[:], in_=rtc2_in[:, 0:1].bitcast(F32R),
                                         allow_slow_non_contiguous=True)); inc("wq", i, 16)
            i = SY(lambda e: e.dma_start(out=iota_f.ap()[:], in_=rtc2_in[:, 1:2],
                                         allow_slow_non_contiguous=True)); inc("wq", i, 16)
            i = SY(lambda e: e.dma_start(out=rtc.ap()[:], in_=rtc_in)); inc("wq", i, 16)
            i = SY(lambda e: e.dma_start(out=gw_rows.ap()[0:E, :], in_=gate_w.bitcast(F32R)))
            w_consts = inc("wq", i, 16)
            i = GP(lambda e: e.memset(idx_sb.ap()[:], 0))
            g_i0 = inc("g", i)
            wt("gpsimd", "g", g_i0)
            i = GP(lambda e: e.dma_start(out=idx_dram.rearrange("(p c) one -> p (c one)", p=P),
                                         in_=idx_sb.ap()[:]))
            inc("ix", i, 16)
            wt("tensor", "wq", w_consts)
            for c in range(NCHUNK):
                i = MM(lambda e, c=c: e.transpose(bank(6, 128, F32R), gw_rows.ap()[:, c * P:(c + 1) * P],
                                                  idr.ap()))
                t_tr = inc("t", i)
                wt("vector", "t", t_tr)
                i = DV(lambda e, c=c: e.tensor_copy(gwT.ap()[:, c * E:(c + 1) * E],
                                                    bank(6, 128, F32R)[:, 0:E]))
                v_e = inc("v", i)
                wt("tensor", "v", v_e)

            st_x1_tile = {}
            st_x1_last_slot = {0: 0, 1: 0}
            so_last_slot = {0: 0, 1: 0}
            bank_free = {}
            lt_last = [0]

            for s in range(n_seq):
                base = s * T
                wt("sync", "t", cnt["t"])
                wt("sync", "v", cnt["v"])
                for pi, pn in enumerate(("q", "k", "v")):
                    for part, lo, hi in (("v", 0, 8), ("r", 8, 15), ("l", 15, 22)):
                        i = SY(lambda e, pn=pn, pi=pi, part=part, lo=lo, hi=hi: e.dma_start(
                            out=qkvW.ap()[:, (pi * 22 + lo) * P:(pi * 22 + hi) * P]
                                .rearrange("p (b q) -> p b q", b=hi - lo),
                            in_=wq[pn + part].rearrange("b p q -> p b q").bitcast(F32R)))
                        inc("wq", i, 16)
                for part, lo, hi in (("v", 0, 8), ("r", 8, 15), ("l", 15, 22)):
                    i = SY(lambda e, part=part, lo=lo, hi=hi: e.dma_start(
                        out=oW.ap()[:, lo * P:hi * P].rearrange("p (b q) -> p b q", b=hi - lo),
                        in_=wq["o" + part].rearrange("b p q -> p b q").bitcast(F32R)))
                    inc("wq", i, 16)
                w_qkv = cnt["wq"]

                s_exp_hist = {}
                t_lnt_guard = [0]
                l2_war = [0]
                slot_guard = {0: ("s", 0), 1: ("s", 0)}
                for g in range(NG):
                    for sub in range(2):
                        for tl in range(2):
                            qt = g * 4 + sub * 2 + tl
                            slot = qt % 2
                            gsem, gval = slot_guard[slot]
                            if gval:
                                wt("gpsimd", gsem, gval)
                            i = GP(lambda e, qt=qt, slot=slot: e.dma_start(
                                out=x_tile.ap()[:, slot * DM:(slot + 1) * DM],
                                in_=x_in[base + qt * P: base + (qt + 1) * P, :]))
                            xl_v = inc(f"xl{slot}", i, 16)
                            src = (lambda slot=slot: x_tile.ap()[:, slot * DM:(slot + 1) * DM])
                            ln_stats(src, wait_xl=(f"xl{slot}", xl_v), wait_t_scalar=t_lnt_guard[0])
                            i = SC(lambda e, slot=slot: e.activation(
                                lnt.ap()[:], x_tile.ap()[:, slot * DM:(slot + 1) * DM],
                                AF.Identity, bias=smc(SM_NMRS), scale=smc(SM_RSTD)))
                            s_ln = inc("s", i)
                            slot_guard[slot] = ("s", s_ln)
                            wt("tensor", "s", s_ln)
                            for c in range(NCHUNK):
                                b = 6 + (c % 2)
                                wt("tensor", "v", bank_free.get(b, 0))
                                i = MM(lambda e, c=c, b=b: e.transpose(
                                    bank(b, 128), lnt.ap()[:, c * P:(c + 1) * P], idf.ap()))
                                t_tr = inc("t", i)
                                wt("vector", "t", t_tr)
                                i = DV(lambda e, c=c, tl=tl, b=b: e.tensor_copy(
                                    ln1_q.ap()[:, c * 256 + tl * P: c * 256 + (tl + 1) * P].bitcast(F32R),
                                    bank(b, 128).bitcast(F32R)))
                                v_ev = inc("v", i)
                                bank_free[b] = v_ev
                            t_lnt_guard[0] = cnt["t"]
                        v_ln1 = cnt["v"]
                        wt("tensor", "wq", w_qkv)
                        wt("tensor", "v", v_ln1)
                        wt("tensor", "s", cnt["s"])
                        for pi, pn in ((0, "q"), (1, "k")):
                            bias_c = qb_col if pn == "q" else kb_col
                            for c in range(NCHUNK):
                                bnk = (0 if pn == "q" else 2) + (c % 2)
                                rlist = urdhva_block_cols(c)
                                wt("tensor", "v", bank_free.get(bnk, 0))
                                i = None
                                for j, (r, wcol) in enumerate(rlist):
                                    i = MM(lambda e, pi=pi, r=r, wcol=wcol, j=j, bnk=bnk, rl=len(rlist):
                                           e.matmul(bank(bnk, 256),
                                                    qkvW.ap()[:, (pi * 22 + wcol) * P:(pi * 22 + wcol + 1) * P],
                                                    ln1_q.ap()[:, r * 256:(r + 1) * 256],
                                                    start=(j == 0), stop=(j == rl - 1)))
                                t_mm = inc("t", i)
                                wt("vector", "t", t_mm)
                                if pn == "q":
                                    dst = lambda c=c, sub=sub: q_grp.ap()[:, c * 512 + sub * 256: c * 512 + (sub + 1) * 256]
                                    scale2 = 0.125
                                else:
                                    dst = lambda c=c, g=g, sub=sub: k_fm.ap()[:, c * T + g * 512 + sub * 256: c * T + g * 512 + (sub + 1) * 256]
                                    scale2 = 1.0
                                i = DV(lambda e, dst=dst, bias_c=bias_c, scale2=scale2, bnk=bnk, c=c:
                                       e.tensor_scalar(dst(), bank(bnk, 256), bias_c.ap()[:, c:c + 1], scale2,
                                                       ALU.add, ALU.mult))
                                v_ev = inc("v", i)
                                bank_free[bnk] = v_ev
                        for tl in range(2):
                            tt = sub * 2 + tl
                            bnk = 4 + (tl % 2)
                            wt("tensor", "v", bank_free.get(bnk, 0))
                            i = None
                            for c in range(NCHUNK):
                                seg = bank(bnk)[:, (c % 4) * P:(c % 4 + 1) * P]
                                rlist = urdhva_block_cols(c)
                                fib = (c % 4 == 0)
                                for j, (r, wcol) in enumerate(rlist):
                                    i = MM(lambda e, r=r, wcol=wcol, j=j, tl=tl, seg=seg, fib=fib:
                                           e.matmul(seg,
                                                    ln1_q.ap()[:, r * 256 + tl * P: r * 256 + (tl + 1) * P],
                                                    qkvW.ap()[:, (2 * 22 + wcol) * P:(2 * 22 + wcol + 1) * P],
                                                    start=(fib and j == 0), stop=False))
                                i = MM(lambda e, c=c, seg=seg: e.matmul(
                                    seg, onesr.ap()[0:1, 0:P], vb_row.ap()[0:1, c * P:(c + 1) * P],
                                    start=False, stop=(c % 4 == 3)))
                                if c % 4 == 3:
                                    t_mm = inc("t", i)
                                    wt("vector", "t", t_mm)
                                    half = c // 4
                                    i2 = DV(lambda e, g=g, tt=tt, half=half, bnk=bnk: e.tensor_copy(
                                        v_tm.ap()[:, (g * 4 + tt) * DM + half * 512:(g * 4 + tt) * DM + (half + 1) * 512],
                                        bank(bnk)))
                                    v_ev = inc("v", i2)
                                    bank_free[bnk] = v_ev
                                    if c == 3:
                                        wt("tensor", "v", v_ev)
                    for qt in range(g * 4, g * 4 + 4):
                        tlq = qt % 4
                        kts = [qt - m for m in MOFF if qt - m >= 0]
                        nb = len(kts)
                        slot = qt % 2
                        gsem, gval = slot_guard[slot]
                        if gval:
                            wt("gpsimd", gsem, gval)
                        i = GP(lambda e, qt=qt, slot=slot: e.dma_start(
                            out=x_tile.ap()[:, slot * DM:(slot + 1) * DM],
                            in_=x_in[base + qt * P: base + (qt + 1) * P, :]))
                        xl_x = inc(f"xl{slot}", i, 16)

                        v_av_evac = 0
                        t_sc_h = {}
                        v_diag_h = {}
                        t_post_softB = {}

                        def emit_qk(h):
                            ch = h // 2
                            po = 64 * (h % 2)
                            sb0 = 0 if h % 2 == 0 else 2
                            prev = s_exp_hist.get((qt, h - 2))
                            if prev is None and h < 2:
                                prev = s_exp_hist.get((qt - 1, h + 14))
                            if h == 1:
                                pg_ = s_exp_hist.get((qt - 1, "gate"))
                                if pg_:
                                    prev = max(prev or 0, pg_)
                            if prev:
                                wt("tensor", "s", prev)
                            i = None
                            for j, kt in enumerate(kts):
                                sbank = sb0 if j < 4 else sb0 + 1
                                seg = bank(sbank)[:, (j % 4) * P:(j % 4 + 1) * P]
                                i = MM(lambda e, seg=seg, ch=ch, po=po, tlq=tlq, kt=kt, j=j:
                                       e.matmul(seg,
                                                q_grp.ap()[po:po + 64, ch * 512 + tlq * P: ch * 512 + (tlq + 1) * P],
                                                k_fm.ap()[po:po + 64, ch * T + kt * P: ch * T + (kt + 1) * P],
                                                start=(j % 4 == 0), stop=False))
                                mi = MOFF.index(qt - kt)
                                i = MM(lambda e, seg=seg, mi=mi, j=j, nb=nb:
                                       e.matmul(seg, idb.ap(), msk.ap()[:, mi * P:(mi + 1) * P],
                                                start=False, stop=(j == nb - 1 or j == 3)))
                            t_sc_h[h] = inc("t", i)

                        def emit_softA(h):
                            sb0 = 0 if h % 2 == 0 else 2
                            pr = h % 2
                            ESX = SM_ESUM if pr == 0 else SM_ESUM1
                            LNX = SM_LNS if pr == 0 else SM_LNS1
                            SIX = SM_SINV if pr == 0 else SM_SINV1
                            pb = probs.ap()[:, pr * 5 * P:(pr * 5 + 5) * P]
                            db = diag.ap()[:, pr * P:(pr + 1) * P]
                            wt("scalar", "t", t_sc_h[h])
                            if h - 2 in t_post_softB:
                                wt("scalar", "t", t_post_softB[h - 2])
                            i = SC(lambda e, sb0=sb0, nb=nb, ESX=ESX, pb=pb: e.activation(
                                pb[:, 0:nb * P], PS.ap()[:, sb0 * 512: sb0 * 512 + nb * P],
                                AF.Exp, accum_out=smc(ESX)))
                            s_exp = inc("s", i)
                            s_exp_hist[(qt, h)] = s_exp
                            wt("scalar", "s", s_exp)
                            i = SC(lambda e, ESX=ESX, LNX=LNX: e.activation(smc(LNX), smc(ESX), AF.Ln))
                            s_l = inc("s", i)
                            wt("scalar", "s", s_l)
                            i = SC(lambda e, LNX=LNX, SIX=SIX: e.activation(smc(SIX), smc(LNX), AF.Exp, scale=-1.0))
                            s_sinv = inc("s", i)
                            wt("vector", "s", s_sinv)
                            i = DV(lambda e, SIX=SIX, db=db: e.tensor_scalar(db, idb.ap(), smc(SIX), None, ALU.mult))
                            v_diag_h[h] = inc("v", i)

                        def emit_softB(h):
                            nonlocal v_av_evac
                            ch = h // 2
                            po = 64 * (h % 2)
                            pr = h % 2
                            pb = probs.ap()[:, pr * 5 * P:(pr * 5 + 5) * P]
                            kb = probs_km.ap()[:, pr * 5 * P:(pr * 5 + 5) * P]
                            db = diag.ap()[:, pr * P:(pr + 1) * P]
                            avb = 6 + (h % 2)
                            wt("tensor", "v", v_diag_h[h])
                            wt("tensor", "v", bank_free.get(avb, 0))
                            wt("tensor", "v", bank_free.get(4, 0))
                            if nb > 4:
                                wt("tensor", "v", bank_free.get(5, 0))
                            i = None
                            for j in range(nb):
                                trb = 4 if j < 4 else 5
                                seg = bank(trb)[:, (j % 4) * P:(j % 4 + 1) * P]
                                i = MM(lambda e, j=j, seg=seg, pb=pb, db=db: e.matmul(
                                    seg, pb[:, j * P:(j + 1) * P], db,
                                    start=True, stop=True))
                            t_tr = inc("t", i)
                            wt("vector", "t", t_tr)
                            nb4 = min(nb, 4)
                            i2 = DV(lambda e, nb4=nb4, kb=kb: e.tensor_copy(
                                kb[:, 0:nb4 * P], bank(4, nb4 * P)))
                            v_km = inc("v", i2)
                            bank_free[4] = v_km
                            if nb > 4:
                                i2 = DV(lambda e, kb=kb: e.tensor_copy(
                                    kb[:, 4 * P:5 * P], bank(5, 128)))
                                v_km = inc("v", i2)
                                bank_free[5] = v_km
                            wt("tensor", "v", v_km)
                            i = None
                            for j in range(nb):
                                i = MM(lambda e, j=j, kt=kts[j], h=h, avb=avb, nb=nb, kb=kb: e.matmul(
                                    bank(avb, 128)[0:64, :],
                                    v_tm.ap()[:, kt * DM + h * 64: kt * DM + (h + 1) * 64],
                                    kb[:, j * P:(j + 1) * P],
                                    start=(j == 0), stop=(j == nb - 1)))
                            t_av = inc("t", i)
                            t_post_softB[h] = t_av
                            wt("vector", "t", t_av)
                            i = DV(lambda e, ch=ch, po=po, avb=avb: e.tensor_copy(
                                attn_fm.ap()[po:po + 64, ch * P:(ch + 1) * P].bitcast(F32R),
                                bank(avb, 128)[0:64, :].bitcast(F32R)))
                            v_av_evac = inc("v", i)
                            bank_free[avb] = v_av_evac

                        for step in range(H + 2):
                            if step < H:
                                emit_qk(step)
                            if 1 <= step <= H:
                                emit_softA(step - 1)
                            if step >= 2:
                                emit_softB(step - 2)

                        wt("tensor", "wq", w_qkv)
                        wt("tensor", "v", v_av_evac)
                        ov = PS.ap()[:, 3 * 1024: 4 * 1024]
                        i = None
                        for c in range(NCHUNK):
                            rlist = urdhva_block_cols(c)
                            for j, (r, wcol) in enumerate(rlist):
                                i = MM(lambda e, c=c, r=r, wcol=wcol, j=j, ov=ov:
                                       e.matmul(ov[:, c * P:(c + 1) * P],
                                                attn_fm.ap()[:, r * P:(r + 1) * P],
                                                oW.ap()[:, wcol * P:(wcol + 1) * P],
                                                start=(c % 4 == 0 and j == 0), stop=False))
                            i = MM(lambda e, c=c, ov=ov: e.matmul(
                                ov[:, c * P:(c + 1) * P], onesr.ap()[0:1, 0:P],
                                ob_row.ap()[0:1, c * P:(c + 1) * P],
                                start=False, stop=(c % 4 == 3)))
                        t_oproj = inc("t", i)
                        wt("vector", "t", t_oproj)
                        i = DV(lambda e, ov=ov: e.tensor_copy(o_tm.ap()[:], ov))
                        v_otm = inc("v", i)
                        bank_free[6] = v_otm
                        bank_free[7] = v_otm
                        ln_stats(lambda: o_tm.ap()[:], wait_v_scalar=v_otm,
                                 wait_v_vector=v_otm, wait_t_scalar=t_lnt_guard[0])
                        i = SC(lambda e: e.activation(o_ln.ap()[:], o_tm.ap()[:], AF.Identity,
                                                      bias=smc(SM_NMRS), scale=smc(SM_RSTD)))
                        s_oln = inc("s", i)
                        wt("vector", "s", s_oln)
                        wt("vector", f"xl{slot}", xl_x)
                        if st_x1_last_slot[slot]:
                            wt("vector", f"st{slot}", st_x1_last_slot[slot])
                        if so_last_slot[slot]:
                            wt("vector", f"so{slot}", so_last_slot[slot])
                        x1s = (qt % 2) * DM
                        i = DV(lambda e, slot=slot, x1s=x1s: e.tensor_add(
                            x1_t.ap()[:, x1s:x1s + DM], x_tile.ap()[:, slot * DM:(slot + 1) * DM], o_ln.ap()[:]))
                        v_x1 = inc("v", i)
                        slot_guard[slot] = ("v", v_x1)
                        wt("gpsimd", "v", v_x1)
                        i = GP(lambda e, qt=qt, x1s=x1s: e.dma_start(
                            out=x1_dram[base + qt * P: base + (qt + 1) * P, :], in_=x1_t.ap()[:, x1s:x1s + DM]))
                        st_x1_tile[(s, qt)] = inc(f"st{slot}", i, 16)
                        st_x1_last_slot[slot] = st_x1_tile[(s, qt)]
                        ln_stats(lambda x1s=x1s: x1_t.ap()[:, x1s:x1s + DM], wait_v_scalar=v_x1,
                                 wait_v_vector=v_x1)
                        i = SC(lambda e, x1s=x1s: e.activation(lnt.ap()[:], x1_t.ap()[:, x1s:x1s + DM],
                                                               AF.Identity, bias=smc(SM_NMRS), scale=smc(SM_RSTD)))
                        s_ln2 = inc("s", i)
                        wt("gpsimd", "s", s_ln2)
                        i = GP(lambda e, qt=qt: e.dma_start(
                            out=ln2tm_dram[base + qt * P: base + (qt + 1) * P, :], in_=lnt.ap()[:]))
                        lt_last[0] = inc("lt", i, 16)
                        wt("tensor", "s", s_ln2)
                        for c in range(NCHUNK):
                            b = 4 + (c % 2)
                            wt("tensor", "v", bank_free.get(b, 0))
                            i = MM(lambda e, c=c, b=b: e.transpose(
                                bank(b, 128), lnt.ap()[:, c * P:(c + 1) * P], idf.ap()))
                            t_tr = inc("t", i)
                            wt("vector", "t", t_tr)
                            if tlq == 0 and c == 0 and l2_war[0]:
                                wt("vector", "l2", l2_war[0])
                            i = DV(lambda e, c=c, tlq=tlq, b=b: e.tensor_copy(
                                ln2_grp.ap()[:, c * 512 + tlq * P: c * 512 + (tlq + 1) * P],
                                bank(b, 128).bitcast(F32R)))
                            v_ev = inc("v", i)
                            bank_free[b] = v_ev
                        t_lnt_guard[0] = cnt["t"]
                        gv = bank(2)[:, 0:E]
                        wt("tensor", "v", cnt["v"])
                        wt("tensor", "s", s_exp_hist[(qt, 15)])
                        i = None
                        for c in range(NCHUNK):
                            i = MM(lambda e, c=c, tlq=tlq, gv=gv: e.matmul(
                                gv, ln2_grp.ap()[:, c * 512 + tlq * P: c * 512 + (tlq + 1) * P],
                                gwT.ap()[:, c * E:(c + 1) * E],
                                start=(c == 0), stop=(c == NCHUNK - 1)))
                        t_gate = inc("t", i)
                        wt("vector", "t", t_gate)
                        i = DV(lambda e, gv=gv: e.reduce_max(smc(SM_NEGMAX), gv, axis=AX.X, negate=True))
                        v_gm = inc("v", i)
                        wt("scalar", "v", v_gm)
                        i = SC(lambda e, gv=gv: e.activation(p8.ap()[:], gv, AF.Exp,
                                                             bias=smc(SM_NEGMAX), accum_out=smc(SM_GES)))
                        s_e8 = inc("s", i)
                        s_exp_hist[(qt, "gate")] = s_e8
                        wt("scalar", "s", s_e8)
                        i = SC(lambda e: e.activation(smc(SM_GLN), smc(SM_GES), AF.Ln))
                        s_l8 = inc("s", i)
                        wt("scalar", "s", s_l8)
                        i = SC(lambda e: e.activation(smc(SM_GSI), smc(SM_GLN), AF.Exp, scale=-1.0))
                        s_g = inc("s", i)
                        wt("vector", "s", s_g)
                        i = DV(lambda e: e.tensor_scalar(p8.ap()[:], p8.ap()[:], smc(SM_GSI), None, ALU.mult))
                        v_p8 = inc("v", i)
                        wt("vector", "v", v_p8)
                        i = DV(lambda e: e.reduce_max(smc(SM_M1), p8.ap()[:], axis=AX.X))
                        v_m1 = inc("v", i)
                        wt("vector", "v", v_m1)
                        i = DV(lambda e: e.tensor_scalar(keep8.ap()[:], p8.ap()[:], smc(SM_M1), None, ALU.is_ge))
                        v_k1 = inc("v", i)
                        wt("vector", "v", v_k1)
                        i = DV(lambda e: e.scalar_tensor_tensor(pw8.ap()[:], keep8.ap()[:], -2.0,
                                                                p8.ap()[:], ALU.mult, ALU.add))
                        v_pw = inc("v", i)
                        wt("vector", "v", v_pw)
                        i = DV(lambda e: e.reduce_max(smc(SM_MM2), pw8.ap()[:], axis=AX.X))
                        v_m2 = inc("v", i)
                        wt("vector", "v", v_m2)
                        i = DV(lambda e: e.tensor_scalar(keep8.ap()[:], p8.ap()[:], smc(SM_MM2), None, ALU.is_ge))
                        v_k2 = inc("v", i)
                        wt("vector", "v", v_k2)
                        i = DV(lambda e, qt=qt: e.tensor_mul(comb.ap()[:, qt * E:(qt + 1) * E],
                                                             p8.ap()[:], keep8.ap()[:]))
                        g_comb = inc("v", i)

                    wt("gpsimd", "v", cnt["v"])
                    i = GP(lambda e, s=s, g=g: e.dma_start(
                        out=ln2_dram[s, g].rearrange("c p t -> p c t").bitcast(F32R),
                        in_=ln2_grp.ap()[:].rearrange("p (c t) -> p c t", c=NCHUNK)))
                    l2_war[0] = inc("l2", i, 16)

                v_comb_done = cnt["v"]
                g_comb_done = cnt["g"]

                for p_i in range(NPASS):
                    tok0 = p_i * PASS_TOK
                    t_pass0 = cnt["t"]
                    wt("tensor", "v", cnt["v"])
                    wt("tensor", "s", cnt["s"])
                    wt("tensor", "g", cnt["g"])
                    wt("sync", "t", cnt["t"])
                    wt("sync", "v", cnt["v"])
                    wt("sync", "l2", cnt["l2"])
                    i = SY(lambda e, s=s, p_i=p_i: e.dma_start(
                        out=ln2_pass.ap()[:].rearrange("p (g c t) -> p g c t", g=GPP, c=NCHUNK),
                        in_=ln2_dram[s, p_i * GPP:(p_i + 1) * GPP].rearrange("g c p t -> p g c t").bitcast(F32R)))
                    l2r_v = inc("l2r", i, 16)
                    wt("tensor", "l2r", l2r_v)

                    wt("vector", "v", cnt["v"])
                    i = DV(lambda e: e.tensor_scalar(base_row.ap()[:], onesr.ap()[0:1, 0:E].bitcast(F32),
                                                     0.0, None, ALU.mult))
                    v_b0 = inc("v", i)
                    wt("vector", "v", v_b0)
                    wt("gpsimd", "xg", cnt["xg"])
                    wt("vector", "eg", cnt["eg"])
                    ix_prev = [cnt["ix"]]
                    def routing_tile(tt):
                        qtl = tok0 // P + tt
                        cslice = comb.ap()[:, qtl * E:(qtl + 1) * E]
                        i = DV(lambda e, cslice=cslice: e.tensor_scalar(
                            ass_r.ap()[:], cslice, 0.0, None, ALU.is_gt))
                        v_ass = inc("v", i)
                        wt("tensor", "v", v_ass)
                        wt("tensor", "v", bank_free.get(0, 0))
                        i = MM(lambda e: e.matmul(bank(0)[:, 0:E], LT_r.ap(), ass_r.ap(),
                                                  start=True, stop=False))
                        i = MM(lambda e: e.matmul(bank(0)[:, 0:E], onesr.ap()[0:1, 0:P],
                                                  base_row.ap(), start=False, stop=True))
                        t_pos = inc("t", i)
                        i = MM(lambda e: e.matmul(bank(1)[0:1, 0:E], onescol_r.ap(), ass_r.ap(),
                                                  start=True, stop=True))
                        t_cnt = inc("t", i)
                        wt("vector", "t", t_cnt)
                        i = DV(lambda e: e.tensor_copy(pos_sb.ap()[:], bank(0)[:, 0:E]))
                        i = DV(lambda e: e.tensor_tensor(base_row.ap()[:],
                                                         base_row.ap()[:].bitcast(F32),
                                                         bank(1)[0:1, 0:E], ALU.add))
                        v_pos = inc("v", i)
                        bank_free[0] = v_pos
                        bank_free[1] = v_pos
                        wt("vector", "v", v_pos)
                        RS = rt_s.ap()
                        def dvstep(f):
                            ii = DV(f)
                            vv = inc("v", ii)
                            wt("vector", "v", vv)
                            return vv
                        dvstep(lambda e: e.tensor_scalar(RS[:, 0:E], pos_sb.ap()[:], float(CAP - 1), None, ALU.min))
                        dvstep(lambda e: e.tensor_scalar(RS[:, 8:8 + E], RS[:, 0:E], 128.0, None, ALU.is_ge))
                        dvstep(lambda e: e.tensor_scalar(p8.ap()[:], RS[:, 0:E], 256.0, None, ALU.is_ge))
                        dvstep(lambda e: e.tensor_add(RS[:, 8:8 + E], RS[:, 8:8 + E], p8.ap()[:]))
                        dvstep(lambda e: e.scalar_tensor_tensor(RS[:, 0:E], RS[:, 8:8 + E], -128.0,
                                                                RS[:, 0:E], ALU.mult, ALU.add))
                        dvstep(lambda e: e.scalar_tensor_tensor(RS[:, 8:8 + E], RS[:, 0:E],
                                                                float(E * CAPT), RS[:, 8:8 + E],
                                                                ALU.mult, ALU.add))
                        dvstep(lambda e: e.tensor_add(RS[:, 8:8 + E], RS[:, 8:8 + E], rtc.ap()[:, E:2 * E]))
                        dvstep(lambda e: e.tensor_scalar(RS[:, 0:E], pos_sb.ap()[:], float(CAP - 1), None, ALU.min))
                        dvstep(lambda e: e.tensor_add(RS[:, 0:E], RS[:, 0:E], rtc.ap()[:, 2 * E:3 * E]))
                        i = DV(lambda e: e.tensor_mul(pos_sb.ap()[:], ass_r.ap()[:].bitcast(F32), rtc.ap()[:, 0:E]))
                        v_sc = inc("v", i)
                        wt("vector", "v", v_sc)
                        i = DV(lambda e: e.reduce_max(smc(SM_M1), pos_sb.ap()[:], axis=AX.X))
                        v_m1 = inc("v", i)
                        wt("vector", "v", v_m1)
                        i = DV(lambda e: e.tensor_scalar(keep8.ap()[:], pos_sb.ap()[:], smc(SM_M1), None, ALU.is_ge))
                        v_s1 = inc("v", i)
                        wt("vector", "v", v_s1)
                        i = DV(lambda e: e.tensor_sub(pw8.ap()[:], ass_r.ap()[:].bitcast(F32), keep8.ap()[:]))
                        v_s2 = inc("v", i)
                        wt("vector", "v", v_s2)
                        wt("vector", "ix", ix_prev[0])
                        for k, selap in ((0, keep8), (1, pw8)):
                            dvstep(lambda e, selap=selap, cslice=cslice: e.tensor_mul(p8.ap()[:], selap.ap()[:], cslice))
                            dvstep(lambda e, tt=tt, k=k: e.reduce_sum(w12_sb.ap()[:, tt * 2 + k: tt * 2 + k + 1],
                                                                      p8.ap()[:], axis=AX.X))
                            dvstep(lambda e, selap=selap: e.tensor_mul(p8.ap()[:], selap.ap()[:], rt_s.ap()[:, 0:8]))
                            dvstep(lambda e: e.reduce_sum(smc(SM_MM2), p8.ap()[:], axis=AX.X))
                            dvstep(lambda e, tt=tt, k=k: e.tensor_copy(
                                sl_sb.ap()[:, tt * 2 + k: tt * 2 + k + 1], smc(SM_MM2)))
                            dvstep(lambda e, selap=selap: e.tensor_mul(p8.ap()[:], selap.ap()[:], rt_s.ap()[:, 8:16]))
                            dvstep(lambda e: e.reduce_sum(smc(SM_MM2), p8.ap()[:], axis=AX.X))
                            dvstep(lambda e, k=k: e.tensor_copy(ad_sb.ap()[:, k:k + 1], smc(SM_MM2)))
                        dvstep(lambda e, qtl=qtl: e.tensor_scalar(tok_sb.ap()[:, 0:1].bitcast(F32),
                                                                  iota_f.ap()[:], float(base + qtl * P),
                                                                  None, ALU.add))
                        v_tok = dvstep(lambda e: e.tensor_copy(tok_sb.ap()[:, 1:2], tok_sb.ap()[:, 0:1].bitcast(F32)))
                        wt("gpsimd", "v", v_tok)
                        for k in range(2):
                            i = GP(lambda e, k=k: e.indirect_dma_start(
                                out=idx_dram,
                                out_offset=bass.IndirectOffsetOnAxis(ap=ad_sb.ap()[:, k:k + 1], axis=0),
                                in_=tok_sb.ap()[:, 1:2], in_offset=None))
                            inc("ix", i, 16)
                        ix_prev[0] = cnt["ix"]
                    t_down_done = {}
                    v_last_evac = 0
                    prev_va = 0
                    y_war = {}
                    first_flag = True
                    tiles = [("sh", 0, f_i) for f_i in range(NFTS)]
                    for ti, (kind, e_i, f_i) in enumerate(tiles):
                        par = ti % 2
                        if ti >= 2:
                            wt("sync", "t", t_down_done[ti - 2])
                        else:
                            wt("sync", "t", t_pass0)
                        gsrc = sh_gate.rearrange("(c p) f -> p c f", p=P)[:, :, f_i * FT:(f_i + 1) * FT]
                        usrc = sh_up.rearrange("(c p) f -> p c f", p=P)[:, :, f_i * FT:(f_i + 1) * FT]
                        dsrc = sh_down[f_i * FT:(f_i + 1) * FT, :].rearrange("(c p) d -> p c d", p=P)
                        i = SY(lambda e, par=par, gsrc=gsrc: e.dma_start(
                            out=gbuf[par].ap()[:].rearrange("p (c f) -> p c f", c=NCHUNK),
                            in_=gsrc.bitcast(F32R)))
                        inc(f"w{par}", i, 16)
                        wt("scalar", "t", t_down_done.get(ti - 2, t_pass0))
                        i = SC(lambda e, par=par, usrc=usrc: e.dma_start(
                            out=ubuf[par].ap()[:].rearrange("p (c f) -> p c f", c=NCHUNK),
                            in_=usrc.bitcast(F32R)))
                        inc(f"w{par}", i, 16)
                        i = SC(lambda e, par=par, dsrc=dsrc: e.dma_start(
                            out=dbuf[par].ap()[:].rearrange("p (c d) -> p c d", c=MSUB),
                            in_=dsrc.bitcast(F32R)))
                        w_tile = inc(f"w{par}", i, 16)
                        wt("tensor", f"w{par}", w_tile)
                        for g_i in range(GPP):
                            for m_i in range(MSUB):
                                pg = bank(0 + (m_i % 2))
                                pu = bank(2 + (m_i % 2))
                                wt("tensor", "v", bank_free.get(0 + (m_i % 2), 0))
                                i = None
                                for c in range(NCHUNK):
                                    i = MM(lambda e, c=c, m_i=m_i, par=par, g_i=g_i, pg=pg:
                                           e.matmul(pg, gbuf[par].ap()[:, c * FT + m_i * P: c * FT + (m_i + 1) * P],
                                                    ln2_pass.ap()[:, (g_i * NCHUNK + c) * 512:(g_i * NCHUNK + c + 1) * 512],
                                                    start=(c == 0), stop=(c == NCHUNK - 1)))
                                t_g = inc("t", i)
                                i = None
                                for c in range(NCHUNK):
                                    i = MM(lambda e, c=c, m_i=m_i, par=par, g_i=g_i, pu=pu:
                                           e.matmul(pu, ubuf[par].ap()[:, c * FT + m_i * P: c * FT + (m_i + 1) * P],
                                                    ln2_pass.ap()[:, (g_i * NCHUNK + c) * 512:(g_i * NCHUNK + c + 1) * 512],
                                                    start=(c == 0), stop=(c == NCHUNK - 1)))
                                t_u = inc("t", i)
                                wt("scalar", "t", t_g)
                                if prev_va:
                                    wt("scalar", "v", prev_va)
                                i = SC(lambda e, pg=pg: e.activation(sg_t.ap()[:], pg, AF.Silu))
                                s_silu = inc("s", i)
                                wt("vector", "s", s_silu)
                                wt("vector", "t", t_u)
                                i = DV(lambda e, m_i=m_i, pu=pu: e.tensor_mul(
                                    a_t.ap()[:, m_i * 512:(m_i + 1) * 512], sg_t.ap()[:], pu))
                                v_a = inc("v", i)
                                prev_va = v_a
                                bank_free[0 + (m_i % 2)] = v_a
                                bank_free[2 + (m_i % 2)] = v_a
                            wt("tensor", "v", prev_va)
                            for tt in range(4):
                                pe_v = PS.ap()[:, (2 + (tt % 2)) * 1024:(3 + (tt % 2)) * 1024]
                                wt("tensor", "v", max(bank_free.get(4 + 2 * (tt % 2), 0),
                                                      bank_free.get(5 + 2 * (tt % 2), 0)))
                                i = None
                                for m_i in range(MSUB):
                                    for half in range(2):
                                        i = MM(lambda e, m_i=m_i, half=half, par=par, tt=tt, g_i=g_i, pe_v=pe_v:
                                               e.matmul(pe_v[:, half * 512:(half + 1) * 512],
                                                        a_t.ap()[:, m_i * 512 + tt * P: m_i * 512 + (tt + 1) * P],
                                                        dbuf[par].ap()[:, m_i * DM + half * 512: m_i * DM + (half + 1) * 512],
                                                        start=(m_i == 0), stop=(m_i == MSUB - 1)))
                                t_eo = inc("t", i)
                                wt("vector", "t", t_eo)
                                ty = (g_i * 4 + tt) * DM
                                if ty in y_war:
                                    wt("vector", "v", y_war[ty])
                                if first_flag:
                                    i = DV(lambda e, ty=ty, pe_v=pe_v: e.tensor_copy(
                                        y_acc.ap()[:, ty:ty + DM], pe_v))
                                else:
                                    i = DV(lambda e, ty=ty, pe_v=pe_v: e.tensor_tensor(
                                        y_acc.ap()[:, ty:ty + DM], y_acc.ap()[:, ty:ty + DM],
                                        pe_v, ALU.add))
                                v_last_evac = inc("v", i)
                                y_war[ty] = v_last_evac
                                bank_free[4 + 2 * (tt % 2)] = v_last_evac
                                bank_free[5 + 2 * (tt % 2)] = v_last_evac
                        first_flag = False
                        t_down_done[ti] = cnt["t"]
                        if ti < TPP:
                            routing_tile(ti)

                    wt("gpsimd", "ix", cnt["ix"])
                    i = GP(lambda e: e.dma_start(out=idx_sb.ap()[:],
                                                 in_=idx_dram.rearrange("(p c) one -> p (c one)", p=P)))
                    ix_ld = inc("ix", i, 16)


                    t_xg_read = 0
                    eo_par_sem = {}
                    wpar_base = NFTS

                    def emit_gathers(e_i):
                        wt("gpsimd", "ix", ix_ld)
                        wt("gpsimd", "lt", cnt["lt"])
                        wt("gpsimd", "t", t_xg_read)
                        for b in range(CAPT):
                            i = GP(lambda e, e_i=e_i, b=b: e.indirect_dma_start(
                                out=xg_tm.ap()[:, b * DM:(b + 1) * DM],
                                out_offset=None,
                                in_=ln2tm_dram,
                                in_offset=bass.IndirectOffsetOnAxis(
                                    ap=idx_sb.ap()[:, e_i * CAPT + b: e_i * CAPT + b + 1], axis=0)))
                            inc("xg", i, 16)
                        return cnt["xg"]

                    xg_ready = emit_gathers(0)
                    for e_i in range(E):
                        wt("tensor", "xg", xg_ready)
                        for b in range(CAPT):
                            for c in range(NCHUNK):
                                bk = 0 + (c % 2)
                                wt("tensor", "v", bank_free.get(bk, 0))
                                i = MM(lambda e, b=b, c=c, bk=bk: e.transpose(
                                    bank(bk, 128), xg_tm.ap()[:, b * DM + c * P: b * DM + (c + 1) * P],
                                    idf.ap()))
                                t_tr = inc("t", i)
                                wt("vector", "t", t_tr)
                                i = DV(lambda e, b=b, c=c, bk=bk: e.tensor_copy(
                                    xg_fm.ap()[:, c * CAP + b * P: c * CAP + (b + 1) * P].bitcast(F32R),
                                    bank(bk, 128).bitcast(F32R)))
                                v_ev = inc("v", i)
                                bank_free[bk] = v_ev
                        t_xg_read = cnt["t"]
                        v_xg_fm = cnt["v"]
                        wt("tensor", "v", v_xg_fm)
                        if e_i + 1 < E:
                            xg_ready = emit_gathers(e_i + 1)

                        for f_i in range(NFT):
                            ti = wpar_base + e_i * NFT + f_i
                            par = ti % 2
                            if ti >= 2:
                                wt("sync", "t", t_down_done[ti - 2])
                            gsrc = gate_k[e_i].rearrange("(c p) f -> p c f", p=P)[:, :, f_i * FT:(f_i + 1) * FT]
                            usrc = up_k[e_i].rearrange("(c p) f -> p c f", p=P)[:, :, f_i * FT:(f_i + 1) * FT]
                            dsrc = down_k[e_i][f_i * FT:(f_i + 1) * FT, :].rearrange("(c p) d -> p c d", p=P)
                            i = SY(lambda e, par=par, gsrc=gsrc: e.dma_start(
                                out=gbuf[par].ap()[:].rearrange("p (c f) -> p c f", c=NCHUNK),
                                in_=gsrc.bitcast(F32R)))
                            inc(f"w{par}", i, 16)
                            wt("scalar", "t", t_down_done.get(ti - 2, t_pass0))
                            i = SC(lambda e, par=par, usrc=usrc: e.dma_start(
                                out=ubuf[par].ap()[:].rearrange("p (c f) -> p c f", c=NCHUNK),
                                in_=usrc.bitcast(F32R)))
                            inc(f"w{par}", i, 16)
                            i = SC(lambda e, par=par, dsrc=dsrc: e.dma_start(
                                out=dbuf[par].ap()[:].rearrange("p (c d) -> p c d", c=MSUB),
                                in_=dsrc.bitcast(F32R)))
                            w_tile = inc(f"w{par}", i, 16)
                            wt("tensor", f"w{par}", w_tile)
                            for m_i in range(MSUB):
                                pg = bank(0 + (m_i % 2))
                                pu = bank(2 + (m_i % 2))
                                wt("tensor", "v", bank_free.get(0 + (m_i % 2), 0))
                                i = None
                                for c in range(NCHUNK):
                                    i = MM(lambda e, c=c, m_i=m_i, par=par, pg=pg:
                                           e.matmul(pg[:, 0:CAP],
                                                    gbuf[par].ap()[:, c * FT + m_i * P: c * FT + (m_i + 1) * P],
                                                    xg_fm.ap()[:, c * CAP:(c + 1) * CAP],
                                                    start=(c == 0), stop=(c == NCHUNK - 1)))
                                t_g = inc("t", i)
                                i = None
                                for c in range(NCHUNK):
                                    i = MM(lambda e, c=c, m_i=m_i, par=par, pu=pu:
                                           e.matmul(pu[:, 0:CAP],
                                                    ubuf[par].ap()[:, c * FT + m_i * P: c * FT + (m_i + 1) * P],
                                                    xg_fm.ap()[:, c * CAP:(c + 1) * CAP],
                                                    start=(c == 0), stop=(c == NCHUNK - 1)))
                                t_u = inc("t", i)
                                wt("scalar", "t", t_g)
                                if prev_va:
                                    wt("scalar", "v", prev_va)
                                i = SC(lambda e, pg=pg: e.activation(sg_t.ap()[:, 0:CAP], pg[:, 0:CAP], AF.Silu))
                                s_silu = inc("s", i)
                                wt("vector", "s", s_silu)
                                wt("vector", "t", t_u)
                                i = DV(lambda e, m_i=m_i, pu=pu: e.tensor_mul(
                                    a_t.ap()[:, m_i * 512:m_i * 512 + CAP], sg_t.ap()[:, 0:CAP], pu[:, 0:CAP]))
                                v_a = inc("v", i)
                                prev_va = v_a
                                bank_free[0 + (m_i % 2)] = v_a
                                bank_free[2 + (m_i % 2)] = v_a
                            wt("tensor", "v", prev_va)
                            for tt in range(CAPT):
                                pe_v = PS.ap()[:, (2 + (tt % 2)) * 1024:(3 + (tt % 2)) * 1024]
                                wt("tensor", "v", max(bank_free.get(4 + 2 * (tt % 2), 0),
                                                      bank_free.get(5 + 2 * (tt % 2), 0)))
                                i = None
                                for m_i in range(MSUB):
                                    for half in range(2):
                                        i = MM(lambda e, m_i=m_i, half=half, par=par, tt=tt, pe_v=pe_v:
                                               e.matmul(pe_v[:, half * 512:(half + 1) * 512],
                                                        a_t.ap()[:, m_i * 512 + tt * P: m_i * 512 + (tt + 1) * P],
                                                        dbuf[par].ap()[:, m_i * DM + half * 512: m_i * DM + (half + 1) * 512],
                                                        start=(m_i == 0), stop=(m_i == MSUB - 1)))
                                t_eo = inc("t", i)
                                wt("vector", "t", t_eo)
                                ty = tt * DM
                                if f_i == 0:
                                    if ("eo", tt) in eo_par_sem:
                                        wt("vector", "eo", eo_par_sem[("eo", tt)])
                                    i = DV(lambda e, ty=ty, pe_v=pe_v: e.tensor_copy(
                                        eo_acc.ap()[:, ty:ty + DM], pe_v))
                                else:
                                    i = DV(lambda e, ty=ty, pe_v=pe_v: e.tensor_tensor(
                                        eo_acc.ap()[:, ty:ty + DM], eo_acc.ap()[:, ty:ty + DM],
                                        pe_v, ALU.add))
                                v_last_evac = inc("v", i)
                                bank_free[4 + 2 * (tt % 2)] = v_last_evac
                                bank_free[5 + 2 * (tt % 2)] = v_last_evac
                            t_down_done[ti] = cnt["t"]
                        v_eo_done = cnt["v"]
                        wt("gpsimd", "v", v_eo_done)
                        wt("gpsimd", "eg", cnt["eg"])
                        for b in range(CAPT):
                            i = GP(lambda e, e_i=e_i, b=b: e.dma_start(
                                out=eo_dram[e_i * CAP + b * P: e_i * CAP + (b + 1) * P, :],
                                in_=eo_acc.ap()[:, b * DM:(b + 1) * DM]))
                            inc("eo", i, 16)
                        for b in range(CAPT):
                            eo_par_sem[("eo", b)] = cnt["eo"]
                    eo_all = cnt["eo"]

                    for tt in range(TPP):
                        qt = tok0 // P + tt
                        ty = tt * DM
                        wt("gpsimd", "eo", eo_all)
                        wt("gpsimd", "t", t_xg_read)
                        wt("gpsimd", "v", y_war.get(ty, 0))
                        g_vals = []
                        for k in range(2):
                            i = GP(lambda e, tt=tt, k=k: e.indirect_dma_start(
                                out=xg_tm.ap()[:, k * DM:(k + 1) * DM],
                                out_offset=None,
                                in_=eo_dram,
                                in_offset=bass.IndirectOffsetOnAxis(
                                    ap=sl_sb.ap()[:, tt * 2 + k: tt * 2 + k + 1], axis=0)))
                            g_vals.append(inc("eg", i, 16))
                        wt("vector", "eg", g_vals[1])
                        i = DV(lambda e, ty=ty, tt=tt: e.scalar_tensor_tensor(
                            o_tm.ap()[:], xg_tm.ap()[:, 0:DM], w12_sb.ap()[:, tt * 2: tt * 2 + 1],
                            y_acc.ap()[:, ty:ty + DM], ALU.mult, ALU.add))
                        v_c1 = inc("v", i)
                        wt("vector", "v", v_c1)
                        i = DV(lambda e, tt=tt: e.scalar_tensor_tensor(
                            o_tm.ap()[:], xg_tm.ap()[:, DM:2 * DM], w12_sb.ap()[:, tt * 2 + 1: tt * 2 + 2],
                            o_tm.ap()[:], ALU.mult, ALU.add))
                        v_comb2 = inc("v", i)
                        ln_stats(lambda: o_tm.ap()[:], wait_v_scalar=v_comb2,
                                 wait_v_vector=v_comb2, wait_t_scalar=t_lnt_guard[0])
                        i = SC(lambda e: e.activation(o_ln.ap()[:], o_tm.ap()[:],
                                                      AF.Identity, bias=smc(SM_NMRS), scale=smc(SM_RSTD)))
                        s_yln = inc("s", i)
                        slot = tt % 2
                        gsem, gval = slot_guard[slot]
                        if gval:
                            wt("gpsimd", gsem, gval)
                        slot_sem, slot_val = (f"st{qt % 2}", st_x1_tile[(s, qt)])
                        wt("gpsimd", slot_sem, slot_val)
                        i = GP(lambda e, qt=qt, slot=slot: e.dma_start(
                            out=x_tile.ap()[:, slot * DM:(slot + 1) * DM],
                            in_=x1_dram[base + qt * P: base + (qt + 1) * P, :]))
                        xl_x1 = inc(f"xl{slot}", i, 16)
                        wt("vector", "s", s_yln)
                        wt("vector", f"xl{slot}", xl_x1)
                        if so_last_slot[slot]:
                            wt("vector", f"so{slot}", so_last_slot[slot])
                        if st_x1_last_slot[slot]:
                            wt("vector", f"st{slot}", st_x1_last_slot[slot])
                        x1s = slot * DM
                        i = DV(lambda e, slot=slot, x1s=x1s: e.tensor_add(
                            x1_t.ap()[:, x1s:x1s + DM], x_tile.ap()[:, slot * DM:(slot + 1) * DM], o_ln.ap()[:]))
                        v_out = inc("v", i)
                        slot_guard[slot] = ("v", v_out)
                        wt("gpsimd", "v", v_out)
                        i = GP(lambda e, qt=qt, x1s=x1s: e.dma_start(
                            out=out_d[base + qt * P: base + (qt + 1) * P, :], in_=x1_t.ap()[:, x1s:x1s + DM]))
                        so_last_slot[slot] = inc(f"so{slot}", i, 16)

            for nm in ("st0", "st1", "so0", "so1"):
                wt("gpsimd", nm, cnt[nm])
            return

        block.sync(lambda e: emit("sync", e))
        block.tensor(lambda e: emit("tensor", e))
        block.vector(lambda e: emit("vector", e))
        block.scalar(lambda e: emit("scalar", e))
        block.gpsimd(lambda e: emit("gpsimd", e))

    return nc


def make_const_inputs():
    lt = (np.arange(P)[:, None] < np.arange(P)[None, :]).astype(np.float32)
    e = np.arange(8, dtype=np.float32)
    rtc = np.concatenate([
        np.broadcast_to(16.0 - e, (P, 8)),
        np.broadcast_to(3.0 * e, (P, 8)),
        np.broadcast_to(float(CAP) * e, (P, 8)),
    ], axis=1).astype(np.float32)
    rtc2 = np.stack([np.ones(P, np.float32), np.arange(P, dtype=np.float32)], axis=1)
    return dict(
        ident_f=np.eye(P, dtype=np.float32),
        ident_b=np.eye(P, dtype=np.float32).astype(ml_dtypes.bfloat16),
        masks_in=build_masks(),
        ones_in=np.ones((1, P), np.float32),
        lt_in=np.ascontiguousarray(lt),
        rtc_in=np.ascontiguousarray(rtc),
        rtc2_in=np.ascontiguousarray(rtc2),
    )


def make_weight_inputs(kw):
    g = lambda k: np.ascontiguousarray(np.asarray(kw[k], np.float32))
    w = {}
    for pn, src in (("q", "q"), ("k", "k"), ("v", "v"), ("o", "o")):
        w[f"{pn}_Wv"] = g(f"L0_attn_{src}_Wv")
        w[f"{pn}_Wr"] = g(f"L0_attn_{src}_Wr")
        w[f"{pn}_Wl"] = g(f"L0_attn_{src}_Wl")
        w[f"{pn}_b"] = g(f"L0_attn_{src}_b")
    w["gate_w"] = g("L0_ffn_gate_w")
    w["gate_k"] = g("L0_ffn_gate_k")
    w["up_k"] = g("L0_ffn_up_k")
    w["down_k"] = g("L0_ffn_down_k")
    w["sh_gate"] = g("L0_ffn_sh_gate")
    w["sh_up"] = g("L0_ffn_sh_up")
    w["sh_down"] = g("L0_ffn_sh_down")
    return w


def kernel(**inputs):
    NCORES = 8
    x = np.ascontiguousarray(np.asarray(inputs["L0_attn_x"], np.float32))
    B, T, D = x.shape
    per = B // NCORES

    weights = make_weight_inputs(inputs)
    consts = make_const_inputs()

    nc = build_program(n_seq=per, T=T)
    in_maps = []
    for c in range(NCORES):
        m = {"x_in": x[c * per:(c + 1) * per].reshape(per * T, D)}
        m.update(weights)
        m.update(consts)
        in_maps.append(m)
    res = run_bass_kernel_spmd(nc, in_maps, core_ids=list(range(NCORES)))
    out = np.concatenate([r["out"].reshape(per, T, D) for r in res.results], axis=0)
    return out


# version 13
# speedup vs baseline: 1.3054x; 1.3054x over previous
import numpy as np
import ml_dtypes
import concourse.bass as bass
import concourse.mybir as mybir
from concourse.bass_utils import run_bass_kernel_spmd

F32 = mybir.dt.float32
F32R = mybir.dt.float32r
BF16 = mybir.dt.bfloat16
AF = mybir.ActivationFunctionType
ALU = mybir.AluOpType
AX = mybir.AxisListType

P = 128
DM = 1024
NCHUNK = DM // P
H = 16
E = 8
F = 4096
FS = 2048
FT = 256
MSUB = FT // P
NEG = -60000.0
EPS = 1e-5
SBUF_BYTES = 229376

MOFF = [0, 1, 2, 4, 8]
CAP = 384
CAPT = CAP // P


def build_masks():
    allowed = set(range(5)) | {8, 16, 32, 64, 128, 256, 512, 1024}
    qi = np.arange(P)[:, None]
    ki = np.arange(P)[None, :]
    tiles = []
    for m in MOFF:
        d = 128 * m + qi - ki
        tiles.append(np.where(np.isin(d, list(allowed)), 0.0, NEG).astype(np.float32))
    return np.stack(tiles).astype(ml_dtypes.bfloat16)


def urdhva_block_cols(c):
    out = [(c, c)]
    if c >= 1:
        out.append((c - 1, 8 + (c - 1)))
    if c <= 6:
        out.append((c + 1, 15 + c))
    return out


def build_program(n_seq=2, T=2048):
    NT = T // P
    NG = T // 512
    PASS_TOK = min(1024, T)
    NPASS = T // PASS_TOK
    GPP = PASS_TOK // 512
    TPP = PASS_TOK // P
    NFT = F // FT
    NFTS = FS // FT

    nc = bass.Bass("TRN2", target_bir_lowering=False)

    x_in = nc.dram_tensor("x_in", [n_seq * T, DM], F32, kind="ExternalInput").ap()
    wq = {}
    for pname in ("q", "k", "v", "o"):
        wq[pname + "v"] = nc.dram_tensor(f"{pname}_Wv", [8, P, P], F32, kind="ExternalInput").ap()
        wq[pname + "r"] = nc.dram_tensor(f"{pname}_Wr", [7, P, P], F32, kind="ExternalInput").ap()
        wq[pname + "l"] = nc.dram_tensor(f"{pname}_Wl", [7, P, P], F32, kind="ExternalInput").ap()
        wq[pname + "b"] = nc.dram_tensor(f"{pname}_b", [DM], F32, kind="ExternalInput").ap()
    gate_w = nc.dram_tensor("gate_w", [E, DM], F32, kind="ExternalInput").ap()
    gate_k = nc.dram_tensor("gate_k", [E, DM, F], F32, kind="ExternalInput").ap()
    up_k = nc.dram_tensor("up_k", [E, DM, F], F32, kind="ExternalInput").ap()
    down_k = nc.dram_tensor("down_k", [E, F, DM], F32, kind="ExternalInput").ap()
    sh_gate = nc.dram_tensor("sh_gate", [DM, FS], F32, kind="ExternalInput").ap()
    sh_up = nc.dram_tensor("sh_up", [DM, FS], F32, kind="ExternalInput").ap()
    sh_down = nc.dram_tensor("sh_down", [FS, DM], F32, kind="ExternalInput").ap()
    ident_f = nc.dram_tensor("ident_f", [P, P], F32, kind="ExternalInput").ap()
    ident_b = nc.dram_tensor("ident_b", [P, P], BF16, kind="ExternalInput").ap()
    masks_in = nc.dram_tensor("masks_in", [5, P, P], BF16, kind="ExternalInput").ap()
    ones_in = nc.dram_tensor("ones_in", [1, P], F32, kind="ExternalInput").ap()
    lt_in = nc.dram_tensor("lt_in", [P, P], F32, kind="ExternalInput").ap()
    rtc_in = nc.dram_tensor("rtc_in", [P, 3 * E], F32, kind="ExternalInput").ap()
    rtc2_in = nc.dram_tensor("rtc2_in", [P, 2], F32, kind="ExternalInput").ap()

    x1_dram = nc.dram_tensor("x1_scratch", [n_seq * T, DM], F32, kind="Internal").ap()
    ln2_dram = nc.dram_tensor("ln2_scratch", [n_seq, NG, NCHUNK, P, 512], F32, kind="Internal").ap()
    ln2tm_dram = nc.dram_tensor("ln2tm_scratch", [n_seq * T, DM], F32, kind="Internal").ap()
    idx_dram = nc.dram_tensor("idx_scratch", [P * E * CAPT, 1], mybir.dt.int32, kind="Internal").ap()
    eo_dram = nc.dram_tensor("eo_scratch", [E * CAP, DM], F32, kind="Internal").ap()
    out_d = nc.dram_tensor("out", [n_seq * T, DM], F32, kind="ExternalOutput").ap()

    off = [18432]

    def sb(name, shape, dtype, at=None):
        nbytes = int(np.prod(shape[1:])) * mybir.dt.size(dtype)
        o = off[0] if at is None else at
        h = nc.alloc_sbuf_tensor_at(name, shape, dtype, offset=o)
        if at is None:
            off[0] = o + ((nbytes + 31) // 32) * 32
        return h

    idb = sb("idb", [P, P], BF16)
    idf = sb("idf", [P, P], F32)
    idr = sb("idr", [P, P], F32R)
    msk = sb("msk", [P, 5 * P], BF16)
    onesr = sb("onesr", [1, P], F32R)
    vb_row = sb("vb_row", [1, DM], F32R)
    ob_row = sb("ob_row", [1, DM], F32R)
    qb_col = sb("qb_col", [P, NCHUNK], F32)
    kb_col = sb("kb_col", [P, NCHUNK], F32)
    gwT = sb("gwT", [P, NCHUNK * E], F32R)
    comb = sb("comb", [P, NT * E], F32)
    sm = sb("smalls", [P, 24], F32)
    SM_SUM, SM_SSQ, SM_MEAN, SM_M2, SM_E2, SM_VAR, SM_RSTD, SM_NMRS = range(8)
    SM_NEGMAX, SM_ESUM, SM_LNS, SM_SINV, SM_M1, SM_MM2 = range(8, 14)
    SM_NEGMAX1, SM_ESUM1, SM_LNS1, SM_SINV1 = range(14, 18)
    SM_GES, SM_GLN, SM_GSI = range(18, 21)
    o_tm = sb("o_tm", [P, DM], F32)
    o_ln = sb("o_ln", [P, DM], F32)
    x_tile = sb("x_tile", [P, 2 * DM], F32)
    x1_t = sb("x1_t", [P, 2 * DM], F32)
    lnt = sb("lnt", [P, DM], F32)
    probs = sb("probs", [P, 2 * 5 * P], BF16)
    probs_km = sb("probs_km", [P, 2 * 5 * P], BF16)
    diag = sb("diag", [P, 2 * P], BF16)
    p8 = sb("p8", [P, E], F32)
    pw8 = sb("pw8", [P, E], F32)
    keep8 = sb("keep8", [P, E], F32)
    attn_fm = sb("attn_fm", [P, NCHUNK * P], F32R)
    LT_r = sb("LT_r", [P, P], F32R)
    onescol_r = sb("onescol_r", [P, 1], F32R)
    rtc = sb("rtc", [P, 3 * E], F32)
    iota_f = sb("iota_f", [P, 1], F32)
    base_row = sb("base_row", [1, E], F32R)
    ass_r = sb("ass_r", [P, E], F32R)
    pos_sb = sb("pos_sb", [P, E], F32)
    rt_s = sb("rt_s", [P, 16], F32)
    w12_sb = sb("w12_sb", [P, 2 * (1024 // P)], F32)
    sl_sb = sb("sl_sb", [P, 2 * (1024 // P)], mybir.dt.int32)
    ad_sb = sb("ad_sb", [P, 2], mybir.dt.int32)
    tok_sb = sb("tok_sb", [P, 2], mybir.dt.int32)
    idx_sb = sb("idx_sb", [P, E * CAPT], mybir.dt.int32)
    PERM_END = off[0]
    assert PERM_END <= 75 * 1024, PERM_END

    R1 = PERM_END
    gw_rows = sb("gw_rows", [P, DM], F32R, at=R1)
    qkvW = sb("qkvW", [P, 3 * 22 * P], F32R, at=R1)
    R1_END = R1 + 3 * 22 * P * 4
    ln1_q = sb("ln1_q", [P, NCHUNK * 256], F32R, at=R1_END)
    ln2_grp = sb("ln2_grp", [P, NCHUNK * 512], F32R, at=R1_END + NCHUNK * 256 * 4)
    R3 = R1_END + NCHUNK * 256 * 4 + NCHUNK * 512 * 4
    q_grp = sb("q_grp", [P, NCHUNK * 512], BF16, at=R3)
    k_fm = sb("k_fm", [P, NCHUNK * T], BF16, at=R3 + NCHUNK * 512 * 2)
    v_tm = sb("v_tm", [P, NT * DM], BF16, at=R3 + NCHUNK * 512 * 2 + NCHUNK * T * 2)
    R3V = R3 + NCHUNK * 512 * 2 + 2 * NCHUNK * T * 2
    oW = sb("oW", [P, 22 * P], F32R, at=R3V)
    assert R3V + 22 * P * 4 <= SBUF_BYTES, R3V + 22 * P * 4

    DD = R1
    ln2_pass = sb("ln2_pass", [P, GPP * NCHUNK * 512], F32R, at=DD)
    DY = DD + GPP * NCHUNK * 512 * 4
    y_acc = sb("y_acc", [P, TPP * DM], F32, at=DY)
    a_t = sb("a_t", [P, MSUB * 512], F32R, at=DY + TPP * DM * 4)
    sg_t = sb("sg_t", [P, 512], F32, at=DY + TPP * DM * 4 + MSUB * 512 * 4)
    DB = DY + TPP * DM * 4 + MSUB * 512 * 4 + 2048
    gbuf = [sb(f"gbuf{i}", [P, NCHUNK * FT], F32R, at=DB + i * NCHUNK * FT * 4) for i in range(2)]
    ubuf = [sb(f"ubuf{i}", [P, NCHUNK * FT], F32R, at=DB + (2 + i) * NCHUNK * FT * 4) for i in range(2)]
    dbuf = [sb(f"dbuf{i}", [P, MSUB * DM], F32R, at=DB + 4 * NCHUNK * FT * 4 + i * MSUB * DM * 4) for i in range(2)]
    DS = DB + 4 * NCHUNK * FT * 4 + 2 * MSUB * DM * 4
    xg_tm = sb("xg_tm", [P, CAPT * DM], F32, at=DS)
    xg_fm = sb("xg_fm", [P, NCHUNK * CAP], F32R, at=DS + CAPT * DM * 4)
    eo_acc = sb("eo_acc", [P, CAPT * DM], F32, at=DS + CAPT * DM * 4 + NCHUNK * CAP * 4)
    D_END = DS + 2 * CAPT * DM * 4 + NCHUNK * CAP * 4
    assert D_END <= SBUF_BYTES, D_END

    PS = nc.alloc_psum_tensor("PS", [P, 4096], F32)

    def bank(b, width=512, dt=F32):
        ap = PS.ap()[:, b * 512: b * 512 + width]
        return ap.bitcast(dt) if dt != F32 else ap

    from contextlib import ExitStack
    with ExitStack() as ctx:
        sems = {}
        for name in ("wq", "w0", "w1", "xl0", "xl1", "st0", "st1", "so0", "so1",
                     "l2", "l2r", "lt", "ix", "xg", "eo", "eg", "t", "v", "s", "g"):
            sems[name] = ctx.enter_context(nc.semaphore(name))
        block = ctx.enter_context(nc.Block())

        def emit(role, eng):
            cnt = {k: 0 for k in sems}

            def inc(sem, inst, n=1):
                cnt[sem] += n
                if inst is not None:
                    inst.then_inc(sems[sem], n)
                return cnt[sem]

            def wt(r, sem, val):
                if role == r and val > 0:
                    eng.wait_ge(sems[sem], val)

            def MM(f): return f(eng) if role == "tensor" else None
            def DV(f): return f(eng) if role == "vector" else None
            def SC(f): return f(eng) if role == "scalar" else None
            def GP(f): return f(eng) if role == "gpsimd" else None
            def SY(f): return f(eng) if role == "sync" else None

            smc = lambda i_: sm.ap()[:, i_:i_ + 1]

            def ln_stats(src, width=DM, wait_v_scalar=0, wait_xl=None,
                         wait_t_scalar=0, wait_v_vector=0):
                if wait_xl:
                    wt("vector", wait_xl[0], wait_xl[1])
                if wait_v_vector:
                    wt("vector", "v", wait_v_vector)
                i = DV(lambda e: e.reduce_sum(smc(SM_SUM), src(), axis=AX.X))
                tv = inc("v", i)
                if wait_xl:
                    wt("scalar", wait_xl[0], wait_xl[1])
                if wait_v_scalar:
                    wt("scalar", "v", wait_v_scalar)
                if wait_t_scalar:
                    wt("scalar", "t", wait_t_scalar)
                if lt_last[0]:
                    wt("scalar", "lt", lt_last[0])
                i = SC(lambda e: e.activation(lnt.ap()[:, 0:width], src(), AF.Square,
                                              accum_out=smc(SM_SSQ)))
                s_sq = inc("s", i)
                wt("scalar", "v", tv)
                i = SC(lambda e: e.activation(smc(SM_MEAN), smc(SM_SUM), AF.Identity, scale=1.0 / width))
                s_mean = inc("s", i)
                wt("scalar", "s", s_mean)
                i = SC(lambda e: e.activation(smc(SM_M2), smc(SM_MEAN), AF.Square))
                i = SC(lambda e: e.activation(smc(SM_E2), smc(SM_SSQ), AF.Identity, scale=1.0 / width))
                ts_ = inc("s", i)
                wt("vector", "s", ts_)
                i = DV(lambda e: e.tensor_sub(smc(SM_VAR), smc(SM_E2), smc(SM_M2)))
                v_sub = inc("v", i)
                wt("vector", "v", v_sub)
                i = DV(lambda e: e.tensor_scalar_add(smc(SM_VAR), smc(SM_VAR), float(EPS)))
                tv2 = inc("v", i)
                wt("scalar", "v", tv2)
                i = SC(lambda e: e.activation(smc(SM_LNS), smc(SM_VAR), AF.Ln))
                s_lnv = inc("s", i)
                wt("scalar", "s", s_lnv)
                i = SC(lambda e: e.activation(smc(SM_RSTD), smc(SM_LNS), AF.Exp, scale=-0.5))
                ts2 = inc("s", i)
                wt("vector", "s", ts2)
                i = DV(lambda e: e.scalar_tensor_tensor(smc(SM_NMRS), smc(SM_MEAN), -1.0,
                                                        smc(SM_RSTD), ALU.mult, ALU.mult))
                tv3 = inc("v", i)
                wt("scalar", "v", tv3)
                return ts2

            i = SY(lambda e: e.dma_start(out=idf.ap()[:], in_=ident_f)); inc("wq", i, 16)
            i = SY(lambda e: e.dma_start(out=idr.ap()[:], in_=ident_f.bitcast(F32R))); inc("wq", i, 16)
            i = SY(lambda e: e.dma_start(out=idb.ap()[:], in_=ident_b)); inc("wq", i, 16)
            i = SY(lambda e: e.dma_start(
                out=msk.ap()[:].rearrange("p (m q) -> p m q", m=5),
                in_=masks_in.rearrange("m p q -> p m q"))); inc("wq", i, 16)
            i = SY(lambda e: e.dma_start(out=onesr.ap()[:], in_=ones_in.bitcast(F32R))); inc("wq", i, 16)
            i = SY(lambda e: e.dma_start(out=vb_row.ap()[:], in_=wq["vb"].unsqueeze(0).bitcast(F32R))); inc("wq", i, 16)
            i = SY(lambda e: e.dma_start(out=ob_row.ap()[:], in_=wq["ob"].unsqueeze(0).bitcast(F32R))); inc("wq", i, 16)
            for c in range(NCHUNK):
                i = SY(lambda e, c=c: e.dma_start(out=qb_col.ap()[:, c:c + 1],
                                                  in_=wq["qb"][c * P:(c + 1) * P].unsqueeze(1)))
                inc("wq", i, 16)
                i = SY(lambda e, c=c: e.dma_start(out=kb_col.ap()[:, c:c + 1],
                                                  in_=wq["kb"][c * P:(c + 1) * P].unsqueeze(1)))
                inc("wq", i, 16)
            i = SY(lambda e: e.dma_start(out=LT_r.ap()[:], in_=lt_in.bitcast(F32R))); inc("wq", i, 16)
            i = SY(lambda e: e.dma_start(out=onescol_r.ap()[:], in_=rtc2_in[:, 0:1].bitcast(F32R),
                                         allow_slow_non_contiguous=True)); inc("wq", i, 16)
            i = SY(lambda e: e.dma_start(out=iota_f.ap()[:], in_=rtc2_in[:, 1:2],
                                         allow_slow_non_contiguous=True)); inc("wq", i, 16)
            i = SY(lambda e: e.dma_start(out=rtc.ap()[:], in_=rtc_in)); inc("wq", i, 16)
            i = SY(lambda e: e.dma_start(out=gw_rows.ap()[0:E, :], in_=gate_w.bitcast(F32R)))
            w_consts = inc("wq", i, 16)
            i = GP(lambda e: e.memset(idx_sb.ap()[:], 0))
            g_i0 = inc("g", i)
            wt("gpsimd", "g", g_i0)
            i = GP(lambda e: e.dma_start(out=idx_dram.rearrange("(p c) one -> p (c one)", p=P),
                                         in_=idx_sb.ap()[:]))
            inc("ix", i, 16)
            wt("tensor", "wq", w_consts)
            for c in range(NCHUNK):
                i = MM(lambda e, c=c: e.transpose(bank(6, 128, F32R), gw_rows.ap()[:, c * P:(c + 1) * P],
                                                  idr.ap()))
                t_tr = inc("t", i)
                wt("vector", "t", t_tr)
                i = DV(lambda e, c=c: e.tensor_copy(gwT.ap()[:, c * E:(c + 1) * E],
                                                    bank(6, 128, F32R)[:, 0:E]))
                v_e = inc("v", i)
                wt("tensor", "v", v_e)

            st_x1_tile = {}
            st_x1_last_slot = {0: 0, 1: 0}
            so_last_slot = {0: 0, 1: 0}
            bank_free = {}
            lt_last = [0]

            for s in range(n_seq):
                base = s * T
                wt("sync", "t", cnt["t"])
                wt("sync", "v", cnt["v"])
                for pi, pn in enumerate(("q", "k", "v")):
                    for part, lo, hi in (("v", 0, 8), ("r", 8, 15), ("l", 15, 22)):
                        i = SY(lambda e, pn=pn, pi=pi, part=part, lo=lo, hi=hi: e.dma_start(
                            out=qkvW.ap()[:, (pi * 22 + lo) * P:(pi * 22 + hi) * P]
                                .rearrange("p (b q) -> p b q", b=hi - lo),
                            in_=wq[pn + part].rearrange("b p q -> p b q").bitcast(F32R)))
                        inc("wq", i, 16)
                for part, lo, hi in (("v", 0, 8), ("r", 8, 15), ("l", 15, 22)):
                    i = SY(lambda e, part=part, lo=lo, hi=hi: e.dma_start(
                        out=oW.ap()[:, lo * P:hi * P].rearrange("p (b q) -> p b q", b=hi - lo),
                        in_=wq["o" + part].rearrange("b p q -> p b q").bitcast(F32R)))
                    inc("wq", i, 16)
                w_qkv = cnt["wq"]

                s_exp_hist = {}
                t_lnt_guard = [0]
                l2_war = [0]
                slot_guard = {0: ("s", 0), 1: ("s", 0)}
                for g in range(NG):
                    for sub in range(2):
                        for tl in range(2):
                            qt = g * 4 + sub * 2 + tl
                            slot = qt % 2
                            gsem, gval = slot_guard[slot]
                            if gval:
                                wt("gpsimd", gsem, gval)
                            i = GP(lambda e, qt=qt, slot=slot: e.dma_start(
                                out=x_tile.ap()[:, slot * DM:(slot + 1) * DM],
                                in_=x_in[base + qt * P: base + (qt + 1) * P, :]))
                            xl_v = inc(f"xl{slot}", i, 16)
                            src = (lambda slot=slot: x_tile.ap()[:, slot * DM:(slot + 1) * DM])
                            ln_stats(src, wait_xl=(f"xl{slot}", xl_v), wait_t_scalar=t_lnt_guard[0])
                            i = SC(lambda e, slot=slot: e.activation(
                                lnt.ap()[:], x_tile.ap()[:, slot * DM:(slot + 1) * DM],
                                AF.Identity, bias=smc(SM_NMRS), scale=smc(SM_RSTD)))
                            s_ln = inc("s", i)
                            slot_guard[slot] = ("s", s_ln)
                            wt("tensor", "s", s_ln)
                            for c in range(NCHUNK):
                                b = 6 + (c % 2)
                                wt("tensor", "v", bank_free.get(b, 0))
                                i = MM(lambda e, c=c, b=b: e.transpose(
                                    bank(b, 128), lnt.ap()[:, c * P:(c + 1) * P], idf.ap()))
                                t_tr = inc("t", i)
                                wt("vector", "t", t_tr)
                                i = DV(lambda e, c=c, tl=tl, b=b: e.tensor_copy(
                                    ln1_q.ap()[:, c * 256 + tl * P: c * 256 + (tl + 1) * P].bitcast(F32R),
                                    bank(b, 128).bitcast(F32R)))
                                v_ev = inc("v", i)
                                bank_free[b] = v_ev
                            t_lnt_guard[0] = cnt["t"]
                        v_ln1 = cnt["v"]
                        wt("tensor", "wq", w_qkv)
                        wt("tensor", "v", v_ln1)
                        wt("tensor", "s", cnt["s"])
                        for pi, pn in ((0, "q"), (1, "k")):
                            bias_c = qb_col if pn == "q" else kb_col
                            for c in range(NCHUNK):
                                bnk = (0 if pn == "q" else 2) + (c % 2)
                                rlist = urdhva_block_cols(c)
                                wt("tensor", "v", bank_free.get(bnk, 0))
                                i = None
                                for j, (r, wcol) in enumerate(rlist):
                                    i = MM(lambda e, pi=pi, r=r, wcol=wcol, j=j, bnk=bnk, rl=len(rlist):
                                           e.matmul(bank(bnk, 256),
                                                    qkvW.ap()[:, (pi * 22 + wcol) * P:(pi * 22 + wcol + 1) * P],
                                                    ln1_q.ap()[:, r * 256:(r + 1) * 256],
                                                    start=(j == 0), stop=(j == rl - 1)))
                                t_mm = inc("t", i)
                                wt("vector", "t", t_mm)
                                if pn == "q":
                                    dst = lambda c=c, sub=sub: q_grp.ap()[:, c * 512 + sub * 256: c * 512 + (sub + 1) * 256]
                                    scale2 = 0.125
                                else:
                                    dst = lambda c=c, g=g, sub=sub: k_fm.ap()[:, c * T + g * 512 + sub * 256: c * T + g * 512 + (sub + 1) * 256]
                                    scale2 = 1.0
                                i = DV(lambda e, dst=dst, bias_c=bias_c, scale2=scale2, bnk=bnk, c=c:
                                       e.tensor_scalar(dst(), bank(bnk, 256), bias_c.ap()[:, c:c + 1], scale2,
                                                       ALU.add, ALU.mult))
                                v_ev = inc("v", i)
                                bank_free[bnk] = v_ev
                        for tl in range(2):
                            tt = sub * 2 + tl
                            bnk = 4 + (tl % 2)
                            wt("tensor", "v", bank_free.get(bnk, 0))
                            i = None
                            for c in range(NCHUNK):
                                seg = bank(bnk)[:, (c % 4) * P:(c % 4 + 1) * P]
                                rlist = urdhva_block_cols(c)
                                fib = (c % 4 == 0)
                                for j, (r, wcol) in enumerate(rlist):
                                    i = MM(lambda e, r=r, wcol=wcol, j=j, tl=tl, seg=seg, fib=fib:
                                           e.matmul(seg,
                                                    ln1_q.ap()[:, r * 256 + tl * P: r * 256 + (tl + 1) * P],
                                                    qkvW.ap()[:, (2 * 22 + wcol) * P:(2 * 22 + wcol + 1) * P],
                                                    start=(fib and j == 0), stop=False))
                                i = MM(lambda e, c=c, seg=seg: e.matmul(
                                    seg, onesr.ap()[0:1, 0:P], vb_row.ap()[0:1, c * P:(c + 1) * P],
                                    start=False, stop=(c % 4 == 3)))
                                if c % 4 == 3:
                                    t_mm = inc("t", i)
                                    wt("vector", "t", t_mm)
                                    half = c // 4
                                    i2 = DV(lambda e, g=g, tt=tt, half=half, bnk=bnk: e.tensor_copy(
                                        v_tm.ap()[:, (g * 4 + tt) * DM + half * 512:(g * 4 + tt) * DM + (half + 1) * 512],
                                        bank(bnk)))
                                    v_ev = inc("v", i2)
                                    bank_free[bnk] = v_ev
                                    if c == 3:
                                        wt("tensor", "v", v_ev)
                    for qt in range(g * 4, g * 4 + 4):
                        tlq = qt % 4
                        kts = [qt - m for m in MOFF if qt - m >= 0]
                        nb = len(kts)
                        slot = qt % 2
                        gsem, gval = slot_guard[slot]
                        if gval:
                            wt("gpsimd", gsem, gval)
                        i = GP(lambda e, qt=qt, slot=slot: e.dma_start(
                            out=x_tile.ap()[:, slot * DM:(slot + 1) * DM],
                            in_=x_in[base + qt * P: base + (qt + 1) * P, :]))
                        xl_x = inc(f"xl{slot}", i, 16)

                        v_av_evac = 0
                        t_sc_h = {}
                        v_diag_h = {}
                        t_post_softB = {}

                        def emit_qk(h):
                            ch = h // 2
                            po = 64 * (h % 2)
                            sb0 = 0 if h % 2 == 0 else 2
                            prev = s_exp_hist.get((qt, h - 2))
                            if prev is None and h < 2:
                                prev = s_exp_hist.get((qt - 1, h + 14))
                            if h == 1:
                                pg_ = s_exp_hist.get((qt - 1, "gate"))
                                if pg_:
                                    prev = max(prev or 0, pg_)
                            if prev:
                                wt("tensor", "s", prev)
                            i = None
                            for j, kt in enumerate(kts):
                                sbank = sb0 if j < 4 else sb0 + 1
                                seg = bank(sbank)[:, (j % 4) * P:(j % 4 + 1) * P]
                                i = MM(lambda e, seg=seg, ch=ch, po=po, tlq=tlq, kt=kt, j=j:
                                       e.matmul(seg,
                                                q_grp.ap()[po:po + 64, ch * 512 + tlq * P: ch * 512 + (tlq + 1) * P],
                                                k_fm.ap()[po:po + 64, ch * T + kt * P: ch * T + (kt + 1) * P],
                                                start=(j % 4 == 0), stop=False))
                                mi = MOFF.index(qt - kt)
                                i = MM(lambda e, seg=seg, mi=mi, j=j, nb=nb:
                                       e.matmul(seg, idb.ap(), msk.ap()[:, mi * P:(mi + 1) * P],
                                                start=False, stop=(j == nb - 1 or j == 3)))
                            t_sc_h[h] = inc("t", i)

                        def emit_softA(h):
                            sb0 = 0 if h % 2 == 0 else 2
                            pr = h % 2
                            ESX = SM_ESUM if pr == 0 else SM_ESUM1
                            LNX = SM_LNS if pr == 0 else SM_LNS1
                            SIX = SM_SINV if pr == 0 else SM_SINV1
                            pb = probs.ap()[:, pr * 5 * P:(pr * 5 + 5) * P]
                            db = diag.ap()[:, pr * P:(pr + 1) * P]
                            wt("scalar", "t", t_sc_h[h])
                            if h - 2 in t_post_softB:
                                wt("scalar", "t", t_post_softB[h - 2])
                            i = SC(lambda e, sb0=sb0, nb=nb, ESX=ESX, pb=pb: e.activation(
                                pb[:, 0:nb * P], PS.ap()[:, sb0 * 512: sb0 * 512 + nb * P],
                                AF.Exp, accum_out=smc(ESX)))
                            s_exp = inc("s", i)
                            s_exp_hist[(qt, h)] = s_exp
                            wt("scalar", "s", s_exp)
                            i = SC(lambda e, ESX=ESX, LNX=LNX: e.activation(smc(LNX), smc(ESX), AF.Ln))
                            s_l = inc("s", i)
                            wt("scalar", "s", s_l)
                            i = SC(lambda e, LNX=LNX, SIX=SIX: e.activation(smc(SIX), smc(LNX), AF.Exp, scale=-1.0))
                            s_sinv = inc("s", i)
                            wt("vector", "s", s_sinv)
                            i = DV(lambda e, SIX=SIX, db=db: e.tensor_scalar(db, idb.ap(), smc(SIX), None, ALU.mult))
                            v_diag_h[h] = inc("v", i)

                        def emit_softB(h):
                            nonlocal v_av_evac
                            ch = h // 2
                            po = 64 * (h % 2)
                            pr = h % 2
                            pb = probs.ap()[:, pr * 5 * P:(pr * 5 + 5) * P]
                            kb = probs_km.ap()[:, pr * 5 * P:(pr * 5 + 5) * P]
                            db = diag.ap()[:, pr * P:(pr + 1) * P]
                            avb = 6 + (h % 2)
                            wt("tensor", "v", v_diag_h[h])
                            wt("tensor", "v", bank_free.get(avb, 0))
                            wt("tensor", "v", bank_free.get(4, 0))
                            if nb > 4:
                                wt("tensor", "v", bank_free.get(5, 0))
                            i = None
                            for j in range(nb):
                                trb = 4 if j < 4 else 5
                                seg = bank(trb)[:, (j % 4) * P:(j % 4 + 1) * P]
                                i = MM(lambda e, j=j, seg=seg, pb=pb, db=db: e.matmul(
                                    seg, pb[:, j * P:(j + 1) * P], db,
                                    start=True, stop=True))
                            t_tr = inc("t", i)
                            wt("vector", "t", t_tr)
                            nb4 = min(nb, 4)
                            i2 = DV(lambda e, nb4=nb4, kb=kb: e.tensor_copy(
                                kb[:, 0:nb4 * P], bank(4, nb4 * P)))
                            v_km = inc("v", i2)
                            bank_free[4] = v_km
                            if nb > 4:
                                i2 = DV(lambda e, kb=kb: e.tensor_copy(
                                    kb[:, 4 * P:5 * P], bank(5, 128)))
                                v_km = inc("v", i2)
                                bank_free[5] = v_km
                            wt("tensor", "v", v_km)
                            i = None
                            for j in range(nb):
                                i = MM(lambda e, j=j, kt=kts[j], h=h, avb=avb, nb=nb, kb=kb: e.matmul(
                                    bank(avb, 128)[0:64, :],
                                    v_tm.ap()[:, kt * DM + h * 64: kt * DM + (h + 1) * 64],
                                    kb[:, j * P:(j + 1) * P],
                                    start=(j == 0), stop=(j == nb - 1)))
                            t_av = inc("t", i)
                            t_post_softB[h] = t_av
                            wt("vector", "t", t_av)
                            i = DV(lambda e, ch=ch, po=po, avb=avb: e.tensor_copy(
                                attn_fm.ap()[po:po + 64, ch * P:(ch + 1) * P].bitcast(F32R),
                                bank(avb, 128)[0:64, :].bitcast(F32R)))
                            v_av_evac = inc("v", i)
                            bank_free[avb] = v_av_evac

                        for step in range(H + 2):
                            if step < H:
                                emit_qk(step)
                            if 1 <= step <= H:
                                emit_softA(step - 1)
                            if step >= 2:
                                emit_softB(step - 2)

                        wt("tensor", "wq", w_qkv)
                        wt("tensor", "v", v_av_evac)
                        ov = PS.ap()[:, 3 * 1024: 4 * 1024]
                        i = None
                        for c in range(NCHUNK):
                            rlist = urdhva_block_cols(c)
                            for j, (r, wcol) in enumerate(rlist):
                                i = MM(lambda e, c=c, r=r, wcol=wcol, j=j, ov=ov:
                                       e.matmul(ov[:, c * P:(c + 1) * P],
                                                attn_fm.ap()[:, r * P:(r + 1) * P],
                                                oW.ap()[:, wcol * P:(wcol + 1) * P],
                                                start=(c % 4 == 0 and j == 0), stop=False))
                            i = MM(lambda e, c=c, ov=ov: e.matmul(
                                ov[:, c * P:(c + 1) * P], onesr.ap()[0:1, 0:P],
                                ob_row.ap()[0:1, c * P:(c + 1) * P],
                                start=False, stop=(c % 4 == 3)))
                        t_oproj = inc("t", i)
                        wt("vector", "t", t_oproj)
                        i = DV(lambda e, ov=ov: e.tensor_copy(o_tm.ap()[:], ov))
                        v_otm = inc("v", i)
                        bank_free[6] = v_otm
                        bank_free[7] = v_otm
                        ln_stats(lambda: o_tm.ap()[:], wait_v_scalar=v_otm,
                                 wait_v_vector=v_otm, wait_t_scalar=t_lnt_guard[0])
                        i = SC(lambda e: e.activation(o_ln.ap()[:], o_tm.ap()[:], AF.Identity,
                                                      bias=smc(SM_NMRS), scale=smc(SM_RSTD)))
                        s_oln = inc("s", i)
                        wt("vector", "s", s_oln)
                        wt("vector", f"xl{slot}", xl_x)
                        if st_x1_last_slot[slot]:
                            wt("vector", f"st{slot}", st_x1_last_slot[slot])
                        if so_last_slot[slot]:
                            wt("vector", f"so{slot}", so_last_slot[slot])
                        x1s = (qt % 2) * DM
                        i = DV(lambda e, slot=slot, x1s=x1s: e.tensor_add(
                            x1_t.ap()[:, x1s:x1s + DM], x_tile.ap()[:, slot * DM:(slot + 1) * DM], o_ln.ap()[:]))
                        v_x1 = inc("v", i)
                        slot_guard[slot] = ("v", v_x1)
                        wt("gpsimd", "v", v_x1)
                        i = GP(lambda e, qt=qt, x1s=x1s: e.dma_start(
                            out=x1_dram[base + qt * P: base + (qt + 1) * P, :], in_=x1_t.ap()[:, x1s:x1s + DM]))
                        st_x1_tile[(s, qt)] = inc(f"st{slot}", i, 16)
                        st_x1_last_slot[slot] = st_x1_tile[(s, qt)]
                        ln_stats(lambda x1s=x1s: x1_t.ap()[:, x1s:x1s + DM], wait_v_scalar=v_x1,
                                 wait_v_vector=v_x1)
                        i = SC(lambda e, x1s=x1s: e.activation(lnt.ap()[:], x1_t.ap()[:, x1s:x1s + DM],
                                                               AF.Identity, bias=smc(SM_NMRS), scale=smc(SM_RSTD)))
                        s_ln2 = inc("s", i)
                        wt("gpsimd", "s", s_ln2)
                        i = GP(lambda e, qt=qt: e.dma_start(
                            out=ln2tm_dram[base + qt * P: base + (qt + 1) * P, :], in_=lnt.ap()[:]))
                        lt_last[0] = inc("lt", i, 16)
                        wt("tensor", "s", s_ln2)
                        for c in range(NCHUNK):
                            b = 4 + (c % 2)
                            wt("tensor", "v", bank_free.get(b, 0))
                            i = MM(lambda e, c=c, b=b: e.transpose(
                                bank(b, 128), lnt.ap()[:, c * P:(c + 1) * P], idf.ap()))
                            t_tr = inc("t", i)
                            wt("vector", "t", t_tr)
                            if tlq == 0 and c == 0 and l2_war[0]:
                                wt("vector", "l2", l2_war[0])
                            i = DV(lambda e, c=c, tlq=tlq, b=b: e.tensor_copy(
                                ln2_grp.ap()[:, c * 512 + tlq * P: c * 512 + (tlq + 1) * P],
                                bank(b, 128).bitcast(F32R)))
                            v_ev = inc("v", i)
                            bank_free[b] = v_ev
                        t_lnt_guard[0] = cnt["t"]
                        gv = bank(2)[:, 0:E]
                        wt("tensor", "v", cnt["v"])
                        wt("tensor", "s", s_exp_hist[(qt, 15)])
                        i = None
                        for c in range(NCHUNK):
                            i = MM(lambda e, c=c, tlq=tlq, gv=gv: e.matmul(
                                gv, ln2_grp.ap()[:, c * 512 + tlq * P: c * 512 + (tlq + 1) * P],
                                gwT.ap()[:, c * E:(c + 1) * E],
                                start=(c == 0), stop=(c == NCHUNK - 1)))
                        t_gate = inc("t", i)
                        wt("vector", "t", t_gate)
                        i = DV(lambda e, gv=gv: e.reduce_max(smc(SM_NEGMAX), gv, axis=AX.X, negate=True))
                        v_gm = inc("v", i)
                        wt("scalar", "v", v_gm)
                        i = SC(lambda e, gv=gv: e.activation(p8.ap()[:], gv, AF.Exp,
                                                             bias=smc(SM_NEGMAX), accum_out=smc(SM_GES)))
                        s_e8 = inc("s", i)
                        s_exp_hist[(qt, "gate")] = s_e8
                        wt("scalar", "s", s_e8)
                        i = SC(lambda e: e.activation(smc(SM_GLN), smc(SM_GES), AF.Ln))
                        s_l8 = inc("s", i)
                        wt("scalar", "s", s_l8)
                        i = SC(lambda e: e.activation(smc(SM_GSI), smc(SM_GLN), AF.Exp, scale=-1.0))
                        s_g = inc("s", i)
                        wt("vector", "s", s_g)
                        i = DV(lambda e: e.tensor_scalar(p8.ap()[:], p8.ap()[:], smc(SM_GSI), None, ALU.mult))
                        v_p8 = inc("v", i)
                        wt("vector", "v", v_p8)
                        i = DV(lambda e: e.reduce_max(smc(SM_M1), p8.ap()[:], axis=AX.X))
                        v_m1 = inc("v", i)
                        wt("vector", "v", v_m1)
                        i = DV(lambda e: e.tensor_scalar(keep8.ap()[:], p8.ap()[:], smc(SM_M1), None, ALU.is_ge))
                        v_k1 = inc("v", i)
                        wt("vector", "v", v_k1)
                        i = DV(lambda e: e.scalar_tensor_tensor(pw8.ap()[:], keep8.ap()[:], -2.0,
                                                                p8.ap()[:], ALU.mult, ALU.add))
                        v_pw = inc("v", i)
                        wt("vector", "v", v_pw)
                        i = DV(lambda e: e.reduce_max(smc(SM_MM2), pw8.ap()[:], axis=AX.X))
                        v_m2 = inc("v", i)
                        wt("vector", "v", v_m2)
                        i = DV(lambda e: e.tensor_scalar(keep8.ap()[:], p8.ap()[:], smc(SM_MM2), None, ALU.is_ge))
                        v_k2 = inc("v", i)
                        wt("vector", "v", v_k2)
                        i = DV(lambda e, qt=qt: e.tensor_mul(comb.ap()[:, qt * E:(qt + 1) * E],
                                                             p8.ap()[:], keep8.ap()[:]))
                        g_comb = inc("v", i)

                    wt("gpsimd", "v", cnt["v"])
                    i = GP(lambda e, s=s, g=g: e.dma_start(
                        out=ln2_dram[s, g].rearrange("c p t -> p c t").bitcast(F32R),
                        in_=ln2_grp.ap()[:].rearrange("p (c t) -> p c t", c=NCHUNK)))
                    l2_war[0] = inc("l2", i, 16)

                v_comb_done = cnt["v"]
                g_comb_done = cnt["g"]

                for p_i in range(NPASS):
                    tok0 = p_i * PASS_TOK
                    t_pass0 = cnt["t"]
                    wt("tensor", "v", cnt["v"])
                    wt("tensor", "s", cnt["s"])
                    wt("tensor", "g", cnt["g"])
                    wt("sync", "t", cnt["t"])
                    wt("sync", "v", cnt["v"])
                    wt("sync", "l2", cnt["l2"])
                    i = SY(lambda e, s=s, p_i=p_i: e.dma_start(
                        out=ln2_pass.ap()[:].rearrange("p (g c t) -> p g c t", g=GPP, c=NCHUNK),
                        in_=ln2_dram[s, p_i * GPP:(p_i + 1) * GPP].rearrange("g c p t -> p g c t").bitcast(F32R)))
                    l2r_v = inc("l2r", i, 16)
                    wt("tensor", "l2r", l2r_v)

                    wt("vector", "v", cnt["v"])
                    i = DV(lambda e: e.tensor_scalar(base_row.ap()[:], onesr.ap()[0:1, 0:E].bitcast(F32),
                                                     0.0, None, ALU.mult))
                    v_b0 = inc("v", i)
                    wt("vector", "v", v_b0)
                    wt("gpsimd", "xg", cnt["xg"])
                    wt("vector", "eg", cnt["eg"])
                    ix_prev = [cnt["ix"]]
                    def routing_tile(tt):
                        qtl = tok0 // P + tt
                        cslice = comb.ap()[:, qtl * E:(qtl + 1) * E]
                        i = DV(lambda e, cslice=cslice: e.tensor_scalar(
                            ass_r.ap()[:], cslice, 0.0, None, ALU.is_gt))
                        v_ass = inc("v", i)
                        wt("tensor", "v", v_ass)
                        wt("tensor", "v", bank_free.get(0, 0))
                        i = MM(lambda e: e.matmul(bank(0)[:, 0:E], LT_r.ap(), ass_r.ap(),
                                                  start=True, stop=False))
                        i = MM(lambda e: e.matmul(bank(0)[:, 0:E], onesr.ap()[0:1, 0:P],
                                                  base_row.ap(), start=False, stop=True))
                        t_pos = inc("t", i)
                        i = MM(lambda e: e.matmul(bank(1)[0:1, 0:E], onescol_r.ap(), ass_r.ap(),
                                                  start=True, stop=True))
                        t_cnt = inc("t", i)
                        wt("vector", "t", t_cnt)
                        i = DV(lambda e: e.tensor_copy(pos_sb.ap()[:], bank(0)[:, 0:E]))
                        i = DV(lambda e: e.tensor_tensor(base_row.ap()[:],
                                                         base_row.ap()[:].bitcast(F32),
                                                         bank(1)[0:1, 0:E], ALU.add))
                        v_pos = inc("v", i)
                        bank_free[0] = v_pos
                        bank_free[1] = v_pos
                        wt("vector", "v", v_pos)
                        RS = rt_s.ap()
                        def dvstep(f):
                            ii = DV(f)
                            vv = inc("v", ii)
                            wt("vector", "v", vv)
                            return vv
                        dvstep(lambda e: e.tensor_scalar(RS[:, 0:E], pos_sb.ap()[:], float(CAP - 1), None, ALU.min))
                        dvstep(lambda e: e.tensor_scalar(RS[:, 8:8 + E], RS[:, 0:E], 128.0, None, ALU.is_ge))
                        dvstep(lambda e: e.tensor_scalar(p8.ap()[:], RS[:, 0:E], 256.0, None, ALU.is_ge))
                        dvstep(lambda e: e.tensor_add(RS[:, 8:8 + E], RS[:, 8:8 + E], p8.ap()[:]))
                        dvstep(lambda e: e.scalar_tensor_tensor(RS[:, 0:E], RS[:, 8:8 + E], -128.0,
                                                                RS[:, 0:E], ALU.mult, ALU.add))
                        dvstep(lambda e: e.scalar_tensor_tensor(RS[:, 8:8 + E], RS[:, 0:E],
                                                                float(E * CAPT), RS[:, 8:8 + E],
                                                                ALU.mult, ALU.add))
                        dvstep(lambda e: e.tensor_add(RS[:, 8:8 + E], RS[:, 8:8 + E], rtc.ap()[:, E:2 * E]))
                        dvstep(lambda e: e.tensor_scalar(RS[:, 0:E], pos_sb.ap()[:], float(CAP - 1), None, ALU.min))
                        dvstep(lambda e: e.tensor_add(RS[:, 0:E], RS[:, 0:E], rtc.ap()[:, 2 * E:3 * E]))
                        i = DV(lambda e: e.tensor_mul(pos_sb.ap()[:], ass_r.ap()[:].bitcast(F32), rtc.ap()[:, 0:E]))
                        v_sc = inc("v", i)
                        wt("vector", "v", v_sc)
                        i = DV(lambda e: e.reduce_max(smc(SM_M1), pos_sb.ap()[:], axis=AX.X))
                        v_m1 = inc("v", i)
                        wt("vector", "v", v_m1)
                        i = DV(lambda e: e.tensor_scalar(keep8.ap()[:], pos_sb.ap()[:], smc(SM_M1), None, ALU.is_ge))
                        v_s1 = inc("v", i)
                        wt("vector", "v", v_s1)
                        i = DV(lambda e: e.tensor_sub(pw8.ap()[:], ass_r.ap()[:].bitcast(F32), keep8.ap()[:]))
                        v_s2 = inc("v", i)
                        wt("vector", "v", v_s2)
                        wt("vector", "ix", ix_prev[0])
                        for k, selap in ((0, keep8), (1, pw8)):
                            dvstep(lambda e, selap=selap, cslice=cslice: e.tensor_mul(p8.ap()[:], selap.ap()[:], cslice))
                            dvstep(lambda e, tt=tt, k=k: e.reduce_sum(w12_sb.ap()[:, tt * 2 + k: tt * 2 + k + 1],
                                                                      p8.ap()[:], axis=AX.X))
                            dvstep(lambda e, selap=selap: e.tensor_mul(p8.ap()[:], selap.ap()[:], rt_s.ap()[:, 0:8]))
                            dvstep(lambda e: e.reduce_sum(smc(SM_MM2), p8.ap()[:], axis=AX.X))
                            dvstep(lambda e, tt=tt, k=k: e.tensor_copy(
                                sl_sb.ap()[:, tt * 2 + k: tt * 2 + k + 1], smc(SM_MM2)))
                            dvstep(lambda e, selap=selap: e.tensor_mul(p8.ap()[:], selap.ap()[:], rt_s.ap()[:, 8:16]))
                            dvstep(lambda e: e.reduce_sum(smc(SM_MM2), p8.ap()[:], axis=AX.X))
                            dvstep(lambda e, k=k: e.tensor_copy(ad_sb.ap()[:, k:k + 1], smc(SM_MM2)))
                        dvstep(lambda e, qtl=qtl: e.tensor_scalar(tok_sb.ap()[:, 0:1].bitcast(F32),
                                                                  iota_f.ap()[:], float(base + qtl * P),
                                                                  None, ALU.add))
                        v_tok = dvstep(lambda e: e.tensor_copy(tok_sb.ap()[:, 1:2], tok_sb.ap()[:, 0:1].bitcast(F32)))
                        wt("gpsimd", "v", v_tok)
                        for k in range(2):
                            i = GP(lambda e, k=k: e.indirect_dma_start(
                                out=idx_dram,
                                out_offset=bass.IndirectOffsetOnAxis(ap=ad_sb.ap()[:, k:k + 1], axis=0),
                                in_=tok_sb.ap()[:, 1:2], in_offset=None))
                            inc("ix", i, 16)
                        ix_prev[0] = cnt["ix"]
                    t_down_done = {}
                    v_last_evac = 0
                    prev_va = 0
                    y_war = {}
                    first_flag = True
                    tiles = [("sh", 0, f_i) for f_i in range(NFTS)]
                    for ti, (kind, e_i, f_i) in enumerate(tiles):
                        par = ti % 2
                        if ti >= 2:
                            wt("sync", "t", t_down_done[ti - 2])
                        else:
                            wt("sync", "t", t_pass0)
                        gsrc = sh_gate.rearrange("(c p) f -> p c f", p=P)[:, :, f_i * FT:(f_i + 1) * FT]
                        usrc = sh_up.rearrange("(c p) f -> p c f", p=P)[:, :, f_i * FT:(f_i + 1) * FT]
                        dsrc = sh_down[f_i * FT:(f_i + 1) * FT, :].rearrange("(c p) d -> p c d", p=P)
                        i = SY(lambda e, par=par, gsrc=gsrc: e.dma_start(
                            out=gbuf[par].ap()[:].rearrange("p (c f) -> p c f", c=NCHUNK),
                            in_=gsrc.bitcast(F32R)))
                        inc(f"w{par}", i, 16)
                        i = SY(lambda e, par=par, usrc=usrc: e.dma_start(
                            out=ubuf[par].ap()[:].rearrange("p (c f) -> p c f", c=NCHUNK),
                            in_=usrc.bitcast(F32R)))
                        inc(f"w{par}", i, 16)
                        i = SY(lambda e, par=par, dsrc=dsrc: e.dma_start(
                            out=dbuf[par].ap()[:].rearrange("p (c d) -> p c d", c=MSUB),
                            in_=dsrc.bitcast(F32R)))
                        w_tile = inc(f"w{par}", i, 16)
                        wt("tensor", f"w{par}", w_tile)
                        for g_i in range(GPP):
                            for m_i in range(MSUB):
                                pg = bank(0 + (m_i % 2))
                                pu = bank(2 + (m_i % 2))
                                wt("tensor", "v", bank_free.get(0 + (m_i % 2), 0))
                                i = None
                                for c in range(NCHUNK):
                                    i = MM(lambda e, c=c, m_i=m_i, par=par, g_i=g_i, pg=pg:
                                           e.matmul(pg, gbuf[par].ap()[:, c * FT + m_i * P: c * FT + (m_i + 1) * P],
                                                    ln2_pass.ap()[:, (g_i * NCHUNK + c) * 512:(g_i * NCHUNK + c + 1) * 512],
                                                    start=(c == 0), stop=(c == NCHUNK - 1)))
                                t_g = inc("t", i)
                                i = None
                                for c in range(NCHUNK):
                                    i = MM(lambda e, c=c, m_i=m_i, par=par, g_i=g_i, pu=pu:
                                           e.matmul(pu, ubuf[par].ap()[:, c * FT + m_i * P: c * FT + (m_i + 1) * P],
                                                    ln2_pass.ap()[:, (g_i * NCHUNK + c) * 512:(g_i * NCHUNK + c + 1) * 512],
                                                    start=(c == 0), stop=(c == NCHUNK - 1)))
                                t_u = inc("t", i)
                                wt("scalar", "t", t_g)
                                if prev_va:
                                    wt("scalar", "v", prev_va)
                                i = SC(lambda e, pg=pg: e.activation(sg_t.ap()[:], pg, AF.Silu))
                                s_silu = inc("s", i)
                                wt("vector", "s", s_silu)
                                wt("vector", "t", t_u)
                                i = DV(lambda e, m_i=m_i, pu=pu: e.tensor_mul(
                                    a_t.ap()[:, m_i * 512:(m_i + 1) * 512], sg_t.ap()[:], pu))
                                v_a = inc("v", i)
                                prev_va = v_a
                                bank_free[0 + (m_i % 2)] = v_a
                                bank_free[2 + (m_i % 2)] = v_a
                            wt("tensor", "v", prev_va)
                            for tt in range(4):
                                pe_v = PS.ap()[:, (2 + (tt % 2)) * 1024:(3 + (tt % 2)) * 1024]
                                wt("tensor", "v", max(bank_free.get(4 + 2 * (tt % 2), 0),
                                                      bank_free.get(5 + 2 * (tt % 2), 0)))
                                i = None
                                for m_i in range(MSUB):
                                    for half in range(2):
                                        i = MM(lambda e, m_i=m_i, half=half, par=par, tt=tt, g_i=g_i, pe_v=pe_v:
                                               e.matmul(pe_v[:, half * 512:(half + 1) * 512],
                                                        a_t.ap()[:, m_i * 512 + tt * P: m_i * 512 + (tt + 1) * P],
                                                        dbuf[par].ap()[:, m_i * DM + half * 512: m_i * DM + (half + 1) * 512],
                                                        start=(m_i == 0), stop=(m_i == MSUB - 1)))
                                t_eo = inc("t", i)
                                wt("vector", "t", t_eo)
                                ty = (g_i * 4 + tt) * DM
                                if ty in y_war:
                                    wt("vector", "v", y_war[ty])
                                if first_flag:
                                    i = DV(lambda e, ty=ty, pe_v=pe_v: e.tensor_copy(
                                        y_acc.ap()[:, ty:ty + DM], pe_v))
                                else:
                                    i = DV(lambda e, ty=ty, pe_v=pe_v: e.tensor_tensor(
                                        y_acc.ap()[:, ty:ty + DM], y_acc.ap()[:, ty:ty + DM],
                                        pe_v, ALU.add))
                                v_last_evac = inc("v", i)
                                y_war[ty] = v_last_evac
                                bank_free[4 + 2 * (tt % 2)] = v_last_evac
                                bank_free[5 + 2 * (tt % 2)] = v_last_evac
                        first_flag = False
                        t_down_done[ti] = cnt["t"]
                        if ti < TPP:
                            routing_tile(ti)

                    wt("gpsimd", "ix", cnt["ix"])
                    i = GP(lambda e: e.dma_start(out=idx_sb.ap()[:],
                                                 in_=idx_dram.rearrange("(p c) one -> p (c one)", p=P)))
                    ix_ld = inc("ix", i, 16)


                    t_xg_read = 0
                    eo_par_sem = {}
                    wpar_base = NFTS

                    def emit_gathers(e_i):
                        wt("gpsimd", "ix", ix_ld)
                        wt("gpsimd", "lt", cnt["lt"])
                        wt("gpsimd", "t", t_xg_read)
                        for b in range(CAPT):
                            i = GP(lambda e, e_i=e_i, b=b: e.indirect_dma_start(
                                out=xg_tm.ap()[:, b * DM:(b + 1) * DM],
                                out_offset=None,
                                in_=ln2tm_dram,
                                in_offset=bass.IndirectOffsetOnAxis(
                                    ap=idx_sb.ap()[:, e_i * CAPT + b: e_i * CAPT + b + 1], axis=0)))
                            inc("xg", i, 16)
                        return cnt["xg"]

                    xg_ready = emit_gathers(0)
                    for e_i in range(E):
                        wt("tensor", "xg", xg_ready)
                        for b in range(CAPT):
                            for c in range(NCHUNK):
                                bk = 0 + (c % 2)
                                wt("tensor", "v", bank_free.get(bk, 0))
                                i = MM(lambda e, b=b, c=c, bk=bk: e.transpose(
                                    bank(bk, 128), xg_tm.ap()[:, b * DM + c * P: b * DM + (c + 1) * P],
                                    idf.ap()))
                                t_tr = inc("t", i)
                                wt("vector", "t", t_tr)
                                i = DV(lambda e, b=b, c=c, bk=bk: e.tensor_copy(
                                    xg_fm.ap()[:, c * CAP + b * P: c * CAP + (b + 1) * P].bitcast(F32R),
                                    bank(bk, 128).bitcast(F32R)))
                                v_ev = inc("v", i)
                                bank_free[bk] = v_ev
                        t_xg_read = cnt["t"]
                        v_xg_fm = cnt["v"]
                        wt("tensor", "v", v_xg_fm)
                        if e_i + 1 < E:
                            xg_ready = emit_gathers(e_i + 1)

                        for f_i in range(NFT):
                            ti = wpar_base + e_i * NFT + f_i
                            par = ti % 2
                            if ti >= 2:
                                wt("sync", "t", t_down_done[ti - 2])
                            gsrc = gate_k[e_i].rearrange("(c p) f -> p c f", p=P)[:, :, f_i * FT:(f_i + 1) * FT]
                            usrc = up_k[e_i].rearrange("(c p) f -> p c f", p=P)[:, :, f_i * FT:(f_i + 1) * FT]
                            dsrc = down_k[e_i][f_i * FT:(f_i + 1) * FT, :].rearrange("(c p) d -> p c d", p=P)
                            i = SY(lambda e, par=par, gsrc=gsrc: e.dma_start(
                                out=gbuf[par].ap()[:].rearrange("p (c f) -> p c f", c=NCHUNK),
                                in_=gsrc.bitcast(F32R)))
                            inc(f"w{par}", i, 16)
                            i = SY(lambda e, par=par, usrc=usrc: e.dma_start(
                                out=ubuf[par].ap()[:].rearrange("p (c f) -> p c f", c=NCHUNK),
                                in_=usrc.bitcast(F32R)))
                            inc(f"w{par}", i, 16)
                            wt("gpsimd", "t", t_down_done.get(ti - 2, t_pass0))
                            i = GP(lambda e, par=par, dsrc=dsrc: e.dma_start(
                                out=dbuf[par].ap()[:].rearrange("p (c d) -> p c d", c=MSUB),
                                in_=dsrc.bitcast(F32R)))
                            w_tile = inc(f"w{par}", i, 16)
                            wt("tensor", f"w{par}", w_tile)
                            for m_i in range(MSUB):
                                pg = bank(0 + (m_i % 2))
                                pu = bank(2 + (m_i % 2))
                                wt("tensor", "v", bank_free.get(0 + (m_i % 2), 0))
                                i = None
                                for c in range(NCHUNK):
                                    i = MM(lambda e, c=c, m_i=m_i, par=par, pg=pg:
                                           e.matmul(pg[:, 0:CAP],
                                                    gbuf[par].ap()[:, c * FT + m_i * P: c * FT + (m_i + 1) * P],
                                                    xg_fm.ap()[:, c * CAP:(c + 1) * CAP],
                                                    start=(c == 0), stop=(c == NCHUNK - 1)))
                                t_g = inc("t", i)
                                i = None
                                for c in range(NCHUNK):
                                    i = MM(lambda e, c=c, m_i=m_i, par=par, pu=pu:
                                           e.matmul(pu[:, 0:CAP],
                                                    ubuf[par].ap()[:, c * FT + m_i * P: c * FT + (m_i + 1) * P],
                                                    xg_fm.ap()[:, c * CAP:(c + 1) * CAP],
                                                    start=(c == 0), stop=(c == NCHUNK - 1)))
                                t_u = inc("t", i)
                                wt("scalar", "t", t_g)
                                if prev_va:
                                    wt("scalar", "v", prev_va)
                                i = SC(lambda e, pg=pg: e.activation(sg_t.ap()[:, 0:CAP], pg[:, 0:CAP], AF.Silu))
                                s_silu = inc("s", i)
                                wt("vector", "s", s_silu)
                                wt("vector", "t", t_u)
                                i = DV(lambda e, m_i=m_i, pu=pu: e.tensor_mul(
                                    a_t.ap()[:, m_i * 512:m_i * 512 + CAP], sg_t.ap()[:, 0:CAP], pu[:, 0:CAP]))
                                v_a = inc("v", i)
                                prev_va = v_a
                                bank_free[0 + (m_i % 2)] = v_a
                                bank_free[2 + (m_i % 2)] = v_a
                            wt("tensor", "v", prev_va)
                            for tt in range(CAPT):
                                pe_v = PS.ap()[:, (2 + (tt % 2)) * 1024:(3 + (tt % 2)) * 1024]
                                wt("tensor", "v", max(bank_free.get(4 + 2 * (tt % 2), 0),
                                                      bank_free.get(5 + 2 * (tt % 2), 0)))
                                i = None
                                for m_i in range(MSUB):
                                    for half in range(2):
                                        i = MM(lambda e, m_i=m_i, half=half, par=par, tt=tt, pe_v=pe_v:
                                               e.matmul(pe_v[:, half * 512:(half + 1) * 512],
                                                        a_t.ap()[:, m_i * 512 + tt * P: m_i * 512 + (tt + 1) * P],
                                                        dbuf[par].ap()[:, m_i * DM + half * 512: m_i * DM + (half + 1) * 512],
                                                        start=(m_i == 0), stop=(m_i == MSUB - 1)))
                                t_eo = inc("t", i)
                                wt("vector", "t", t_eo)
                                ty = tt * DM
                                if f_i == 0:
                                    if ("eo", tt) in eo_par_sem:
                                        wt("vector", "eo", eo_par_sem[("eo", tt)])
                                    i = DV(lambda e, ty=ty, pe_v=pe_v: e.tensor_copy(
                                        eo_acc.ap()[:, ty:ty + DM], pe_v))
                                else:
                                    i = DV(lambda e, ty=ty, pe_v=pe_v: e.tensor_tensor(
                                        eo_acc.ap()[:, ty:ty + DM], eo_acc.ap()[:, ty:ty + DM],
                                        pe_v, ALU.add))
                                v_last_evac = inc("v", i)
                                bank_free[4 + 2 * (tt % 2)] = v_last_evac
                                bank_free[5 + 2 * (tt % 2)] = v_last_evac
                            t_down_done[ti] = cnt["t"]
                        v_eo_done = cnt["v"]
                        wt("gpsimd", "v", v_eo_done)
                        wt("gpsimd", "eg", cnt["eg"])
                        for b in range(CAPT):
                            i = GP(lambda e, e_i=e_i, b=b: e.dma_start(
                                out=eo_dram[e_i * CAP + b * P: e_i * CAP + (b + 1) * P, :],
                                in_=eo_acc.ap()[:, b * DM:(b + 1) * DM]))
                            inc("eo", i, 16)
                        for b in range(CAPT):
                            eo_par_sem[("eo", b)] = cnt["eo"]
                    eo_all = cnt["eo"]

                    for tt in range(TPP):
                        qt = tok0 // P + tt
                        ty = tt * DM
                        wt("gpsimd", "eo", eo_all)
                        wt("gpsimd", "t", t_xg_read)
                        wt("gpsimd", "v", y_war.get(ty, 0))
                        g_vals = []
                        for k in range(2):
                            i = GP(lambda e, tt=tt, k=k: e.indirect_dma_start(
                                out=xg_tm.ap()[:, k * DM:(k + 1) * DM],
                                out_offset=None,
                                in_=eo_dram,
                                in_offset=bass.IndirectOffsetOnAxis(
                                    ap=sl_sb.ap()[:, tt * 2 + k: tt * 2 + k + 1], axis=0)))
                            g_vals.append(inc("eg", i, 16))
                        wt("vector", "eg", g_vals[1])
                        i = DV(lambda e, ty=ty, tt=tt: e.scalar_tensor_tensor(
                            o_tm.ap()[:], xg_tm.ap()[:, 0:DM], w12_sb.ap()[:, tt * 2: tt * 2 + 1],
                            y_acc.ap()[:, ty:ty + DM], ALU.mult, ALU.add))
                        v_c1 = inc("v", i)
                        wt("vector", "v", v_c1)
                        i = DV(lambda e, tt=tt: e.scalar_tensor_tensor(
                            o_tm.ap()[:], xg_tm.ap()[:, DM:2 * DM], w12_sb.ap()[:, tt * 2 + 1: tt * 2 + 2],
                            o_tm.ap()[:], ALU.mult, ALU.add))
                        v_comb2 = inc("v", i)
                        ln_stats(lambda: o_tm.ap()[:], wait_v_scalar=v_comb2,
                                 wait_v_vector=v_comb2, wait_t_scalar=t_lnt_guard[0])
                        i = SC(lambda e: e.activation(o_ln.ap()[:], o_tm.ap()[:],
                                                      AF.Identity, bias=smc(SM_NMRS), scale=smc(SM_RSTD)))
                        s_yln = inc("s", i)
                        slot = tt % 2
                        gsem, gval = slot_guard[slot]
                        if gval:
                            wt("gpsimd", gsem, gval)
                        slot_sem, slot_val = (f"st{qt % 2}", st_x1_tile[(s, qt)])
                        wt("gpsimd", slot_sem, slot_val)
                        i = GP(lambda e, qt=qt, slot=slot: e.dma_start(
                            out=x_tile.ap()[:, slot * DM:(slot + 1) * DM],
                            in_=x1_dram[base + qt * P: base + (qt + 1) * P, :]))
                        xl_x1 = inc(f"xl{slot}", i, 16)
                        wt("vector", "s", s_yln)
                        wt("vector", f"xl{slot}", xl_x1)
                        if so_last_slot[slot]:
                            wt("vector", f"so{slot}", so_last_slot[slot])
                        if st_x1_last_slot[slot]:
                            wt("vector", f"st{slot}", st_x1_last_slot[slot])
                        x1s = slot * DM
                        i = DV(lambda e, slot=slot, x1s=x1s: e.tensor_add(
                            x1_t.ap()[:, x1s:x1s + DM], x_tile.ap()[:, slot * DM:(slot + 1) * DM], o_ln.ap()[:]))
                        v_out = inc("v", i)
                        slot_guard[slot] = ("v", v_out)
                        wt("gpsimd", "v", v_out)
                        i = GP(lambda e, qt=qt, x1s=x1s: e.dma_start(
                            out=out_d[base + qt * P: base + (qt + 1) * P, :], in_=x1_t.ap()[:, x1s:x1s + DM]))
                        so_last_slot[slot] = inc(f"so{slot}", i, 16)

            for nm in ("st0", "st1", "so0", "so1"):
                wt("gpsimd", nm, cnt[nm])
            return

        block.sync(lambda e: emit("sync", e))
        block.tensor(lambda e: emit("tensor", e))
        block.vector(lambda e: emit("vector", e))
        block.scalar(lambda e: emit("scalar", e))
        block.gpsimd(lambda e: emit("gpsimd", e))

    return nc


def make_const_inputs():
    lt = (np.arange(P)[:, None] < np.arange(P)[None, :]).astype(np.float32)
    e = np.arange(8, dtype=np.float32)
    rtc = np.concatenate([
        np.broadcast_to(16.0 - e, (P, 8)),
        np.broadcast_to(3.0 * e, (P, 8)),
        np.broadcast_to(float(CAP) * e, (P, 8)),
    ], axis=1).astype(np.float32)
    rtc2 = np.stack([np.ones(P, np.float32), np.arange(P, dtype=np.float32)], axis=1)
    return dict(
        ident_f=np.eye(P, dtype=np.float32),
        ident_b=np.eye(P, dtype=np.float32).astype(ml_dtypes.bfloat16),
        masks_in=build_masks(),
        ones_in=np.ones((1, P), np.float32),
        lt_in=np.ascontiguousarray(lt),
        rtc_in=np.ascontiguousarray(rtc),
        rtc2_in=np.ascontiguousarray(rtc2),
    )


def make_weight_inputs(kw):
    g = lambda k: np.ascontiguousarray(np.asarray(kw[k], np.float32))
    w = {}
    for pn, src in (("q", "q"), ("k", "k"), ("v", "v"), ("o", "o")):
        w[f"{pn}_Wv"] = g(f"L0_attn_{src}_Wv")
        w[f"{pn}_Wr"] = g(f"L0_attn_{src}_Wr")
        w[f"{pn}_Wl"] = g(f"L0_attn_{src}_Wl")
        w[f"{pn}_b"] = g(f"L0_attn_{src}_b")
    w["gate_w"] = g("L0_ffn_gate_w")
    w["gate_k"] = g("L0_ffn_gate_k")
    w["up_k"] = g("L0_ffn_up_k")
    w["down_k"] = g("L0_ffn_down_k")
    w["sh_gate"] = g("L0_ffn_sh_gate")
    w["sh_up"] = g("L0_ffn_sh_up")
    w["sh_down"] = g("L0_ffn_sh_down")
    return w


def kernel(**inputs):
    NCORES = 8
    x = np.ascontiguousarray(np.asarray(inputs["L0_attn_x"], np.float32))
    B, T, D = x.shape
    per = B // NCORES

    weights = make_weight_inputs(inputs)
    consts = make_const_inputs()

    nc = build_program(n_seq=per, T=T)
    in_maps = []
    for c in range(NCORES):
        m = {"x_in": x[c * per:(c + 1) * per].reshape(per * T, D)}
        m.update(weights)
        m.update(consts)
        in_maps.append(m)
    res = run_bass_kernel_spmd(nc, in_maps, core_ids=list(range(NCORES)))
    out = np.concatenate([r["out"].reshape(per, T, D) for r in res.results], axis=0)
    return out


# version 14 (final)
# speedup vs baseline: 1.3208x; 1.0118x over previous
import numpy as np
import ml_dtypes
import concourse.bass as bass
import concourse.mybir as mybir
from concourse.bass_utils import run_bass_kernel_spmd

F32 = mybir.dt.float32
F32R = mybir.dt.float32r
BF16 = mybir.dt.bfloat16
AF = mybir.ActivationFunctionType
ALU = mybir.AluOpType
AX = mybir.AxisListType

P = 128
DM = 1024
NCHUNK = DM // P
H = 16
E = 8
F = 4096
FS = 2048
FT = 256
MSUB = FT // P
NEG = -60000.0
EPS = 1e-5
SBUF_BYTES = 229376

MOFF = [0, 1, 2, 4, 8]
CAP = 384
CAPT = CAP // P


def build_masks():
    allowed = set(range(5)) | {8, 16, 32, 64, 128, 256, 512, 1024}
    qi = np.arange(P)[:, None]
    ki = np.arange(P)[None, :]
    tiles = []
    for m in MOFF:
        d = 128 * m + qi - ki
        tiles.append(np.where(np.isin(d, list(allowed)), 0.0, NEG).astype(np.float32))
    tiles = tiles + [tiles[2], tiles[1], tiles[0]]
    return np.stack(tiles).astype(ml_dtypes.bfloat16)


def urdhva_block_cols(c):
    out = [(c, c)]
    if c >= 1:
        out.append((c - 1, 8 + (c - 1)))
    if c <= 6:
        out.append((c + 1, 15 + c))
    return out


def build_program(n_seq=2, T=2048):
    NT = T // P
    NG = T // 512
    PASS_TOK = min(1024, T)
    NPASS = T // PASS_TOK
    GPP = PASS_TOK // 512
    TPP = PASS_TOK // P
    NFT = F // FT
    NFTS = FS // FT

    nc = bass.Bass("TRN2", target_bir_lowering=False)

    x_in = nc.dram_tensor("x_in", [n_seq * T, DM], F32, kind="ExternalInput").ap()
    wq = {}
    for pname in ("q", "k", "v", "o"):
        wq[pname + "v"] = nc.dram_tensor(f"{pname}_Wv", [8, P, P], F32, kind="ExternalInput").ap()
        wq[pname + "r"] = nc.dram_tensor(f"{pname}_Wr", [7, P, P], F32, kind="ExternalInput").ap()
        wq[pname + "l"] = nc.dram_tensor(f"{pname}_Wl", [7, P, P], F32, kind="ExternalInput").ap()
        wq[pname + "b"] = nc.dram_tensor(f"{pname}_b", [DM], F32, kind="ExternalInput").ap()
    gate_w = nc.dram_tensor("gate_w", [E, DM], F32, kind="ExternalInput").ap()
    gate_k = nc.dram_tensor("gate_k", [E, DM, F], F32, kind="ExternalInput").ap()
    up_k = nc.dram_tensor("up_k", [E, DM, F], F32, kind="ExternalInput").ap()
    down_k = nc.dram_tensor("down_k", [E, F, DM], F32, kind="ExternalInput").ap()
    sh_gate = nc.dram_tensor("sh_gate", [DM, FS], F32, kind="ExternalInput").ap()
    sh_up = nc.dram_tensor("sh_up", [DM, FS], F32, kind="ExternalInput").ap()
    sh_down = nc.dram_tensor("sh_down", [FS, DM], F32, kind="ExternalInput").ap()
    ident_f = nc.dram_tensor("ident_f", [P, P], F32, kind="ExternalInput").ap()
    ident_b = nc.dram_tensor("ident_b", [P, P], BF16, kind="ExternalInput").ap()
    masks_in = nc.dram_tensor("masks_in", [8, P, P], BF16, kind="ExternalInput").ap()
    ones_in = nc.dram_tensor("ones_in", [1, P], F32, kind="ExternalInput").ap()
    lt_in = nc.dram_tensor("lt_in", [P, P], F32, kind="ExternalInput").ap()
    rtc_in = nc.dram_tensor("rtc_in", [P, 3 * E], F32, kind="ExternalInput").ap()
    rtc2_in = nc.dram_tensor("rtc2_in", [P, 2], F32, kind="ExternalInput").ap()

    x1_dram = nc.dram_tensor("x1_scratch", [n_seq * T, DM], F32, kind="Internal").ap()
    ln2_dram = nc.dram_tensor("ln2_scratch", [n_seq, NG, NCHUNK, P, 512], F32, kind="Internal").ap()
    ln2tm_dram = nc.dram_tensor("ln2tm_scratch", [n_seq * T, DM], F32, kind="Internal").ap()
    idx_dram = nc.dram_tensor("idx_scratch", [P * E * CAPT, 1], mybir.dt.int32, kind="Internal").ap()
    eo_dram = nc.dram_tensor("eo_scratch", [E * CAP, DM], F32, kind="Internal").ap()
    out_d = nc.dram_tensor("out", [n_seq * T, DM], F32, kind="ExternalOutput").ap()

    off = [18432]

    def sb(name, shape, dtype, at=None):
        nbytes = int(np.prod(shape[1:])) * mybir.dt.size(dtype)
        o = off[0] if at is None else at
        h = nc.alloc_sbuf_tensor_at(name, shape, dtype, offset=o)
        if at is None:
            off[0] = o + ((nbytes + 31) // 32) * 32
        return h

    idb = sb("idb", [P, P], BF16)
    idf = sb("idf", [P, P], F32)
    idr = sb("idr", [P, P], F32R)
    msk = sb("msk", [P, 8 * P], BF16)
    onesr = sb("onesr", [1, P], F32R)
    vb_row = sb("vb_row", [1, DM], F32R)
    ob_row = sb("ob_row", [1, DM], F32R)
    qb_col = sb("qb_col", [P, NCHUNK], F32)
    kb_col = sb("kb_col", [P, NCHUNK], F32)
    gwT = sb("gwT", [P, NCHUNK * E], F32R)
    comb = sb("comb", [P, NT * E], F32)
    sm = sb("smalls", [P, 24], F32)
    SM_SUM, SM_SSQ, SM_MEAN, SM_M2, SM_E2, SM_VAR, SM_RSTD, SM_NMRS = range(8)
    SM_NEGMAX, SM_ESUM, SM_LNS, SM_SINV, SM_M1, SM_MM2 = range(8, 14)
    SM_NEGMAX1, SM_ESUM1, SM_LNS1, SM_SINV1 = range(14, 18)
    SM_GES, SM_GLN, SM_GSI = range(18, 21)
    o_tm = sb("o_tm", [P, DM], F32)
    o_ln = sb("o_ln", [P, DM], F32)
    x_tile = sb("x_tile", [P, 2 * DM], F32)
    x1_t = sb("x1_t", [P, 2 * DM], F32)
    lnt = sb("lnt", [P, DM], F32)
    probs = sb("probs", [P, 2 * 5 * P], BF16)
    probs_km = sb("probs_km", [P, 2 * 5 * P], BF16)
    diag = sb("diag", [P, 2 * P], BF16)
    p8 = sb("p8", [P, E], F32)
    pw8 = sb("pw8", [P, E], F32)
    keep8 = sb("keep8", [P, E], F32)
    attn_fm = sb("attn_fm", [P, NCHUNK * P], F32R)
    LT_r = sb("LT_r", [P, P], F32R)
    onescol_r = sb("onescol_r", [P, 1], F32R)
    rtc = sb("rtc", [P, 3 * E], F32)
    iota_f = sb("iota_f", [P, 1], F32)
    base_row = sb("base_row", [1, E], F32R)
    ass_r = sb("ass_r", [P, E], F32R)
    pos_sb = sb("pos_sb", [P, E], F32)
    rt_s = sb("rt_s", [P, 16], F32)
    w12_sb = sb("w12_sb", [P, 2 * (1024 // P)], F32)
    sl_sb = sb("sl_sb", [P, 2 * (1024 // P)], mybir.dt.int32)
    ad_sb = sb("ad_sb", [P, 2], mybir.dt.int32)
    tok_sb = sb("tok_sb", [P, 2], mybir.dt.int32)
    idx_sb = sb("idx_sb", [P, E * CAPT], mybir.dt.int32)
    PERM_END = off[0]
    assert PERM_END <= 75 * 1024, PERM_END

    R1 = PERM_END
    gw_rows = sb("gw_rows", [P, DM], F32R, at=R1)
    qkvW = sb("qkvW", [P, 3 * 22 * P], F32R, at=R1)
    R1_END = R1 + 3 * 22 * P * 4
    ln1_q = sb("ln1_q", [P, NCHUNK * 256], F32R, at=R1_END)
    ln2_grp = sb("ln2_grp", [P, NCHUNK * 512], F32R, at=R1_END + NCHUNK * 256 * 4)
    R3 = R1_END + NCHUNK * 256 * 4 + NCHUNK * 512 * 4
    q_grp = sb("q_grp", [P, NCHUNK * 512], BF16, at=R3)
    k_fm = sb("k_fm", [P, NCHUNK * T], BF16, at=R3 + NCHUNK * 512 * 2)
    v_tm = sb("v_tm", [P, NT * DM], BF16, at=R3 + NCHUNK * 512 * 2 + NCHUNK * T * 2)
    R3V = R3 + NCHUNK * 512 * 2 + 2 * NCHUNK * T * 2
    oW = sb("oW", [P, 22 * P], F32R, at=R3V)
    assert R3V + 22 * P * 4 <= SBUF_BYTES, R3V + 22 * P * 4

    DD = R1
    ln2_pass = sb("ln2_pass", [P, GPP * NCHUNK * 512], F32R, at=DD)
    DY = DD + GPP * NCHUNK * 512 * 4
    y_acc = sb("y_acc", [P, TPP * DM], F32, at=DY)
    a_t = sb("a_t", [P, MSUB * 512], F32R, at=DY + TPP * DM * 4)
    sg_t = sb("sg_t", [P, 512], F32, at=DY + TPP * DM * 4 + MSUB * 512 * 4)
    DB = DY + TPP * DM * 4 + MSUB * 512 * 4 + 2048
    gbuf = [sb(f"gbuf{i}", [P, NCHUNK * FT], F32R, at=DB + i * NCHUNK * FT * 4) for i in range(2)]
    ubuf = [sb(f"ubuf{i}", [P, NCHUNK * FT], F32R, at=DB + (2 + i) * NCHUNK * FT * 4) for i in range(2)]
    dbuf = [sb(f"dbuf{i}", [P, MSUB * DM], F32R, at=DB + 4 * NCHUNK * FT * 4 + i * MSUB * DM * 4) for i in range(2)]
    DS = DB + 4 * NCHUNK * FT * 4 + 2 * MSUB * DM * 4
    xg_tm = sb("xg_tm", [P, CAPT * DM], F32, at=DS)
    xg_fm = sb("xg_fm", [P, NCHUNK * CAP], F32R, at=DS + CAPT * DM * 4)
    eo_acc = sb("eo_acc", [P, CAPT * DM], F32, at=DS + CAPT * DM * 4 + NCHUNK * CAP * 4)
    D_END = DS + 2 * CAPT * DM * 4 + NCHUNK * CAP * 4
    assert D_END <= SBUF_BYTES, D_END

    PS = nc.alloc_psum_tensor("PS", [P, 4096], F32)

    def bank(b, width=512, dt=F32):
        ap = PS.ap()[:, b * 512: b * 512 + width]
        return ap.bitcast(dt) if dt != F32 else ap

    from contextlib import ExitStack
    with ExitStack() as ctx:
        sems = {}
        for name in ("wq", "w0", "w1", "xl0", "xl1", "st0", "st1", "so0", "so1",
                     "l2", "l2r", "lt", "ix", "xg", "eo", "eg", "t", "v", "s", "g"):
            sems[name] = ctx.enter_context(nc.semaphore(name))
        block = ctx.enter_context(nc.Block())

        def emit(role, eng):
            cnt = {k: 0 for k in sems}

            def inc(sem, inst, n=1):
                cnt[sem] += n
                if inst is not None:
                    inst.then_inc(sems[sem], n)
                return cnt[sem]

            def wt(r, sem, val):
                if role == r and val > 0:
                    eng.wait_ge(sems[sem], val)

            def MM(f): return f(eng) if role == "tensor" else None
            def DV(f): return f(eng) if role == "vector" else None
            def SC(f): return f(eng) if role == "scalar" else None
            def GP(f): return f(eng) if role == "gpsimd" else None
            def SY(f): return f(eng) if role == "sync" else None

            smc = lambda i_: sm.ap()[:, i_:i_ + 1]

            def ln_stats(src, width=DM, wait_v_scalar=0, wait_xl=None,
                         wait_t_scalar=0, wait_v_vector=0):
                if wait_xl:
                    wt("vector", wait_xl[0], wait_xl[1])
                if wait_v_vector:
                    wt("vector", "v", wait_v_vector)
                i = DV(lambda e: e.reduce_sum(smc(SM_SUM), src(), axis=AX.X))
                tv = inc("v", i)
                if wait_xl:
                    wt("scalar", wait_xl[0], wait_xl[1])
                if wait_v_scalar:
                    wt("scalar", "v", wait_v_scalar)
                if wait_t_scalar:
                    wt("scalar", "t", wait_t_scalar)
                if lt_last[0]:
                    wt("scalar", "lt", lt_last[0])
                i = SC(lambda e: e.activation(lnt.ap()[:, 0:width], src(), AF.Square,
                                              accum_out=smc(SM_SSQ)))
                s_sq = inc("s", i)
                wt("scalar", "v", tv)
                i = SC(lambda e: e.activation(smc(SM_MEAN), smc(SM_SUM), AF.Identity, scale=1.0 / width))
                s_mean = inc("s", i)
                wt("scalar", "s", s_mean)
                i = SC(lambda e: e.activation(smc(SM_M2), smc(SM_MEAN), AF.Square))
                i = SC(lambda e: e.activation(smc(SM_E2), smc(SM_SSQ), AF.Identity, scale=1.0 / width))
                ts_ = inc("s", i)
                wt("vector", "s", ts_)
                i = DV(lambda e: e.tensor_sub(smc(SM_VAR), smc(SM_E2), smc(SM_M2)))
                v_sub = inc("v", i)
                wt("vector", "v", v_sub)
                i = DV(lambda e: e.tensor_scalar_add(smc(SM_VAR), smc(SM_VAR), float(EPS)))
                tv2 = inc("v", i)
                wt("scalar", "v", tv2)
                i = SC(lambda e: e.activation(smc(SM_LNS), smc(SM_VAR), AF.Ln))
                s_lnv = inc("s", i)
                wt("scalar", "s", s_lnv)
                i = SC(lambda e: e.activation(smc(SM_RSTD), smc(SM_LNS), AF.Exp, scale=-0.5))
                ts2 = inc("s", i)
                wt("vector", "s", ts2)
                i = DV(lambda e: e.scalar_tensor_tensor(smc(SM_NMRS), smc(SM_MEAN), -1.0,
                                                        smc(SM_RSTD), ALU.mult, ALU.mult))
                tv3 = inc("v", i)
                wt("scalar", "v", tv3)
                return ts2

            i = SY(lambda e: e.dma_start(out=idf.ap()[:], in_=ident_f)); inc("wq", i, 16)
            i = SY(lambda e: e.dma_start(out=idr.ap()[:], in_=ident_f.bitcast(F32R))); inc("wq", i, 16)
            i = SY(lambda e: e.dma_start(out=idb.ap()[:], in_=ident_b)); inc("wq", i, 16)
            i = SY(lambda e: e.dma_start(
                out=msk.ap()[:].rearrange("p (m q) -> p m q", m=8),
                in_=masks_in.rearrange("m p q -> p m q"))); inc("wq", i, 16)
            i = SY(lambda e: e.dma_start(out=onesr.ap()[:], in_=ones_in.bitcast(F32R))); inc("wq", i, 16)
            i = SY(lambda e: e.dma_start(out=vb_row.ap()[:], in_=wq["vb"].unsqueeze(0).bitcast(F32R))); inc("wq", i, 16)
            i = SY(lambda e: e.dma_start(out=ob_row.ap()[:], in_=wq["ob"].unsqueeze(0).bitcast(F32R))); inc("wq", i, 16)
            for c in range(NCHUNK):
                i = SY(lambda e, c=c: e.dma_start(out=qb_col.ap()[:, c:c + 1],
                                                  in_=wq["qb"][c * P:(c + 1) * P].unsqueeze(1)))
                inc("wq", i, 16)
                i = SY(lambda e, c=c: e.dma_start(out=kb_col.ap()[:, c:c + 1],
                                                  in_=wq["kb"][c * P:(c + 1) * P].unsqueeze(1)))
                inc("wq", i, 16)
            i = SY(lambda e: e.dma_start(out=LT_r.ap()[:], in_=lt_in.bitcast(F32R))); inc("wq", i, 16)
            i = SY(lambda e: e.dma_start(out=onescol_r.ap()[:], in_=rtc2_in[:, 0:1].bitcast(F32R),
                                         allow_slow_non_contiguous=True)); inc("wq", i, 16)
            i = SY(lambda e: e.dma_start(out=iota_f.ap()[:], in_=rtc2_in[:, 1:2],
                                         allow_slow_non_contiguous=True)); inc("wq", i, 16)
            i = SY(lambda e: e.dma_start(out=rtc.ap()[:], in_=rtc_in)); inc("wq", i, 16)
            i = SY(lambda e: e.dma_start(out=gw_rows.ap()[0:E, :], in_=gate_w.bitcast(F32R)))
            w_consts = inc("wq", i, 16)
            i = GP(lambda e: e.memset(idx_sb.ap()[:], 0))
            g_i0 = inc("g", i)
            wt("gpsimd", "g", g_i0)
            i = GP(lambda e: e.dma_start(out=idx_dram.rearrange("(p c) one -> p (c one)", p=P),
                                         in_=idx_sb.ap()[:]))
            inc("ix", i, 16)
            wt("tensor", "wq", w_consts)
            for c in range(NCHUNK):
                i = MM(lambda e, c=c: e.transpose(bank(6, 128, F32R), gw_rows.ap()[:, c * P:(c + 1) * P],
                                                  idr.ap()))
                t_tr = inc("t", i)
                wt("vector", "t", t_tr)
                i = DV(lambda e, c=c: e.tensor_copy(gwT.ap()[:, c * E:(c + 1) * E],
                                                    bank(6, 128, F32R)[:, 0:E]))
                v_e = inc("v", i)
                wt("tensor", "v", v_e)

            st_x1_tile = {}
            st_x1_last_slot = {0: 0, 1: 0}
            so_last_slot = {0: 0, 1: 0}
            bank_free = {}
            lt_last = [0]

            for s in range(n_seq):
                base = s * T
                wt("sync", "t", cnt["t"])
                wt("sync", "v", cnt["v"])
                for pi, pn in enumerate(("q", "k", "v")):
                    for part, lo, hi in (("v", 0, 8), ("r", 8, 15), ("l", 15, 22)):
                        i = SY(lambda e, pn=pn, pi=pi, part=part, lo=lo, hi=hi: e.dma_start(
                            out=qkvW.ap()[:, (pi * 22 + lo) * P:(pi * 22 + hi) * P]
                                .rearrange("p (b q) -> p b q", b=hi - lo),
                            in_=wq[pn + part].rearrange("b p q -> p b q").bitcast(F32R)))
                        inc("wq", i, 16)
                for part, lo, hi in (("v", 0, 8), ("r", 8, 15), ("l", 15, 22)):
                    i = SY(lambda e, part=part, lo=lo, hi=hi: e.dma_start(
                        out=oW.ap()[:, lo * P:hi * P].rearrange("p (b q) -> p b q", b=hi - lo),
                        in_=wq["o" + part].rearrange("b p q -> p b q").bitcast(F32R)))
                    inc("wq", i, 16)
                w_qkv = cnt["wq"]

                s_exp_hist = {}
                t_lnt_guard = [0]
                l2_war = [0]
                slot_guard = {0: ("s", 0), 1: ("s", 0)}
                for g in range(NG):
                    for sub in range(2):
                        for tl in range(2):
                            qt = g * 4 + sub * 2 + tl
                            slot = qt % 2
                            gsem, gval = slot_guard[slot]
                            if gval:
                                wt("gpsimd", gsem, gval)
                            i = GP(lambda e, qt=qt, slot=slot: e.dma_start(
                                out=x_tile.ap()[:, slot * DM:(slot + 1) * DM],
                                in_=x_in[base + qt * P: base + (qt + 1) * P, :]))
                            xl_v = inc(f"xl{slot}", i, 16)
                            src = (lambda slot=slot: x_tile.ap()[:, slot * DM:(slot + 1) * DM])
                            ln_stats(src, wait_xl=(f"xl{slot}", xl_v), wait_t_scalar=t_lnt_guard[0])
                            i = SC(lambda e, slot=slot: e.activation(
                                lnt.ap()[:], x_tile.ap()[:, slot * DM:(slot + 1) * DM],
                                AF.Identity, bias=smc(SM_NMRS), scale=smc(SM_RSTD)))
                            s_ln = inc("s", i)
                            slot_guard[slot] = ("s", s_ln)
                            wt("tensor", "s", s_ln)
                            for c in range(NCHUNK):
                                b = 6 + (c % 2)
                                wt("tensor", "v", bank_free.get(b, 0))
                                i = MM(lambda e, c=c, b=b: e.transpose(
                                    bank(b, 128), lnt.ap()[:, c * P:(c + 1) * P], idf.ap()))
                                t_tr = inc("t", i)
                                wt("vector", "t", t_tr)
                                i = DV(lambda e, c=c, tl=tl, b=b: e.tensor_copy(
                                    ln1_q.ap()[:, c * 256 + tl * P: c * 256 + (tl + 1) * P].bitcast(F32R),
                                    bank(b, 128).bitcast(F32R)))
                                v_ev = inc("v", i)
                                bank_free[b] = v_ev
                            t_lnt_guard[0] = cnt["t"]
                        v_ln1 = cnt["v"]
                        wt("tensor", "wq", w_qkv)
                        wt("tensor", "v", v_ln1)
                        wt("tensor", "s", cnt["s"])
                        for pi, pn in ((0, "q"), (1, "k")):
                            bias_c = qb_col if pn == "q" else kb_col
                            for c in range(NCHUNK):
                                bnk = (0 if pn == "q" else 2) + (c % 2)
                                rlist = urdhva_block_cols(c)
                                wt("tensor", "v", bank_free.get(bnk, 0))
                                i = None
                                for j, (r, wcol) in enumerate(rlist):
                                    i = MM(lambda e, pi=pi, r=r, wcol=wcol, j=j, bnk=bnk, rl=len(rlist):
                                           e.matmul(bank(bnk, 256),
                                                    qkvW.ap()[:, (pi * 22 + wcol) * P:(pi * 22 + wcol + 1) * P],
                                                    ln1_q.ap()[:, r * 256:(r + 1) * 256],
                                                    start=(j == 0), stop=(j == rl - 1)))
                                t_mm = inc("t", i)
                                wt("vector", "t", t_mm)
                                if pn == "q":
                                    dst = lambda c=c, sub=sub: q_grp.ap()[:, c * 512 + sub * 256: c * 512 + (sub + 1) * 256]
                                    scale2 = 0.125
                                else:
                                    dst = lambda c=c, g=g, sub=sub: k_fm.ap()[:, c * T + g * 512 + sub * 256: c * T + g * 512 + (sub + 1) * 256]
                                    scale2 = 1.0
                                i = DV(lambda e, dst=dst, bias_c=bias_c, scale2=scale2, bnk=bnk, c=c:
                                       e.tensor_scalar(dst(), bank(bnk, 256), bias_c.ap()[:, c:c + 1], scale2,
                                                       ALU.add, ALU.mult))
                                v_ev = inc("v", i)
                                bank_free[bnk] = v_ev
                        for tl in range(2):
                            tt = sub * 2 + tl
                            bnk = 4 + (tl % 2)
                            wt("tensor", "v", bank_free.get(bnk, 0))
                            i = None
                            for c in range(NCHUNK):
                                seg = bank(bnk)[:, (c % 4) * P:(c % 4 + 1) * P]
                                rlist = urdhva_block_cols(c)
                                fib = (c % 4 == 0)
                                for j, (r, wcol) in enumerate(rlist):
                                    i = MM(lambda e, r=r, wcol=wcol, j=j, tl=tl, seg=seg, fib=fib:
                                           e.matmul(seg,
                                                    ln1_q.ap()[:, r * 256 + tl * P: r * 256 + (tl + 1) * P],
                                                    qkvW.ap()[:, (2 * 22 + wcol) * P:(2 * 22 + wcol + 1) * P],
                                                    start=(fib and j == 0), stop=False))
                                i = MM(lambda e, c=c, seg=seg: e.matmul(
                                    seg, onesr.ap()[0:1, 0:P], vb_row.ap()[0:1, c * P:(c + 1) * P],
                                    start=False, stop=(c % 4 == 3)))
                                if c % 4 == 3:
                                    t_mm = inc("t", i)
                                    wt("vector", "t", t_mm)
                                    half = c // 4
                                    i2 = DV(lambda e, g=g, tt=tt, half=half, bnk=bnk: e.tensor_copy(
                                        v_tm.ap()[:, (g * 4 + tt) * DM + half * 512:(g * 4 + tt) * DM + (half + 1) * 512],
                                        bank(bnk)))
                                    v_ev = inc("v", i2)
                                    bank_free[bnk] = v_ev
                                    if c == 3:
                                        wt("tensor", "v", v_ev)
                    for qt in range(g * 4, g * 4 + 4):
                        tlq = qt % 4
                        ctn = min(3, qt + 1)
                        kts = [qt - m for m in range(ctn - 1, -1, -1)]
                        kts += [qt - m for m in (4, 8) if qt - m >= 0]
                        nb = len(kts)
                        slot = qt % 2
                        gsem, gval = slot_guard[slot]
                        if gval:
                            wt("gpsimd", gsem, gval)
                        i = GP(lambda e, qt=qt, slot=slot: e.dma_start(
                            out=x_tile.ap()[:, slot * DM:(slot + 1) * DM],
                            in_=x_in[base + qt * P: base + (qt + 1) * P, :]))
                        xl_x = inc(f"xl{slot}", i, 16)

                        v_av_evac = 0
                        t_sc_h = {}
                        v_diag_h = {}
                        t_post_softB = {}

                        def emit_qk(h):
                            ch = h // 2
                            po = 64 * (h % 2)
                            sb0 = 0 if h % 2 == 0 else 2
                            prev = s_exp_hist.get((qt, h - 2))
                            if prev is None and h < 2:
                                prev = s_exp_hist.get((qt - 1, h + 14))
                            if h == 1:
                                pg_ = s_exp_hist.get((qt - 1, "gate"))
                                if pg_:
                                    prev = max(prev or 0, pg_)
                            if prev:
                                wt("tensor", "s", prev)
                            i = None
                            seg0 = bank(sb0)[:, 0:ctn * P]
                            kt0 = kts[0]
                            i = MM(lambda e, seg0=seg0, ch=ch, po=po, tlq=tlq, kt0=kt0, ctn=ctn:
                                   e.matmul(seg0,
                                            q_grp.ap()[po:po + 64, ch * 512 + tlq * P: ch * 512 + (tlq + 1) * P],
                                            k_fm.ap()[po:po + 64, ch * T + kt0 * P: ch * T + (kt0 + ctn) * P],
                                            start=True, stop=False))
                            i = MM(lambda e, seg0=seg0, ctn=ctn, nb=nb:
                                   e.matmul(seg0, idb.ap(), msk.ap()[:, (8 - ctn) * P:8 * P],
                                            start=False, stop=(nb == ctn)))
                            for j in range(ctn, nb):
                                kt = kts[j]
                                sbank = sb0 if j < 4 else sb0 + 1
                                seg = bank(sbank)[:, (j % 4) * P:(j % 4 + 1) * P]
                                i = MM(lambda e, seg=seg, ch=ch, po=po, tlq=tlq, kt=kt, j=j:
                                       e.matmul(seg,
                                                q_grp.ap()[po:po + 64, ch * 512 + tlq * P: ch * 512 + (tlq + 1) * P],
                                                k_fm.ap()[po:po + 64, ch * T + kt * P: ch * T + (kt + 1) * P],
                                                start=(j == 4), stop=False))
                                mi = MOFF.index(qt - kt)
                                i = MM(lambda e, seg=seg, mi=mi, j=j, nb=nb:
                                       e.matmul(seg, idb.ap(), msk.ap()[:, mi * P:(mi + 1) * P],
                                                start=False, stop=(j == nb - 1 or j == 3)))
                            t_sc_h[h] = inc("t", i)

                        def emit_softA(h):
                            sb0 = 0 if h % 2 == 0 else 2
                            pr = h % 2
                            ESX = SM_ESUM if pr == 0 else SM_ESUM1
                            LNX = SM_LNS if pr == 0 else SM_LNS1
                            SIX = SM_SINV if pr == 0 else SM_SINV1
                            pb = probs.ap()[:, pr * 5 * P:(pr * 5 + 5) * P]
                            db = diag.ap()[:, pr * P:(pr + 1) * P]
                            wt("scalar", "t", t_sc_h[h])
                            if h - 2 in t_post_softB:
                                wt("scalar", "t", t_post_softB[h - 2])
                            i = SC(lambda e, sb0=sb0, nb=nb, ESX=ESX, pb=pb: e.activation(
                                pb[:, 0:nb * P], PS.ap()[:, sb0 * 512: sb0 * 512 + nb * P],
                                AF.Exp, accum_out=smc(ESX)))
                            s_exp = inc("s", i)
                            s_exp_hist[(qt, h)] = s_exp
                            wt("scalar", "s", s_exp)
                            i = SC(lambda e, ESX=ESX, LNX=LNX: e.activation(smc(LNX), smc(ESX), AF.Ln))
                            s_l = inc("s", i)
                            wt("scalar", "s", s_l)
                            i = SC(lambda e, LNX=LNX, SIX=SIX: e.activation(smc(SIX), smc(LNX), AF.Exp, scale=-1.0))
                            s_sinv = inc("s", i)
                            wt("vector", "s", s_sinv)
                            i = DV(lambda e, SIX=SIX, db=db: e.tensor_scalar(db, idb.ap(), smc(SIX), None, ALU.mult))
                            v_diag_h[h] = inc("v", i)

                        def emit_softB(h):
                            nonlocal v_av_evac
                            ch = h // 2
                            po = 64 * (h % 2)
                            pr = h % 2
                            pb = probs.ap()[:, pr * 5 * P:(pr * 5 + 5) * P]
                            kb = probs_km.ap()[:, pr * 5 * P:(pr * 5 + 5) * P]
                            db = diag.ap()[:, pr * P:(pr + 1) * P]
                            avb = 6 + (h % 2)
                            wt("tensor", "v", v_diag_h[h])
                            wt("tensor", "v", bank_free.get(avb, 0))
                            wt("tensor", "v", bank_free.get(4, 0))
                            if nb > 4:
                                wt("tensor", "v", bank_free.get(5, 0))
                            i = None
                            for j in range(nb):
                                trb = 4 if j < 4 else 5
                                seg = bank(trb)[:, (j % 4) * P:(j % 4 + 1) * P]
                                i = MM(lambda e, j=j, seg=seg, pb=pb, db=db: e.matmul(
                                    seg, pb[:, j * P:(j + 1) * P], db,
                                    start=True, stop=True))
                            t_tr = inc("t", i)
                            wt("vector", "t", t_tr)
                            nb4 = min(nb, 4)
                            i2 = DV(lambda e, nb4=nb4, kb=kb: e.tensor_copy(
                                kb[:, 0:nb4 * P], bank(4, nb4 * P)))
                            v_km = inc("v", i2)
                            bank_free[4] = v_km
                            if nb > 4:
                                i2 = DV(lambda e, kb=kb: e.tensor_copy(
                                    kb[:, 4 * P:5 * P], bank(5, 128)))
                                v_km = inc("v", i2)
                                bank_free[5] = v_km
                            wt("tensor", "v", v_km)
                            i = None
                            for j in range(nb):
                                i = MM(lambda e, j=j, kt=kts[j], h=h, avb=avb, nb=nb, kb=kb: e.matmul(
                                    bank(avb, 128)[0:64, :],
                                    v_tm.ap()[:, kt * DM + h * 64: kt * DM + (h + 1) * 64],
                                    kb[:, j * P:(j + 1) * P],
                                    start=(j == 0), stop=(j == nb - 1)))
                            t_av = inc("t", i)
                            t_post_softB[h] = t_av
                            wt("vector", "t", t_av)
                            i = DV(lambda e, ch=ch, po=po, avb=avb: e.tensor_copy(
                                attn_fm.ap()[po:po + 64, ch * P:(ch + 1) * P].bitcast(F32R),
                                bank(avb, 128)[0:64, :].bitcast(F32R)))
                            v_av_evac = inc("v", i)
                            bank_free[avb] = v_av_evac

                        for step in range(H + 2):
                            if step < H:
                                emit_qk(step)
                            if 1 <= step <= H:
                                emit_softA(step - 1)
                            if step >= 2:
                                emit_softB(step - 2)

                        wt("tensor", "wq", w_qkv)
                        wt("tensor", "v", v_av_evac)
                        ov = PS.ap()[:, 3 * 1024: 4 * 1024]
                        i = None
                        for c in range(NCHUNK):
                            rlist = urdhva_block_cols(c)
                            for j, (r, wcol) in enumerate(rlist):
                                i = MM(lambda e, c=c, r=r, wcol=wcol, j=j, ov=ov:
                                       e.matmul(ov[:, c * P:(c + 1) * P],
                                                attn_fm.ap()[:, r * P:(r + 1) * P],
                                                oW.ap()[:, wcol * P:(wcol + 1) * P],
                                                start=(c % 4 == 0 and j == 0), stop=False))
                            i = MM(lambda e, c=c, ov=ov: e.matmul(
                                ov[:, c * P:(c + 1) * P], onesr.ap()[0:1, 0:P],
                                ob_row.ap()[0:1, c * P:(c + 1) * P],
                                start=False, stop=(c % 4 == 3)))
                        t_oproj = inc("t", i)
                        wt("vector", "t", t_oproj)
                        i = DV(lambda e, ov=ov: e.tensor_copy(o_tm.ap()[:], ov))
                        v_otm = inc("v", i)
                        bank_free[6] = v_otm
                        bank_free[7] = v_otm
                        ln_stats(lambda: o_tm.ap()[:], wait_v_scalar=v_otm,
                                 wait_v_vector=v_otm, wait_t_scalar=t_lnt_guard[0])
                        i = SC(lambda e: e.activation(o_ln.ap()[:], o_tm.ap()[:], AF.Identity,
                                                      bias=smc(SM_NMRS), scale=smc(SM_RSTD)))
                        s_oln = inc("s", i)
                        wt("vector", "s", s_oln)
                        wt("vector", f"xl{slot}", xl_x)
                        if st_x1_last_slot[slot]:
                            wt("vector", f"st{slot}", st_x1_last_slot[slot])
                        if so_last_slot[slot]:
                            wt("vector", f"so{slot}", so_last_slot[slot])
                        x1s = (qt % 2) * DM
                        i = DV(lambda e, slot=slot, x1s=x1s: e.tensor_add(
                            x1_t.ap()[:, x1s:x1s + DM], x_tile.ap()[:, slot * DM:(slot + 1) * DM], o_ln.ap()[:]))
                        v_x1 = inc("v", i)
                        slot_guard[slot] = ("v", v_x1)
                        wt("gpsimd", "v", v_x1)
                        i = GP(lambda e, qt=qt, x1s=x1s: e.dma_start(
                            out=x1_dram[base + qt * P: base + (qt + 1) * P, :], in_=x1_t.ap()[:, x1s:x1s + DM]))
                        st_x1_tile[(s, qt)] = inc(f"st{slot}", i, 16)
                        st_x1_last_slot[slot] = st_x1_tile[(s, qt)]
                        ln_stats(lambda x1s=x1s: x1_t.ap()[:, x1s:x1s + DM], wait_v_scalar=v_x1,
                                 wait_v_vector=v_x1)
                        i = SC(lambda e, x1s=x1s: e.activation(lnt.ap()[:], x1_t.ap()[:, x1s:x1s + DM],
                                                               AF.Identity, bias=smc(SM_NMRS), scale=smc(SM_RSTD)))
                        s_ln2 = inc("s", i)
                        wt("gpsimd", "s", s_ln2)
                        i = GP(lambda e, qt=qt: e.dma_start(
                            out=ln2tm_dram[base + qt * P: base + (qt + 1) * P, :], in_=lnt.ap()[:]))
                        lt_last[0] = inc("lt", i, 16)
                        wt("tensor", "s", s_ln2)
                        for c in range(NCHUNK):
                            b = 4 + (c % 2)
                            wt("tensor", "v", bank_free.get(b, 0))
                            i = MM(lambda e, c=c, b=b: e.transpose(
                                bank(b, 128), lnt.ap()[:, c * P:(c + 1) * P], idf.ap()))
                            t_tr = inc("t", i)
                            wt("vector", "t", t_tr)
                            if tlq == 0 and c == 0 and l2_war[0]:
                                wt("vector", "l2", l2_war[0])
                            i = DV(lambda e, c=c, tlq=tlq, b=b: e.tensor_copy(
                                ln2_grp.ap()[:, c * 512 + tlq * P: c * 512 + (tlq + 1) * P],
                                bank(b, 128).bitcast(F32R)))
                            v_ev = inc("v", i)
                            bank_free[b] = v_ev
                        t_lnt_guard[0] = cnt["t"]
                        gv = bank(2)[:, 0:E]
                        wt("tensor", "v", cnt["v"])
                        wt("tensor", "s", s_exp_hist[(qt, 15)])
                        i = None
                        for c in range(NCHUNK):
                            i = MM(lambda e, c=c, tlq=tlq, gv=gv: e.matmul(
                                gv, ln2_grp.ap()[:, c * 512 + tlq * P: c * 512 + (tlq + 1) * P],
                                gwT.ap()[:, c * E:(c + 1) * E],
                                start=(c == 0), stop=(c == NCHUNK - 1)))
                        t_gate = inc("t", i)
                        wt("vector", "t", t_gate)
                        i = DV(lambda e, gv=gv: e.reduce_max(smc(SM_NEGMAX), gv, axis=AX.X, negate=True))
                        v_gm = inc("v", i)
                        wt("scalar", "v", v_gm)
                        i = SC(lambda e, gv=gv: e.activation(p8.ap()[:], gv, AF.Exp,
                                                             bias=smc(SM_NEGMAX), accum_out=smc(SM_GES)))
                        s_e8 = inc("s", i)
                        s_exp_hist[(qt, "gate")] = s_e8
                        wt("scalar", "s", s_e8)
                        i = SC(lambda e: e.activation(smc(SM_GLN), smc(SM_GES), AF.Ln))
                        s_l8 = inc("s", i)
                        wt("scalar", "s", s_l8)
                        i = SC(lambda e: e.activation(smc(SM_GSI), smc(SM_GLN), AF.Exp, scale=-1.0))
                        s_g = inc("s", i)
                        wt("vector", "s", s_g)
                        i = DV(lambda e: e.tensor_scalar(p8.ap()[:], p8.ap()[:], smc(SM_GSI), None, ALU.mult))
                        v_p8 = inc("v", i)
                        wt("vector", "v", v_p8)
                        i = DV(lambda e: e.reduce_max(smc(SM_M1), p8.ap()[:], axis=AX.X))
                        v_m1 = inc("v", i)
                        wt("vector", "v", v_m1)
                        i = DV(lambda e: e.tensor_scalar(keep8.ap()[:], p8.ap()[:], smc(SM_M1), None, ALU.is_ge))
                        v_k1 = inc("v", i)
                        wt("vector", "v", v_k1)
                        i = DV(lambda e: e.scalar_tensor_tensor(pw8.ap()[:], keep8.ap()[:], -2.0,
                                                                p8.ap()[:], ALU.mult, ALU.add))
                        v_pw = inc("v", i)
                        wt("vector", "v", v_pw)
                        i = DV(lambda e: e.reduce_max(smc(SM_MM2), pw8.ap()[:], axis=AX.X))
                        v_m2 = inc("v", i)
                        wt("vector", "v", v_m2)
                        i = DV(lambda e: e.tensor_scalar(keep8.ap()[:], p8.ap()[:], smc(SM_MM2), None, ALU.is_ge))
                        v_k2 = inc("v", i)
                        wt("vector", "v", v_k2)
                        i = DV(lambda e, qt=qt: e.tensor_mul(comb.ap()[:, qt * E:(qt + 1) * E],
                                                             p8.ap()[:], keep8.ap()[:]))
                        g_comb = inc("v", i)

                    wt("gpsimd", "v", cnt["v"])
                    i = GP(lambda e, s=s, g=g: e.dma_start(
                        out=ln2_dram[s, g].rearrange("c p t -> p c t").bitcast(F32R),
                        in_=ln2_grp.ap()[:].rearrange("p (c t) -> p c t", c=NCHUNK)))
                    l2_war[0] = inc("l2", i, 16)

                v_comb_done = cnt["v"]
                g_comb_done = cnt["g"]

                for p_i in range(NPASS):
                    tok0 = p_i * PASS_TOK
                    t_pass0 = cnt["t"]
                    wt("tensor", "v", cnt["v"])
                    wt("tensor", "s", cnt["s"])
                    wt("tensor", "g", cnt["g"])
                    wt("sync", "t", cnt["t"])
                    wt("sync", "v", cnt["v"])
                    wt("sync", "l2", cnt["l2"])
                    i = SY(lambda e, s=s, p_i=p_i: e.dma_start(
                        out=ln2_pass.ap()[:].rearrange("p (g c t) -> p g c t", g=GPP, c=NCHUNK),
                        in_=ln2_dram[s, p_i * GPP:(p_i + 1) * GPP].rearrange("g c p t -> p g c t").bitcast(F32R)))
                    l2r_v = inc("l2r", i, 16)
                    wt("tensor", "l2r", l2r_v)

                    wt("vector", "v", cnt["v"])
                    i = DV(lambda e: e.tensor_scalar(base_row.ap()[:], onesr.ap()[0:1, 0:E].bitcast(F32),
                                                     0.0, None, ALU.mult))
                    v_b0 = inc("v", i)
                    wt("vector", "v", v_b0)
                    wt("gpsimd", "xg", cnt["xg"])
                    wt("vector", "eg", cnt["eg"])
                    ix_prev = [cnt["ix"]]
                    def routing_tile(tt):
                        qtl = tok0 // P + tt
                        cslice = comb.ap()[:, qtl * E:(qtl + 1) * E]
                        i = DV(lambda e, cslice=cslice: e.tensor_scalar(
                            ass_r.ap()[:], cslice, 0.0, None, ALU.is_gt))
                        v_ass = inc("v", i)
                        wt("tensor", "v", v_ass)
                        wt("tensor", "v", bank_free.get(0, 0))
                        i = MM(lambda e: e.matmul(bank(0)[:, 0:E], LT_r.ap(), ass_r.ap(),
                                                  start=True, stop=False))
                        i = MM(lambda e: e.matmul(bank(0)[:, 0:E], onesr.ap()[0:1, 0:P],
                                                  base_row.ap(), start=False, stop=True))
                        t_pos = inc("t", i)
                        i = MM(lambda e: e.matmul(bank(1)[0:1, 0:E], onescol_r.ap(), ass_r.ap(),
                                                  start=True, stop=True))
                        t_cnt = inc("t", i)
                        wt("vector", "t", t_cnt)
                        i = DV(lambda e: e.tensor_copy(pos_sb.ap()[:], bank(0)[:, 0:E]))
                        i = DV(lambda e: e.tensor_tensor(base_row.ap()[:],
                                                         base_row.ap()[:].bitcast(F32),
                                                         bank(1)[0:1, 0:E], ALU.add))
                        v_pos = inc("v", i)
                        bank_free[0] = v_pos
                        bank_free[1] = v_pos
                        wt("vector", "v", v_pos)
                        RS = rt_s.ap()
                        def dvstep(f):
                            ii = DV(f)
                            vv = inc("v", ii)
                            wt("vector", "v", vv)
                            return vv
                        dvstep(lambda e: e.tensor_scalar(RS[:, 0:E], pos_sb.ap()[:], float(CAP - 1), None, ALU.min))
                        dvstep(lambda e: e.tensor_scalar(RS[:, 8:8 + E], RS[:, 0:E], 128.0, None, ALU.is_ge))
                        dvstep(lambda e: e.tensor_scalar(p8.ap()[:], RS[:, 0:E], 256.0, None, ALU.is_ge))
                        dvstep(lambda e: e.tensor_add(RS[:, 8:8 + E], RS[:, 8:8 + E], p8.ap()[:]))
                        dvstep(lambda e: e.scalar_tensor_tensor(RS[:, 0:E], RS[:, 8:8 + E], -128.0,
                                                                RS[:, 0:E], ALU.mult, ALU.add))
                        dvstep(lambda e: e.scalar_tensor_tensor(RS[:, 8:8 + E], RS[:, 0:E],
                                                                float(E * CAPT), RS[:, 8:8 + E],
                                                                ALU.mult, ALU.add))
                        dvstep(lambda e: e.tensor_add(RS[:, 8:8 + E], RS[:, 8:8 + E], rtc.ap()[:, E:2 * E]))
                        dvstep(lambda e: e.tensor_scalar(RS[:, 0:E], pos_sb.ap()[:], float(CAP - 1), None, ALU.min))
                        dvstep(lambda e: e.tensor_add(RS[:, 0:E], RS[:, 0:E], rtc.ap()[:, 2 * E:3 * E]))
                        i = DV(lambda e: e.tensor_mul(pos_sb.ap()[:], ass_r.ap()[:].bitcast(F32), rtc.ap()[:, 0:E]))
                        v_sc = inc("v", i)
                        wt("vector", "v", v_sc)
                        i = DV(lambda e: e.reduce_max(smc(SM_M1), pos_sb.ap()[:], axis=AX.X))
                        v_m1 = inc("v", i)
                        wt("vector", "v", v_m1)
                        i = DV(lambda e: e.tensor_scalar(keep8.ap()[:], pos_sb.ap()[:], smc(SM_M1), None, ALU.is_ge))
                        v_s1 = inc("v", i)
                        wt("vector", "v", v_s1)
                        i = DV(lambda e: e.tensor_sub(pw8.ap()[:], ass_r.ap()[:].bitcast(F32), keep8.ap()[:]))
                        v_s2 = inc("v", i)
                        wt("vector", "v", v_s2)
                        wt("vector", "ix", ix_prev[0])
                        for k, selap in ((0, keep8), (1, pw8)):
                            dvstep(lambda e, selap=selap, cslice=cslice: e.tensor_mul(p8.ap()[:], selap.ap()[:], cslice))
                            dvstep(lambda e, tt=tt, k=k: e.reduce_sum(w12_sb.ap()[:, tt * 2 + k: tt * 2 + k + 1],
                                                                      p8.ap()[:], axis=AX.X))
                            dvstep(lambda e, selap=selap: e.tensor_mul(p8.ap()[:], selap.ap()[:], rt_s.ap()[:, 0:8]))
                            dvstep(lambda e: e.reduce_sum(smc(SM_MM2), p8.ap()[:], axis=AX.X))
                            dvstep(lambda e, tt=tt, k=k: e.tensor_copy(
                                sl_sb.ap()[:, tt * 2 + k: tt * 2 + k + 1], smc(SM_MM2)))
                            dvstep(lambda e, selap=selap: e.tensor_mul(p8.ap()[:], selap.ap()[:], rt_s.ap()[:, 8:16]))
                            dvstep(lambda e: e.reduce_sum(smc(SM_MM2), p8.ap()[:], axis=AX.X))
                            dvstep(lambda e, k=k: e.tensor_copy(ad_sb.ap()[:, k:k + 1], smc(SM_MM2)))
                        dvstep(lambda e, qtl=qtl: e.tensor_scalar(tok_sb.ap()[:, 0:1].bitcast(F32),
                                                                  iota_f.ap()[:], float(base + qtl * P),
                                                                  None, ALU.add))
                        v_tok = dvstep(lambda e: e.tensor_copy(tok_sb.ap()[:, 1:2], tok_sb.ap()[:, 0:1].bitcast(F32)))
                        wt("gpsimd", "v", v_tok)
                        for k in range(2):
                            i = GP(lambda e, k=k: e.indirect_dma_start(
                                out=idx_dram,
                                out_offset=bass.IndirectOffsetOnAxis(ap=ad_sb.ap()[:, k:k + 1], axis=0),
                                in_=tok_sb.ap()[:, 1:2], in_offset=None))
                            inc("ix", i, 16)
                        ix_prev[0] = cnt["ix"]
                    t_down_done = {}
                    v_last_evac = 0
                    prev_va = 0
                    y_war = {}
                    first_flag = True
                    tiles = [("sh", 0, f_i) for f_i in range(NFTS)]
                    for ti, (kind, e_i, f_i) in enumerate(tiles):
                        par = ti % 2
                        if ti >= 2:
                            wt("sync", "t", t_down_done[ti - 2])
                        else:
                            wt("sync", "t", t_pass0)
                        gsrc = sh_gate.rearrange("(c p) f -> p c f", p=P)[:, :, f_i * FT:(f_i + 1) * FT]
                        usrc = sh_up.rearrange("(c p) f -> p c f", p=P)[:, :, f_i * FT:(f_i + 1) * FT]
                        dsrc = sh_down[f_i * FT:(f_i + 1) * FT, :].rearrange("(c p) d -> p c d", p=P)
                        i = SY(lambda e, par=par, gsrc=gsrc: e.dma_start(
                            out=gbuf[par].ap()[:].rearrange("p (c f) -> p c f", c=NCHUNK),
                            in_=gsrc.bitcast(F32R)))
                        inc(f"w{par}", i, 16)
                        i = SY(lambda e, par=par, usrc=usrc: e.dma_start(
                            out=ubuf[par].ap()[:].rearrange("p (c f) -> p c f", c=NCHUNK),
                            in_=usrc.bitcast(F32R)))
                        inc(f"w{par}", i, 16)
                        i = SY(lambda e, par=par, dsrc=dsrc: e.dma_start(
                            out=dbuf[par].ap()[:].rearrange("p (c d) -> p c d", c=MSUB),
                            in_=dsrc.bitcast(F32R)))
                        w_tile = inc(f"w{par}", i, 16)
                        wt("tensor", f"w{par}", w_tile)
                        for g_i in range(GPP):
                            for m_i in range(MSUB):
                                pg = bank(0 + (m_i % 2))
                                pu = bank(2 + (m_i % 2))
                                wt("tensor", "v", bank_free.get(0 + (m_i % 2), 0))
                                i = None
                                for c in range(NCHUNK):
                                    i = MM(lambda e, c=c, m_i=m_i, par=par, g_i=g_i, pg=pg:
                                           e.matmul(pg, gbuf[par].ap()[:, c * FT + m_i * P: c * FT + (m_i + 1) * P],
                                                    ln2_pass.ap()[:, (g_i * NCHUNK + c) * 512:(g_i * NCHUNK + c + 1) * 512],
                                                    start=(c == 0), stop=(c == NCHUNK - 1)))
                                t_g = inc("t", i)
                                i = None
                                for c in range(NCHUNK):
                                    i = MM(lambda e, c=c, m_i=m_i, par=par, g_i=g_i, pu=pu:
                                           e.matmul(pu, ubuf[par].ap()[:, c * FT + m_i * P: c * FT + (m_i + 1) * P],
                                                    ln2_pass.ap()[:, (g_i * NCHUNK + c) * 512:(g_i * NCHUNK + c + 1) * 512],
                                                    start=(c == 0), stop=(c == NCHUNK - 1)))
                                t_u = inc("t", i)
                                wt("scalar", "t", t_g)
                                if prev_va:
                                    wt("scalar", "v", prev_va)
                                i = SC(lambda e, pg=pg: e.activation(sg_t.ap()[:], pg, AF.Silu))
                                s_silu = inc("s", i)
                                wt("vector", "s", s_silu)
                                wt("vector", "t", t_u)
                                i = DV(lambda e, m_i=m_i, pu=pu: e.tensor_mul(
                                    a_t.ap()[:, m_i * 512:(m_i + 1) * 512], sg_t.ap()[:], pu))
                                v_a = inc("v", i)
                                prev_va = v_a
                                bank_free[0 + (m_i % 2)] = v_a
                                bank_free[2 + (m_i % 2)] = v_a
                            wt("tensor", "v", prev_va)
                            for tt in range(4):
                                pe_v = PS.ap()[:, (2 + (tt % 2)) * 1024:(3 + (tt % 2)) * 1024]
                                wt("tensor", "v", max(bank_free.get(4 + 2 * (tt % 2), 0),
                                                      bank_free.get(5 + 2 * (tt % 2), 0)))
                                i = None
                                for m_i in range(MSUB):
                                    for half in range(2):
                                        i = MM(lambda e, m_i=m_i, half=half, par=par, tt=tt, g_i=g_i, pe_v=pe_v:
                                               e.matmul(pe_v[:, half * 512:(half + 1) * 512],
                                                        a_t.ap()[:, m_i * 512 + tt * P: m_i * 512 + (tt + 1) * P],
                                                        dbuf[par].ap()[:, m_i * DM + half * 512: m_i * DM + (half + 1) * 512],
                                                        start=(m_i == 0), stop=(m_i == MSUB - 1)))
                                t_eo = inc("t", i)
                                wt("vector", "t", t_eo)
                                ty = (g_i * 4 + tt) * DM
                                if ty in y_war:
                                    wt("vector", "v", y_war[ty])
                                if first_flag:
                                    i = DV(lambda e, ty=ty, pe_v=pe_v: e.tensor_copy(
                                        y_acc.ap()[:, ty:ty + DM], pe_v))
                                else:
                                    i = DV(lambda e, ty=ty, pe_v=pe_v: e.tensor_tensor(
                                        y_acc.ap()[:, ty:ty + DM], y_acc.ap()[:, ty:ty + DM],
                                        pe_v, ALU.add))
                                v_last_evac = inc("v", i)
                                y_war[ty] = v_last_evac
                                bank_free[4 + 2 * (tt % 2)] = v_last_evac
                                bank_free[5 + 2 * (tt % 2)] = v_last_evac
                        first_flag = False
                        t_down_done[ti] = cnt["t"]
                        if ti < TPP:
                            routing_tile(ti)

                    wt("gpsimd", "ix", cnt["ix"])
                    i = GP(lambda e: e.dma_start(out=idx_sb.ap()[:],
                                                 in_=idx_dram.rearrange("(p c) one -> p (c one)", p=P)))
                    ix_ld = inc("ix", i, 16)


                    t_xg_read = 0
                    eo_par_sem = {}
                    wpar_base = NFTS

                    def emit_gathers(e_i):
                        wt("gpsimd", "ix", ix_ld)
                        wt("gpsimd", "lt", cnt["lt"])
                        wt("gpsimd", "t", t_xg_read)
                        for b in range(CAPT):
                            i = GP(lambda e, e_i=e_i, b=b: e.indirect_dma_start(
                                out=xg_tm.ap()[:, b * DM:(b + 1) * DM],
                                out_offset=None,
                                in_=ln2tm_dram,
                                in_offset=bass.IndirectOffsetOnAxis(
                                    ap=idx_sb.ap()[:, e_i * CAPT + b: e_i * CAPT + b + 1], axis=0)))
                            inc("xg", i, 16)
                        return cnt["xg"]

                    xg_ready = emit_gathers(0)
                    for e_i in range(E):
                        wt("tensor", "xg", xg_ready)
                        for b in range(CAPT):
                            for c in range(NCHUNK):
                                bk = 0 + (c % 2)
                                wt("tensor", "v", bank_free.get(bk, 0))
                                i = MM(lambda e, b=b, c=c, bk=bk: e.transpose(
                                    bank(bk, 128), xg_tm.ap()[:, b * DM + c * P: b * DM + (c + 1) * P],
                                    idf.ap()))
                                t_tr = inc("t", i)
                                wt("vector", "t", t_tr)
                                i = DV(lambda e, b=b, c=c, bk=bk: e.tensor_copy(
                                    xg_fm.ap()[:, c * CAP + b * P: c * CAP + (b + 1) * P].bitcast(F32R),
                                    bank(bk, 128).bitcast(F32R)))
                                v_ev = inc("v", i)
                                bank_free[bk] = v_ev
                        t_xg_read = cnt["t"]
                        v_xg_fm = cnt["v"]
                        wt("tensor", "v", v_xg_fm)
                        if e_i + 1 < E:
                            xg_ready = emit_gathers(e_i + 1)

                        for f_i in range(NFT):
                            ti = wpar_base + e_i * NFT + f_i
                            par = ti % 2
                            if ti >= 2:
                                wt("sync", "t", t_down_done[ti - 2])
                            gsrc = gate_k[e_i].rearrange("(c p) f -> p c f", p=P)[:, :, f_i * FT:(f_i + 1) * FT]
                            usrc = up_k[e_i].rearrange("(c p) f -> p c f", p=P)[:, :, f_i * FT:(f_i + 1) * FT]
                            dsrc = down_k[e_i][f_i * FT:(f_i + 1) * FT, :].rearrange("(c p) d -> p c d", p=P)
                            i = SY(lambda e, par=par, gsrc=gsrc: e.dma_start(
                                out=gbuf[par].ap()[:].rearrange("p (c f) -> p c f", c=NCHUNK),
                                in_=gsrc.bitcast(F32R)))
                            inc(f"w{par}", i, 16)
                            i = SY(lambda e, par=par, usrc=usrc: e.dma_start(
                                out=ubuf[par].ap()[:].rearrange("p (c f) -> p c f", c=NCHUNK),
                                in_=usrc.bitcast(F32R)))
                            inc(f"w{par}", i, 16)
                            i = SY(lambda e, par=par, dsrc=dsrc: e.dma_start(
                                out=dbuf[par].ap()[:].rearrange("p (c d) -> p c d", c=MSUB),
                                in_=dsrc.bitcast(F32R)))
                            w_tile = inc(f"w{par}", i, 16)
                            wt("tensor", f"w{par}", w_tile)
                            for m_i in range(MSUB):
                                pg = bank(0 + (m_i % 2))
                                pu = bank(2 + (m_i % 2))
                                wt("tensor", "v", bank_free.get(0 + (m_i % 2), 0))
                                i = None
                                for c in range(NCHUNK):
                                    i = MM(lambda e, c=c, m_i=m_i, par=par, pg=pg:
                                           e.matmul(pg[:, 0:CAP],
                                                    gbuf[par].ap()[:, c * FT + m_i * P: c * FT + (m_i + 1) * P],
                                                    xg_fm.ap()[:, c * CAP:(c + 1) * CAP],
                                                    start=(c == 0), stop=(c == NCHUNK - 1)))
                                t_g = inc("t", i)
                                i = None
                                for c in range(NCHUNK):
                                    i = MM(lambda e, c=c, m_i=m_i, par=par, pu=pu:
                                           e.matmul(pu[:, 0:CAP],
                                                    ubuf[par].ap()[:, c * FT + m_i * P: c * FT + (m_i + 1) * P],
                                                    xg_fm.ap()[:, c * CAP:(c + 1) * CAP],
                                                    start=(c == 0), stop=(c == NCHUNK - 1)))
                                t_u = inc("t", i)
                                wt("scalar", "t", t_g)
                                if prev_va:
                                    wt("scalar", "v", prev_va)
                                i = SC(lambda e, pg=pg: e.activation(sg_t.ap()[:, 0:CAP], pg[:, 0:CAP], AF.Silu))
                                s_silu = inc("s", i)
                                wt("vector", "s", s_silu)
                                wt("vector", "t", t_u)
                                i = DV(lambda e, m_i=m_i, pu=pu: e.tensor_mul(
                                    a_t.ap()[:, m_i * 512:m_i * 512 + CAP], sg_t.ap()[:, 0:CAP], pu[:, 0:CAP]))
                                v_a = inc("v", i)
                                prev_va = v_a
                                bank_free[0 + (m_i % 2)] = v_a
                                bank_free[2 + (m_i % 2)] = v_a
                            wt("tensor", "v", prev_va)
                            for tt in range(CAPT):
                                pe_v = PS.ap()[:, (2 + (tt % 2)) * 1024:(3 + (tt % 2)) * 1024]
                                wt("tensor", "v", max(bank_free.get(4 + 2 * (tt % 2), 0),
                                                      bank_free.get(5 + 2 * (tt % 2), 0)))
                                i = None
                                for m_i in range(MSUB):
                                    for half in range(2):
                                        i = MM(lambda e, m_i=m_i, half=half, par=par, tt=tt, pe_v=pe_v:
                                               e.matmul(pe_v[:, half * 512:(half + 1) * 512],
                                                        a_t.ap()[:, m_i * 512 + tt * P: m_i * 512 + (tt + 1) * P],
                                                        dbuf[par].ap()[:, m_i * DM + half * 512: m_i * DM + (half + 1) * 512],
                                                        start=(m_i == 0), stop=(m_i == MSUB - 1)))
                                t_eo = inc("t", i)
                                wt("vector", "t", t_eo)
                                ty = tt * DM
                                if f_i == 0:
                                    if ("eo", tt) in eo_par_sem:
                                        wt("vector", "eo", eo_par_sem[("eo", tt)])
                                    i = DV(lambda e, ty=ty, pe_v=pe_v: e.tensor_copy(
                                        eo_acc.ap()[:, ty:ty + DM], pe_v))
                                else:
                                    i = DV(lambda e, ty=ty, pe_v=pe_v: e.tensor_tensor(
                                        eo_acc.ap()[:, ty:ty + DM], eo_acc.ap()[:, ty:ty + DM],
                                        pe_v, ALU.add))
                                v_last_evac = inc("v", i)
                                bank_free[4 + 2 * (tt % 2)] = v_last_evac
                                bank_free[5 + 2 * (tt % 2)] = v_last_evac
                            t_down_done[ti] = cnt["t"]
                        v_eo_done = cnt["v"]
                        wt("gpsimd", "v", v_eo_done)
                        wt("gpsimd", "eg", cnt["eg"])
                        for b in range(CAPT):
                            i = GP(lambda e, e_i=e_i, b=b: e.dma_start(
                                out=eo_dram[e_i * CAP + b * P: e_i * CAP + (b + 1) * P, :],
                                in_=eo_acc.ap()[:, b * DM:(b + 1) * DM]))
                            inc("eo", i, 16)
                        for b in range(CAPT):
                            eo_par_sem[("eo", b)] = cnt["eo"]
                    eo_all = cnt["eo"]

                    for tt in range(TPP):
                        qt = tok0 // P + tt
                        ty = tt * DM
                        wt("gpsimd", "eo", eo_all)
                        wt("gpsimd", "t", t_xg_read)
                        wt("gpsimd", "v", y_war.get(ty, 0))
                        g_vals = []
                        for k in range(2):
                            i = GP(lambda e, tt=tt, k=k: e.indirect_dma_start(
                                out=xg_tm.ap()[:, k * DM:(k + 1) * DM],
                                out_offset=None,
                                in_=eo_dram,
                                in_offset=bass.IndirectOffsetOnAxis(
                                    ap=sl_sb.ap()[:, tt * 2 + k: tt * 2 + k + 1], axis=0)))
                            g_vals.append(inc("eg", i, 16))
                        wt("vector", "eg", g_vals[1])
                        i = DV(lambda e, ty=ty, tt=tt: e.scalar_tensor_tensor(
                            o_tm.ap()[:], xg_tm.ap()[:, 0:DM], w12_sb.ap()[:, tt * 2: tt * 2 + 1],
                            y_acc.ap()[:, ty:ty + DM], ALU.mult, ALU.add))
                        v_c1 = inc("v", i)
                        wt("vector", "v", v_c1)
                        i = DV(lambda e, tt=tt: e.scalar_tensor_tensor(
                            o_tm.ap()[:], xg_tm.ap()[:, DM:2 * DM], w12_sb.ap()[:, tt * 2 + 1: tt * 2 + 2],
                            o_tm.ap()[:], ALU.mult, ALU.add))
                        v_comb2 = inc("v", i)
                        ln_stats(lambda: o_tm.ap()[:], wait_v_scalar=v_comb2,
                                 wait_v_vector=v_comb2, wait_t_scalar=t_lnt_guard[0])
                        i = SC(lambda e: e.activation(o_ln.ap()[:], o_tm.ap()[:],
                                                      AF.Identity, bias=smc(SM_NMRS), scale=smc(SM_RSTD)))
                        s_yln = inc("s", i)
                        slot = tt % 2
                        gsem, gval = slot_guard[slot]
                        if gval:
                            wt("gpsimd", gsem, gval)
                        slot_sem, slot_val = (f"st{qt % 2}", st_x1_tile[(s, qt)])
                        wt("gpsimd", slot_sem, slot_val)
                        i = GP(lambda e, qt=qt, slot=slot: e.dma_start(
                            out=x_tile.ap()[:, slot * DM:(slot + 1) * DM],
                            in_=x1_dram[base + qt * P: base + (qt + 1) * P, :]))
                        xl_x1 = inc(f"xl{slot}", i, 16)
                        wt("vector", "s", s_yln)
                        wt("vector", f"xl{slot}", xl_x1)
                        if so_last_slot[slot]:
                            wt("vector", f"so{slot}", so_last_slot[slot])
                        if st_x1_last_slot[slot]:
                            wt("vector", f"st{slot}", st_x1_last_slot[slot])
                        x1s = slot * DM
                        i = DV(lambda e, slot=slot, x1s=x1s: e.tensor_add(
                            x1_t.ap()[:, x1s:x1s + DM], x_tile.ap()[:, slot * DM:(slot + 1) * DM], o_ln.ap()[:]))
                        v_out = inc("v", i)
                        slot_guard[slot] = ("v", v_out)
                        wt("gpsimd", "v", v_out)
                        i = GP(lambda e, qt=qt, x1s=x1s: e.dma_start(
                            out=out_d[base + qt * P: base + (qt + 1) * P, :], in_=x1_t.ap()[:, x1s:x1s + DM]))
                        so_last_slot[slot] = inc(f"so{slot}", i, 16)

            for nm in ("st0", "st1", "so0", "so1"):
                wt("gpsimd", nm, cnt[nm])
            return

        block.sync(lambda e: emit("sync", e))
        block.tensor(lambda e: emit("tensor", e))
        block.vector(lambda e: emit("vector", e))
        block.scalar(lambda e: emit("scalar", e))
        block.gpsimd(lambda e: emit("gpsimd", e))

    return nc


def make_const_inputs():
    lt = (np.arange(P)[:, None] < np.arange(P)[None, :]).astype(np.float32)
    e = np.arange(8, dtype=np.float32)
    rtc = np.concatenate([
        np.broadcast_to(16.0 - e, (P, 8)),
        np.broadcast_to(3.0 * e, (P, 8)),
        np.broadcast_to(float(CAP) * e, (P, 8)),
    ], axis=1).astype(np.float32)
    rtc2 = np.stack([np.ones(P, np.float32), np.arange(P, dtype=np.float32)], axis=1)
    return dict(
        ident_f=np.eye(P, dtype=np.float32),
        ident_b=np.eye(P, dtype=np.float32).astype(ml_dtypes.bfloat16),
        masks_in=build_masks(),
        ones_in=np.ones((1, P), np.float32),
        lt_in=np.ascontiguousarray(lt),
        rtc_in=np.ascontiguousarray(rtc),
        rtc2_in=np.ascontiguousarray(rtc2),
    )


def make_weight_inputs(kw):
    g = lambda k: np.ascontiguousarray(np.asarray(kw[k], np.float32))
    w = {}
    for pn, src in (("q", "q"), ("k", "k"), ("v", "v"), ("o", "o")):
        w[f"{pn}_Wv"] = g(f"L0_attn_{src}_Wv")
        w[f"{pn}_Wr"] = g(f"L0_attn_{src}_Wr")
        w[f"{pn}_Wl"] = g(f"L0_attn_{src}_Wl")
        w[f"{pn}_b"] = g(f"L0_attn_{src}_b")
    w["gate_w"] = g("L0_ffn_gate_w")
    w["gate_k"] = g("L0_ffn_gate_k")
    w["up_k"] = g("L0_ffn_up_k")
    w["down_k"] = g("L0_ffn_down_k")
    w["sh_gate"] = g("L0_ffn_sh_gate")
    w["sh_up"] = g("L0_ffn_sh_up")
    w["sh_down"] = g("L0_ffn_sh_down")
    return w


def kernel(**inputs):
    NCORES = 8
    x = np.ascontiguousarray(np.asarray(inputs["L0_attn_x"], np.float32))
    B, T, D = x.shape
    per = B // NCORES

    weights = make_weight_inputs(inputs)
    consts = make_const_inputs()

    nc = build_program(n_seq=per, T=T)
    in_maps = []
    for c in range(NCORES):
        m = {"x_in": x[c * per:(c + 1) * per].reshape(per * T, D)}
        m.update(weights)
        m.update(consts)
        in_maps.append(m)
    res = run_bass_kernel_spmd(nc, in_maps, core_ids=list(range(NCORES)))
    out = np.concatenate([r["out"].reshape(per, T, D) for r in res.results], axis=0)
    return out
